# version 10
# baseline (speedup 1.0000x reference)
import numpy as np
import ml_dtypes
from contextlib import ExitStack

import concourse.bass as bass
import concourse.mybir as mybir
from concourse.bass_utils import run_bass_kernel_spmd

F32 = mybir.dt.float32
BF16 = mybir.dt.bfloat16
I32 = mybir.dt.int32
U32 = mybir.dt.uint32
AF = mybir.ActivationFunctionType
ALU = mybir.AluOpType
AXX = mybir.AxisListType.X
ds = bass.ds
AP = bass.AP

L, DM, H, HD = 4096, 1024, 16, 64
NSLAB, NSLICE, NCH = 32, 8, 64
HIDDEN = 3072
TOUT = 2048
EPS_MAIN, EPS_SCAN = 1e-5, 1e-6


def _ap(t, offset, dims):
    tensor = t.tensor if hasattr(t, "tensor") else t
    return AP(tensor, offset, [list(d) for d in dims])


def _grp(t):
    return _ap(t, 0, [[8, 128], [1, 8], [0, 64]])


def _r3(t):
    return t[:, :].rearrange("p (g d) -> p g d", g=8)


def _build_tables(lvl, no_lr):
    vaN, saN = {}, {}
    va, sa = [0], [0]

    def V(*tag):
        va[0] += 1
        vaN[tag] = va[0]

    def A(*tag):
        sa[0] += 1
        saN[tag] = sa[0]

    if lvl >= 2:
        A("c0")
    for s in range(NSLICE):
        for u in range(4):
            k = 4 * s + u
            A("q", k)
            A("t", k)
        if lvl >= 2 and s >= 1:
            for i in range(8 * (s - 1), 8 * s):
                for n in range(1, 7):
                    A("s", i, n)
        if lvl >= 1 and not no_lr:
            for u in range(4):
                A("g", 4 * s + u)
    if lvl >= 2:
        for i in range(56, 64):
            for n in range(1, 7):
                A("s", i, n)
    for s in range(NSLICE):
        for u in range(4):
            V("r", 4 * s + u)
        if lvl >= 2 and s >= 1:
            for i in range(8 * (s - 1), 8 * s):
                for n in range(1, 9):
                    V("c", i, n)
        if lvl >= 1 and not no_lr:
            for u in range(4):
                V("l", 4 * s + u)
            for u in range(4):
                V("e", 4 * s + u)
    if lvl >= 2:
        for i in range(56, 64):
            for n in range(1, 9):
                V("c", i, n)
    return vaN, saN


def PMM(s, j):
    return 28 * s + j + 1


def PMM_LR(s, u):
    return 28 * s + 24 + u + 1


PH = {"ln": 0, "a1": 1, "a2": 2, "b1": 3, "full": 4}


def build_nc(upto="full", dbg=False, no_lr=False, no_dyn=False):
    lvl = PH[upto]
    vaN, saN = _build_tables(lvl, no_lr)
    nc = bass.Bass("TRN2", target_bir_lowering=False, detect_race_conditions=False)
    scratch_kind = "ExternalOutput" if dbg else "Internal"

    x_seq = nc.dram_tensor("x_seq", [L, DM], F32, kind="ExternalInput")
    dynr = nc.dram_tensor("dynr", [NSLAB + 16, 1], U32, kind="ExternalInput")
    wq_d = nc.dram_tensor("wq", [DM, DM], F32, kind="ExternalInput")
    wk_d = nc.dram_tensor("wk", [DM, DM], F32, kind="ExternalInput")
    wv_d = nc.dram_tensor("wv", [DM, DM], F32, kind="ExternalInput")
    wo_d = nc.dram_tensor("wo", [DM, DM], F32, kind="ExternalInput")
    thetab_d = nc.dram_tensor("thetab", [DM, H], F32, kind="ExternalInput")
    tbmat_d = nc.dram_tensor("tbmat", [128, H], F32, kind="ExternalInput")
    invk_d = nc.dram_tensor("invk", [128, 1], F32, kind="ExternalInput")
    alpha2_d = nc.dram_tensor("alpha2", [128, 1], F32, kind="ExternalInput")
    gpar_d = nc.dram_tensor("gpar", [128, 512], F32, kind="ExternalInput")
    bepar_d = nc.dram_tensor("bepar", [128, 512], F32, kind="ExternalInput")
    gtm_d = nc.dram_tensor("gtm", [128, DM], F32, kind="ExternalInput")
    betm_d = nc.dram_tensor("betm", [128, DM], F32, kind="ExternalInput")
    w0par_d = nc.dram_tensor("w0par", [128, 512], F32, kind="ExternalInput")
    b0par_d = nc.dram_tensor("b0par", [128, 512], F32, kind="ExternalInput")
    wgT_d = nc.dram_tensor("wgT", [DM, HIDDEN], I32, kind="ExternalInput")
    wuT_d = nc.dram_tensor("wuT", [DM, HIDDEN], I32, kind="ExternalInput")
    wdT_d = nc.dram_tensor("wdT", [HIDDEN, DM], I32, kind="ExternalInput")
    out_d = nc.dram_tensor("out", [TOUT, DM], F32, kind="ExternalOutput")

    xn_hbm = nc.dram_tensor("xn_hbm", [L, DM], BF16, kind=scratch_kind)
    xn2_hbm = nc.dram_tensor("xn2_hbm", [2 * L, DM], BF16, kind=scratch_kind)
    kfm_hbm = nc.dram_tensor("kfm_hbm", [DM, L], BF16, kind=scratch_kind)
    qfm_hbm = nc.dram_tensor("qfm_hbm", [DM, L], BF16, kind=scratch_kind)
    vfm_hbm = nc.dram_tensor("vfm_hbm", [DM, L], BF16, kind=scratch_kind)
    attn2_hbm = nc.dram_tensor("attn2_hbm", [2 * L, DM], BF16, kind=scratch_kind)
    h_hbm = nc.dram_tensor("h_hbm", [TOUT, HIDDEN], BF16, kind=scratch_kind)
    wgbf_hbm = nc.dram_tensor("wgbf_hbm", [DM, HIDDEN], BF16, kind="Internal")
    wubf_hbm = nc.dram_tensor("wubf_hbm", [DM, HIDDEN], BF16, kind="Internal")
    wdbf_hbm = nc.dram_tensor("wdbf_hbm", [HIDDEN, DM], BF16, kind="Internal")
    wobf_hbm = nc.dram_tensor("wobf_hbm", [DM, DM], BF16, kind="Internal")

    with ExitStack() as top:
        e = top.enter_context
        sem_names = [
            "cld", "wld", "x_ld", "xn_done", "xnsp", "xnsp2", "xnf_ld",
            "pmm", "evac", "ksp", "txp0", "txp1", "kvg",
            "va", "sa", "z1mm", "dwmm", "zqmm", "asp",
            "pc0", "pc1", "pc2", "pc3", "ac0", "ac1", "pre", "wdc", "bxs",
            "phb", "fwl", "fws", "bxn", "gumm", "silu", "hready",
            "hsp", "hld", "afm_ld", "xres_ld", "omm", "oev", "ost",
        ]
        S = {n: e(nc.semaphore(n)) for n in sem_names}
        dynr_sb = e(nc.sbuf_tensor("dynr_sb", [NSLAB + 16, 1], U32))

        with ExitStack() as pa:
            a = pa.enter_context
            thetab_sb = a(nc.sbuf_tensor("thetab_sb", [128, 8 * H], BF16))
            tbmat_sb = a(nc.sbuf_tensor("tbmat_sb", [128, H], F32))
            invk_sb = a(nc.sbuf_tensor("invk_sb", [128, 1], F32))
            alpha2_sb = a(nc.sbuf_tensor("alpha2_sb", [128, 1], F32))
            etab_sb = a(nc.sbuf_tensor("etab_sb", [128, 1], F32))
            gpar_sb = a(nc.sbuf_tensor("gpar_sb", [128, 512], BF16))
            bepar_sb = a(nc.sbuf_tensor("bepar_sb", [128, 512], BF16))
            g2par_sb = a(nc.sbuf_tensor("g2par_sb", [128, 512], BF16))
            gtm_sb = a(nc.sbuf_tensor("gtm_sb", [128, DM], BF16))
            betm_sb = a(nc.sbuf_tensor("betm_sb", [128, DM], BF16))
            gbetm_sb = a(nc.sbuf_tensor("gbetm_sb", [128, DM], BF16))
            wq_sb = a(nc.sbuf_tensor("wq_sb", [128, 8 * DM], BF16))
            wk_sb = a(nc.sbuf_tensor("wk_sb", [128, 8 * DM], BF16))
            wv_sb = a(nc.sbuf_tensor("wv_sb", [128, 8 * DM], BF16))
            x_slab = [a(nc.sbuf_tensor(f"x_slab{i}", [128, DM], F32)) for i in range(2)]
            sqx_sb = a(nc.sbuf_tensor("sqx_sb", [128, DM], F32))
            xnstg = a(nc.sbuf_tensor("xnstg", [128, 4 * DM], BF16))
            xnf_sb = [a(nc.sbuf_tensor(f"xnf_sb{i}", [128, 8 * 512], BF16)) for i in range(2)]
            kfm_sb = [a(nc.sbuf_tensor(f"kfm_sb{i}", [128, 8 * 512], BF16)) for i in range(2)]
            qfm_sb = [a(nc.sbuf_tensor(f"qfm_sb{i}", [128, 8 * 512], BF16)) for i in range(2)]
            vfm_sb = [a(nc.sbuf_tensor(f"vfm_sb{i}", [128, 8 * 512], BF16)) for i in range(2)]
            st_s1 = a(nc.sbuf_tensor("st_s1", [128, 1], F32))
            st_s2 = a(nc.sbuf_tensor("st_s2", [128, 1], F32))
            st_m = a(nc.sbuf_tensor("st_m", [128, 1], F32))
            st_t1 = a(nc.sbuf_tensor("st_t1", [128, 1], F32))
            st_t2 = a(nc.sbuf_tensor("st_t2", [128, 1], F32))
            st_v = a(nc.sbuf_tensor("st_v", [128, 1], F32))
            st_d = a(nc.sbuf_tensor("st_d", [128, 1], F32))
            st_r = a(nc.sbuf_tensor("st_r", [128, 1], F32))
            lrall_sb = a(nc.sbuf_tensor("lrall_sb", [128, NSLAB * H], F32))
            sigall_sb = a(nc.sbuf_tensor("sigall_sb", [128, NSLAB * H], F32))
            eta_sb = a(nc.sbuf_tensor("eta_sb", [128, NSLAB * H], F32))
            txk_sb = [a(nc.sbuf_tensor(f"txk_sb{i}", [128, DM], BF16)) for i in range(2)]
            txq_sb = [a(nc.sbuf_tensor(f"txq_sb{i}", [128, DM], BF16)) for i in range(2)]
            txv_sb = [a(nc.sbuf_tensor(f"txv_sb{i}", [128, DM], BF16)) for i in range(2)]
            kvg_sb = [a(nc.sbuf_tensor(f"kvg_sb{i}", [128, DM], BF16)) for i in range(2)]
            w_par = a(nc.sbuf_tensor("w_par", [128, 512], F32))
            wbf = a(nc.sbuf_tensor("wbf", [128, 512], BF16))
            b_mat = a(nc.sbuf_tensor("b_mat", [128, 512], F32))
            kc_par = [a(nc.sbuf_tensor(f"kc_par{j}", [128, 512], BF16)) for j in range(2)]
            qc_par = [a(nc.sbuf_tensor(f"qc_par{j}", [128, 512], BF16)) for j in range(2)]
            kvg_par = [a(nc.sbuf_tensor(f"kvg_par{j}", [128, 512], BF16)) for j in range(2)]
            eta_par = [a(nc.sbuf_tensor(f"eta_par{j}", [128, 8], F32)) for j in range(2)]
            z1b_sb = a(nc.sbuf_tensor("z1b_sb", [128, 512], F32))
            sq_sb = a(nc.sbuf_tensor("sq_sb", [128, 512], F32))
            xh_sb = a(nc.sbuf_tensor("xh_sb", [128, 512], BF16))
            gxh_sb = a(nc.sbuf_tensor("gxh_sb", [128, 512], BF16))
            gx2_sb = a(nc.sbuf_tensor("gx2_sb", [128, 512], BF16))
            u_sb = a(nc.sbuf_tensor("u_sb", [128, 512], BF16))
            grad_sb = a(nc.sbuf_tensor("grad_sb", [128, 512], BF16))
            sck_sb = a(nc.sbuf_tensor("sck_sb", [128, 512], BF16))
            w1_sb = a(nc.sbuf_tensor("w1_sb", [128, 512], BF16))
            outp_sb = [a(nc.sbuf_tensor(f"outp_sb{j}", [128, 512], BF16)) for j in range(2)]
            c_s1 = a(nc.sbuf_tensor("c_s1", [128, 8], F32))
            c_s2 = a(nc.sbuf_tensor("c_s2", [128, 8], F32))
            c_m = a(nc.sbuf_tensor("c_m", [128, 8], F32))
            c_t = a(nc.sbuf_tensor("c_t", [128, 8], F32))
            c_v0 = a(nc.sbuf_tensor("c_v0", [128, 8], F32))
            c_d = a(nc.sbuf_tensor("c_d", [128, 8], F32))
            c_var = a(nc.sbuf_tensor("c_var", [128, 8], F32))
            c_std = a(nc.sbuf_tensor("c_std", [128, 8], F32))
            c_rstd = a(nc.sbuf_tensor("c_rstd", [128, 8], F32))
            c_r1 = a(nc.sbuf_tensor("c_r1", [128, 8], F32))
            c_r2 = a(nc.sbuf_tensor("c_r2", [128, 8], F32))
            c_a1 = a(nc.sbuf_tensor("c_a1", [128, 8], F32))
            c_a2 = a(nc.sbuf_tensor("c_a2", [128, 8], F32))
            c_eta2 = a(nc.sbuf_tensor("c_eta2", [128, 8], F32))
            c_eta2b = a(nc.sbuf_tensor("c_eta2b", [128, 8], BF16))
            attn_stage = a(nc.sbuf_tensor("attn_stage", [128, 4 * DM], BF16))
            pp_proj = [a(nc.psum_tensor(f"pp_proj{i}", [128, 512], F32)) for i in range(2)]
            pp_lr = a(nc.psum_tensor("pp_lr", [128, H], F32))
            pp_z1 = [a(nc.psum_tensor(f"pp_z1{i}", [128, 512], F32)) for i in range(2)]
            pp_zq = a(nc.psum_tensor("pp_zq", [128, 512], F32))
            pp_dw = a(nc.psum_tensor("pp_dw", [128, 512], F32))
            pp_db = a(nc.psum_tensor("pp_db", [128, 512], F32))

            blkA = pa.enter_context(nc.Block())

            def wsb(t):
                return (wk_sb, wq_sb, wv_sb)[t]

            def fmsb(t):
                return (kfm_sb, qfm_sb, vfm_sb)[t]

            def fmhbm(t):
                return (kfm_hbm, qfm_hbm, vfm_hbm)[t]

            @blkA.sync
            def _(sync):
                n_cld = 0
                for dst, src in ((tbmat_sb, tbmat_d), (invk_sb, invk_d),
                                 (alpha2_sb, alpha2_d), (w_par, w0par_d),
                                 (b_mat, b0par_d), (dynr_sb, dynr)):
                    sync.dma_start(out=dst[:, :], in_=src[:, :]).then_inc(S["cld"], 16)
                    n_cld += 1
                assert n_cld == 6
                sync.wait_ge(S["cld"], 16 * 6)
                if True:
                    def kqv_spill(s):
                        for t in range(3):
                            for p in range(8):
                                sync.wait_ge(S["evac"], 24 * s + t * 8 + p + 1)
                                sync.dma_start(
                                    out=fmhbm(t)[128 * p:128 * (p + 1), 512 * s:512 * (s + 1)],
                                    in_=fmsb(t)[s % 2][:, 512 * p:512 * (p + 1)],
                                ).then_inc(S["ksp"], 16)
                    def tx_load(m):
                        txs = S["txp%d" % (m % 2)]
                        sl = m // 4
                        sync.wait_ge(S["evac"], 24 * (sl + 1))
                        if m >= 2:
                            sync.wait_ge(S["kvg"], m - 1)
                            j = 2 * m - 3
                            sync.wait_ge(S["pc%d" % (j % 4)],
                                         16 * (6 * (j // 4) + 6))
                        for src, dst in ((kfm_sb, txk_sb), (vfm_sb, txv_sb),
                                         (qfm_sb, txq_sb)):
                            for p in range(8):
                                sync.dma_start(
                                    out=dst[m % 2][:, 128 * p:128 * (p + 1)],
                                    in_=src[sl % 2][:, 512 * p + 128 * (m % 4):
                                                    512 * p + 128 * (m % 4) + 128],
                                    transpose=True).then_inc(txs, 16)

                    def par_copies(i):
                        m, u = i // 2, i % 2
                        pcs = S["pc%d" % (i % 4)]
                        txs = S["txp%d" % (m % 2)]
                        if i >= 2:
                            sync.wait_ge(S["va"], vaN[("c", i - 2, 3)])
                        sync.wait_ge(S["kvg"], m + 1)
                        for q in range(2):
                            sync.dma_start(
                                out=_ap(kvg_par[i % 2], (64 * q) * 512,
                                        [[512, 64], [64, 8], [1, 64]]),
                                in_=_ap(kvg_sb[m % 2], (64 * u) * DM + 64 * q,
                                        [[DM, 64], [128, 8], [1, 64]]),
                            ).then_inc(pcs, 16)
                        sync.wait_ge(txs, 16 * (24 * (m // 2) + 8))
                        for q in range(2):
                            sync.dma_start(
                                out=_ap(kc_par[i % 2], (64 * q) * 512,
                                        [[512, 64], [64, 8], [1, 64]]),
                                in_=_ap(txk_sb[m % 2], (64 * u) * DM + 64 * q,
                                        [[DM, 64], [128, 8], [1, 64]]),
                            ).then_inc(pcs, 16)
                        if i >= 2:
                            sync.wait_ge(S["va"], vaN[("c", i - 2, 8)])
                        sync.wait_ge(txs, 16 * (24 * (m // 2) + 24))
                        for q in range(2):
                            sync.dma_start(
                                out=_ap(qc_par[i % 2], (64 * q) * 512,
                                        [[512, 64], [64, 8], [1, 64]]),
                                in_=_ap(txq_sb[m % 2], (64 * u) * DM + 64 * q,
                                        [[DM, 64], [128, 8], [1, 64]]),
                            ).then_inc(pcs, 16)

                    def attn_copy(i):
                        m, u = i // 2, i % 2
                        sync.wait_ge(S["va"], vaN[("c", i, 8)])
                        if m >= 4 and u == 0:
                            sync.wait_ge(S["asp"], 16 * (m // 4))
                        for q in range(2):
                            sync.dma_start(
                                out=_ap(attn_stage, (m % 4) * DM
                                        + (64 * u) * (4 * DM) + 64 * q,
                                        [[4 * DM, 64], [128, 8], [1, 64]]),
                                in_=_ap(outp_sb[i % 2], (64 * q) * 512,
                                        [[512, 64], [64, 8], [1, 64]]),
                            ).then_inc(S["ac%d" % (i % 2)], 16)
                        if i % 8 == 7:
                            g = i // 8
                            sync.wait_ge(S["ac0"], 16 * 2 * (4 * g + 4))
                            sync.wait_ge(S["ac1"], 16 * 2 * (4 * g + 4))
                            v = sync.value_load(dynr_sb[g:g + 1, 0:1])
                            sync.dma_start(
                                out=attn2_hbm[ds(v, 512), :].rearrange(
                                    "(u t) d -> t u d", t=128),
                                in_=_ap(attn_stage, 0,
                                        [[4 * DM, 128], [DM, 4], [1, DM]]),
                            ).then_inc(S["asp"], 16)

                    def batch_sync(y):
                        tx_load(4 * y)
                        tx_load(4 * y + 1)
                        for i in range(8 * y, 8 * y + 8):
                            if i == 8 * y + 2:
                                tx_load(4 * y + 2)
                            if i == 8 * y + 4:
                                tx_load(4 * y + 3)
                            par_copies(i)
                            if i >= 2:
                                attn_copy(i - 2)

                    for s in range(NSLICE):
                        for u in range(4):
                            k = 4 * s + u
                            if k >= 2:
                                sync.wait_ge(S["xn_done"], k - 1)
                            sync.dma_start(out=x_slab[k % 2][:, :],
                                           in_=x_seq[128 * k:128 * (k + 1), :],
                                           ).then_inc(S["x_ld"], 16)
                            sync.wait_ge(S["xn_done"], k + 1)
                            sync.dma_start(out=xn_hbm[128 * k:128 * (k + 1), :],
                                           in_=xnstg[:, DM * (k % 4):DM * (k % 4 + 1)],
                                           ).then_inc(S["xnsp"], 16)
                        if lvl == 1 and s >= 1:
                            kqv_spill(s - 1)
                        if lvl >= 2 and s >= 1:
                            batch_sync(s - 1)
                    if lvl == 1:
                        kqv_spill(NSLICE - 1)
                    if lvl >= 2:
                        batch_sync(NSLICE - 1)
                        attn_copy(NCH - 2)
                        attn_copy(NCH - 1)
                        sync.wait_ge(S["asp"], 16 * 8)
                    sync.wait_ge(S["xnsp"], 16 * NSLAB)
                    if not no_dyn:
                        sync.wait_ge(S["xnsp2"], 16 * 8)
                    if lvl == 1:
                        sync.wait_ge(S["ksp"], 16 * 24 * NSLICE)

            @blkA.gpsimd
            def _(gpsimd):
                n = 0
                for dst, src in ((wk_sb, wk_d), (wq_sb, wq_d), (wv_sb, wv_d)):
                    for c in range(8):
                        gpsimd.dma_start(out=dst[:, DM * c:DM * (c + 1)],
                                         in_=src[128 * c:128 * (c + 1), :]).then_inc(S["wld"], 16)
                        n += 1
                for c in range(8):
                    gpsimd.dma_start(out=thetab_sb[:, H * c:H * (c + 1)],
                                     in_=thetab_d[128 * c:128 * (c + 1), :]).then_inc(S["wld"], 16)
                    n += 1
                for dst, src in ((gpar_sb, gpar_d), (bepar_sb, bepar_d),
                                 (gtm_sb, gtm_d), (betm_sb, betm_d)):
                    gpsimd.dma_start(out=dst[:, :], in_=src[:, :]).then_inc(S["wld"], 16)
                    n += 1
                assert n == 36
                gpsimd.wait_ge(S["cld"], 16 * 6)
                if lvl >= 2:
                    gpsimd.wait_ge(S["pre"], 1)
                _casts = {1: (wgbf_hbm, wgT_d), 2: (wubf_hbm, wuT_d),
                          3: (wdbf_hbm, wdT_d), 4: (wobf_hbm, wo_d)}

                def kvg_batch(y):
                    for m in range(4 * y, 4 * y + 4):
                        if m >= 2:
                            for j in (2 * m - 4, 2 * m - 3):
                                gpsimd.wait_ge(S["pc%d" % (j % 4)],
                                               16 * (6 * (j // 4) + 2))
                        gpsimd.wait_ge(S["txp%d" % (m % 2)],
                                       16 * (24 * (m // 2) + 16))
                        gpsimd.tensor_sub(kvg_sb[m % 2][:, :], txk_sb[m % 2][:, :],
                                          txv_sb[m % 2][:, :])
                        gpsimd.tensor_mul(kvg_sb[m % 2][:, :], kvg_sb[m % 2][:, :],
                                          gtm_sb[:, :])
                        gpsimd.tensor_add(kvg_sb[m % 2][:, :], kvg_sb[m % 2][:, :],
                                          gbetm_sb[:, :]).then_inc(S["kvg"], 1)

                for s in range(NSLICE):
                    if lvl >= 3 and s in _casts:
                        dst, src = _casts[s]
                        gpsimd.dma_start(out=dst[:, :],
                                         in_=src[:, :]).then_inc(S["wdc"], 16)
                    if lvl >= 2 and s >= 1:
                        kvg_batch(s - 1)
                    if not no_dyn:
                        gpsimd.wait_ge(S["xn_done"], 4 * s + 4)
                        v = gpsimd.value_load(dynr_sb[s:s + 1, 0:1])
                        src = _ap(xnstg, 0, [[4 * DM, 128], [DM, 4], [1, DM]])
                        gpsimd.dma_start(
                            out=xn2_hbm[ds(v, 512), :].rearrange("(u t) d -> t u d", t=128),
                            in_=src).then_inc(S["xnsp2"], 16)
                if lvl >= 2:
                    kvg_batch(NSLICE - 1)

            WLD_ALL = 16 * 36

            def batch_mms_impl(tensor, y):
                for i in range(8 * y, 8 * y + 8):
                    m, u = i // 2, i % 2
                    sl = m // 4
                    off = (i * 64) % 512
                    tensor.wait_ge(S["sa"], saN[("c0",)] if i == 0
                                   else saN[("s", i - 1, 4)])
                    if i >= 2:
                        tensor.wait_ge(S["va"], vaN[("c", i - 2, 1)])
                    for p in range(8):
                        for q in range(2):
                            mm = tensor.matmul(
                                pp_z1[i % 2][64 * q:64 * (q + 1), 64 * p:64 * (p + 1)],
                                kfm_sb[sl % 2][64 * q:64 * (q + 1),
                                               512 * p + off:512 * p + off + 64],
                                wbf[64 * q:64 * (q + 1), 64 * p:64 * (p + 1)],
                                start=True, stop=True, tile_position=(64 * q, 64 * q))
                            if p == 7 and q == 1:
                                mm.then_inc(S["z1mm"], 1)
                    tensor.wait_ge(S["va"], vaN[("c", i, 3)])
                    if i >= 1:
                        tensor.wait_ge(S["va"], vaN[("c", i - 1, 5)])
                    for p in range(8):
                        for q in range(2):
                            mm = tensor.matmul(
                                pp_dw[64 * q:64 * (q + 1), 64 * p:64 * (p + 1)],
                                sck_sb[64 * q:64 * (q + 1), 64 * p:64 * (p + 1)],
                                grad_sb[64 * q:64 * (q + 1), 64 * p:64 * (p + 1)],
                                start=True, stop=True, tile_position=(64 * q, 64 * q))
                            if p == 7 and q == 1:
                                mm.then_inc(S["dwmm"], 1)
                    for p in range(8):
                        for q in range(2):
                            lhs = _ap(c_eta2b, q * 64 * 8 + p, [[8, 64], [0, 64]])
                            mm = tensor.matmul(
                                pp_db[64 * q:64 * (q + 1), 64 * p:64 * (p + 1)],
                                lhs,
                                grad_sb[64 * q:64 * (q + 1), 64 * p:64 * (p + 1)],
                                start=True, stop=True, tile_position=(64 * q, 64 * q))
                            if p == 7 and q == 1:
                                mm.then_inc(S["dwmm"], 1)
                    tensor.wait_ge(S["sa"], saN[("s", i, 4)])
                    if i >= 1:
                        tensor.wait_ge(S["va"], vaN[("c", i - 1, 6)])
                    for p in range(8):
                        for q in range(2):
                            mm = tensor.matmul(
                                pp_zq[64 * q:64 * (q + 1), 64 * p:64 * (p + 1)],
                                qfm_sb[sl % 2][64 * q:64 * (q + 1),
                                               512 * p + off:512 * p + off + 64],
                                wbf[64 * q:64 * (q + 1), 64 * p:64 * (p + 1)],
                                start=True, stop=True, tile_position=(64 * q, 64 * q))
                            if p == 7 and q == 1:
                                mm.then_inc(S["zqmm"], 1)

            @blkA.tensor
            def _(tensor):
                def batch_mms(y):
                    tensor.wait_ge(S["evac"], 24 * (y + 1))
                    batch_mms_impl(tensor, y)

                tensor.wait_ge(S["wld"], WLD_ALL)
                for s in range(NSLICE if lvl >= 1 else 0):
                    if lvl >= 2 and s >= 1:
                        batch_mms(s - 1)
                    tensor.wait_ge(S["xnf_ld"], 16 * 8 * (s + 1))
                    for t in range(3):
                        for p in range(8):
                            g = 24 * s + t * 8 + p
                            if g >= 2:
                                tensor.wait_ge(S["evac"], g - 1)
                            for c in range(8):
                                mm = tensor.matmul(
                                    pp_proj[g % 2][:, :],
                                    wsb(t)[:, DM * c + 128 * p: DM * c + 128 * (p + 1)],
                                    xnf_sb[s % 2][:, 512 * c:512 * (c + 1)],
                                    start=(c == 0), stop=(c == 7))
                                if c == 7:
                                    mm.then_inc(S["pmm"], 1)
                    tensor.wait_ge(S["evac"], 24 * s + 8)
                    for u in range(4 if not no_lr else 0):
                        k = 4 * s + u
                        if k >= 1:
                            tensor.wait_ge(S["va"], vaN[("l", k - 1)])
                        for p in range(8):
                            mm = tensor.matmul(
                                pp_lr[:, :],
                                kfm_sb[s % 2][:, 512 * p + 128 * u: 512 * p + 128 * (u + 1)],
                                thetab_sb[:, H * p:H * (p + 1)],
                                start=(p == 0), stop=(p == 7))
                            if p == 7:
                                mm.then_inc(S["pmm"], 1)
                if lvl >= 2:
                    batch_mms(NSLICE - 1)

            @blkA.scalar
            def _(scalar):
                def batch_sc(y):
                    for i in range(8 * y, 8 * y + 8):
                        m, u = i // 2, i % 2
                        scalar.wait_ge(S["va"], vaN[("e", m)])
                        if i >= 2:
                            scalar.wait_ge(S["va"], vaN[("c", i - 2, 3)])
                        last = None
                        for q in range(2):
                            src = _ap(eta_sb, (64 * u) * (NSLAB * H) + H * m + q,
                                      [[NSLAB * H, 64], [2, 8]])
                            dst_ap = _ap(eta_par[i % 2], (64 * q) * 8, [[8, 64], [1, 8]])
                            last = scalar.copy(dst_ap, src)
                        last.then_inc(S["sa"], 1)
                        scalar.wait_ge(S["va"], vaN[("c", i, 1)])
                        scalar.activation(sq_sb[:, :], z1b_sb[:, :],
                                          AF.Square).then_inc(S["sa"], 1)
                        scalar.wait_ge(S["va"], vaN[("c", i, 2)])
                        scalar.activation(c_std[:, :], c_var[:, :],
                                          AF.Sqrt).then_inc(S["sa"], 1)
                        scalar.wait_ge(S["va"], vaN[("c", i, 4)])
                        scalar.copy(wbf[:, :], w_par[:, :]).then_inc(S["sa"], 1)
                        scalar.wait_ge(S["va"], vaN[("c", i, 6)])
                        scalar.activation(sq_sb[:, :], z1b_sb[:, :],
                                          AF.Square).then_inc(S["sa"], 1)
                        scalar.wait_ge(S["va"], vaN[("c", i, 7)])
                        scalar.activation(c_std[:, :], c_var[:, :],
                                          AF.Sqrt).then_inc(S["sa"], 1)

                if lvl >= 2:
                    scalar.wait_ge(S["cld"], 16 * 6)
                    scalar.copy(wbf[:, :], w_par[:, :]).then_inc(S["sa"], 1)
                for s in range(NSLICE):
                    for u in range(4):
                        k = 4 * s + u
                        scalar.wait_ge(S["x_ld"], 16 * (k + 1))
                        scalar.activation(sqx_sb[:, :], x_slab[k % 2][:, :],
                                          AF.Square).then_inc(S["sa"], 1)
                        scalar.wait_ge(S["va"], vaN[("r", k)])
                        scalar.activation(st_r[:, :], st_t1[:, :],
                                          AF.Sqrt).then_inc(S["sa"], 1)
                    if lvl >= 1:
                        scalar.wait_ge(S["xnsp"], 16 * 4 * (s + 1))
                        if s >= 2:
                            scalar.wait_ge(S["pmm"], 28 * (s - 1))
                        for c in range(8):
                            scalar.dma_start(
                                out=xnf_sb[s % 2][:, 512 * c:512 * (c + 1)],
                                in_=xn_hbm[512 * s:512 * (s + 1), 128 * c:128 * (c + 1)],
                                transpose=True).then_inc(S["xnf_ld"], 16)
                    if lvl >= 2 and s >= 1:
                        batch_sc(s - 1)
                    if lvl >= 1:
                        if lvl >= 2 and s >= 2:
                            scalar.wait_ge(S["zqmm"], 8 * (s - 1))
                            mlast = 4 * (s - 2) + 3
                            scalar.wait_ge(S["txp%d" % (mlast % 2)],
                                           16 * (24 * (mlast // 2) + 24))
                            mprev = mlast - 1
                            scalar.wait_ge(S["txp%d" % (mprev % 2)],
                                           16 * (24 * (mprev // 2) + 24))
                        for t in range(3):
                            for p in range(8):
                                g = 24 * s + t * 8 + p
                                j = t * 8 + p
                                scalar.wait_ge(S["pmm"], PMM(s, j))
                                scalar.copy(fmsb(t)[s % 2][:, 512 * p:512 * (p + 1)],
                                            pp_proj[g % 2][:, :]).then_inc(S["evac"], 1)
                    if lvl >= 1 and not no_lr:
                        for u in range(4):
                            k = 4 * s + u
                            scalar.wait_ge(S["va"], vaN[("l", k)])
                            scalar.activation(sigall_sb[:, H * k:H * (k + 1)],
                                              lrall_sb[:, H * k:H * (k + 1)],
                                              AF.Sigmoid).then_inc(S["sa"], 1)
                if lvl >= 2:
                    batch_sc(NSLICE - 1)

            def batch_vec_impl(vector, y):
                for i in range(8 * y, 8 * y + 8):
                    m, u = i // 2, i % 2
                    vector.wait_ge(S["z1mm"], i + 1)
                    vector.tensor_add(z1b_sb[:, :], pp_z1[i % 2][:, :],
                                      b_mat[:, :]).then_inc(S["va"], 1)
                    vector.reduce_sum(c_s1[:, :], _r3(z1b_sb), axis=AXX)
                    vector.wait_ge(S["sa"], saN[("s", i, 2)])
                    vector.reduce_sum(c_s2[:, :], _r3(sq_sb), axis=AXX)
                    vector.tensor_scalar_mul(c_m[:, :], c_s1[:, :], -1.0 / 64.0)
                    vector.tensor_scalar(c_v0[:, :], c_s2[:, :], 64.0 * EPS_SCAN,
                                         1.0 / 64.0, ALU.add, ALU.mult)
                    vector.tensor_mul(c_t[:, :], c_m[:, :], c_m[:, :])
                    vector.tensor_scalar_mul(c_d[:, :], c_s1[:, :], 1.0)
                    vector.tensor_sub(c_var[:, :], c_v0[:, :],
                                      c_t[:, :]).then_inc(S["va"], 1)
                    vector.wait_ge(S["sa"], saN[("s", i, 3)])
                    vector.reciprocal(c_rstd[:, :], c_std[:, :])
                    z3 = _r3(z1b_sb)
                    vector.tensor_tensor(z3, z3, _grp(c_m), ALU.add)
                    vector.tensor_tensor(_r3(xh_sb), z3, _grp(c_rstd), ALU.mult)
                    vector.wait_ge(S["sa"], saN[("s", i, 1)])
                    vector.wait_ge(S["pc%d" % (i % 4)], 16 * (6 * (i // 4) + 4))
                    vector.tensor_mul(gxh_sb[:, :], xh_sb[:, :], g2par_sb[:, :])
                    vector.tensor_add(gxh_sb[:, :], gxh_sb[:, :], kvg_par[i % 2][:, :])
                    vector.reduce_sum(c_r1[:, :], _r3(gxh_sb), axis=AXX)
                    vector.tensor_mul(gx2_sb[:, :], gxh_sb[:, :], xh_sb[:, :])
                    vector.reduce_sum(c_r2[:, :], _r3(gx2_sb), axis=AXX)
                    vector.tensor_scalar_mul(c_a1[:, :], c_r1[:, :], 1.0 / 64.0)
                    vector.tensor_scalar_mul(c_a2[:, :], c_r2[:, :], 1.0 / 64.0)
                    vector.tensor_mul(c_eta2[:, :], eta_par[i % 2][:, :], c_rstd[:, :])
                    vector.tensor_tensor(_r3(u_sb), _r3(xh_sb), _grp(c_a2), ALU.mult)
                    vector.tensor_copy(c_eta2b[:, :], c_eta2[:, :])
                    vector.tensor_sub(grad_sb[:, :], gxh_sb[:, :], u_sb[:, :])
                    vector.tensor_tensor(_r3(grad_sb), _r3(grad_sb), _grp(c_a1), ALU.subtract)
                    vector.tensor_tensor(_r3(sck_sb), _r3(kc_par[i % 2]),
                                         _ap(c_eta2b, 0, [[8, 128], [1, 8], [0, 64]]),
                                         ALU.mult).then_inc(S["va"], 1)
                    vector.wait_ge(S["dwmm"], 2 * i + 2)
                    vector.tensor_sub(w_par[:, :], w_par[:, :],
                                      pp_dw[:, :]).then_inc(S["va"], 1)
                    vector.tensor_sub(b_mat[:, :], b_mat[:, :],
                                      pp_db[:, :]).then_inc(S["va"], 1)
                    vector.wait_ge(S["zqmm"], i + 1)
                    vector.tensor_add(z1b_sb[:, :], pp_zq[:, :],
                                      b_mat[:, :]).then_inc(S["va"], 1)
                    vector.reduce_sum(c_s1[:, :], _r3(z1b_sb), axis=AXX)
                    vector.wait_ge(S["sa"], saN[("s", i, 5)])
                    vector.reduce_sum(c_s2[:, :], _r3(sq_sb), axis=AXX)
                    vector.tensor_scalar_mul(c_m[:, :], c_s1[:, :], -1.0 / 64.0)
                    vector.tensor_scalar(c_v0[:, :], c_s2[:, :], 64.0 * EPS_SCAN,
                                         1.0 / 64.0, ALU.add, ALU.mult)
                    vector.tensor_mul(c_t[:, :], c_m[:, :], c_m[:, :])
                    vector.tensor_scalar_mul(c_d[:, :], c_s1[:, :], 1.0)
                    vector.tensor_sub(c_var[:, :], c_v0[:, :],
                                      c_t[:, :]).then_inc(S["va"], 1)
                    vector.wait_ge(S["sa"], saN[("s", i, 6)])
                    vector.reciprocal(c_rstd[:, :], c_std[:, :])
                    z3b = _r3(z1b_sb)
                    vector.tensor_tensor(z3b, z3b, _grp(c_m), ALU.add)
                    vector.tensor_tensor(_r3(w1_sb), z3b, _grp(c_rstd), ALU.mult)
                    vector.tensor_mul(w1_sb[:, :], w1_sb[:, :], gpar_sb[:, :])
                    vector.tensor_add(w1_sb[:, :], w1_sb[:, :], bepar_sb[:, :])
                    vector.wait_ge(S["pc%d" % (i % 4)], 16 * (6 * (i // 4) + 6))
                    if i >= 2:
                        vector.wait_ge(S["ac%d" % (i % 2)], 16 * 2 * (i // 2))
                    vector.tensor_add(outp_sb[i % 2][:, :], w1_sb[:, :],
                                      qc_par[i % 2][:, :]).then_inc(S["va"], 1)

            @blkA.vector
            def _(vector):
                def batch_vec(y):
                    batch_vec_impl(vector, y)

                vector.wait_ge(S["wld"], WLD_ALL)
                vector.wait_ge(S["cld"], 16 * 6)
                vector.tensor_add(etab_sb[:, :], invk_sb[:, :], alpha2_sb[:, :])
                vector.tensor_mul(g2par_sb[:, :], gpar_sb[:, :], gpar_sb[:, :])
                vector.tensor_mul(gbetm_sb[:, :], gtm_sb[:, :],
                                  betm_sb[:, :]).then_inc(S["pre"], 1)
                vector.tensor_relu(etab_sb[:, :], etab_sb[:, :])
                for s in range(NSLICE):
                    for u in range(4):
                        k = 4 * s + u
                        vector.wait_ge(S["x_ld"], 16 * (k + 1))
                        vector.reduce_sum(st_s1[:, :], x_slab[k % 2][:, :], axis=AXX)
                        vector.wait_ge(S["sa"], saN[("q", k)])
                        vector.reduce_sum(st_s2[:, :], sqx_sb[:, :], axis=AXX)
                        vector.tensor_scalar_mul(st_m[:, :], st_s1[:, :], -1.0 / DM)
                        vector.tensor_scalar(st_v[:, :], st_s2[:, :], DM * EPS_MAIN,
                                             1.0 / DM, ALU.add, ALU.mult)
                        vector.tensor_mul(st_t2[:, :], st_m[:, :], st_m[:, :])
                        vector.tensor_scalar_mul(st_d[:, :], st_s1[:, :], 1.0)
                        vector.tensor_sub(st_v[:, :], st_v[:, :], st_t2[:, :])
                        vector.tensor_scalar_mul(st_d[:, :], st_s1[:, :], 1.0)
                        vector.reciprocal(st_t1[:, :], st_v[:, :]).then_inc(S["va"], 1)
                        vector.wait_ge(S["sa"], saN[("t", k)])
                        if k >= 4:
                            vector.wait_ge(S["xnsp"], 16 * (k - 3))
                            if not no_dyn:
                                vector.wait_ge(S["xnsp2"], 16 * (k // 4))
                        vector.tensor_scalar(xnstg[:, DM * (k % 4):DM * (k % 4 + 1)],
                                             x_slab[k % 2][:, :],
                                             st_m[:, :], st_r[:, :],
                                             ALU.add, ALU.mult).then_inc(S["xn_done"], 1)
                    if lvl >= 2 and s >= 1:
                        batch_vec(s - 1)
                    for u in range(4 if (lvl >= 1 and not no_lr) else 0):
                        k = 4 * s + u
                        vector.wait_ge(S["pmm"], PMM_LR(s, u))
                        vector.tensor_add(lrall_sb[:, H * k:H * (k + 1)], pp_lr[:, :],
                                          tbmat_sb[:, :]).then_inc(S["va"], 1)
                    for u in range(4 if (lvl >= 1 and not no_lr) else 0):
                        k = 4 * s + u
                        vector.wait_ge(S["sa"], saN[("g", k)])
                        vector.tensor_scalar(
                            eta_sb[:, H * k:H * (k + 1)],
                            sigall_sb[:, H * k:H * (k + 1)], etab_sb[:, :], 1.0 / 64.0,
                            ALU.mult, ALU.mult).then_inc(S["va"], 1)
                if lvl >= 2:
                    batch_vec(NSLICE - 1)

        if lvl < 3:
            return nc
        with ExitStack() as pb1:
            b1 = pb1.enter_context
            wg_sb = b1(nc.sbuf_tensor("wg_sb", [128, 8 * HIDDEN], BF16))
            wu_sb = b1(nc.sbuf_tensor("wu_sb", [128, 8 * HIDDEN], BF16))
            xnfB = [b1(nc.sbuf_tensor(f"xnfB{i}", [128, 8 * 512], BF16)) for i in range(2)]
            h_sb = b1(nc.sbuf_tensor("h_sb", [128, 24 * 512], BF16))
            sg_sb = b1(nc.sbuf_tensor("sg_sb", [128, 512], F32))
            pp_gate = [b1(nc.psum_tensor(f"pp_gate{i}", [128, 512], F32)) for i in range(2)]
            pp_up = [b1(nc.psum_tensor(f"pp_up{i}", [128, 512], F32)) for i in range(2)]

            blkB1 = pb1.enter_context(nc.Block())

            @blkB1.sync
            def _(sync):
                sync.sem_inc(S["phb"], 1)
                for j in range(4):
                    for oc in range(24):
                        sync.wait_ge(S["hready"], 24 * j + oc + 1)
                        sync.dma_start(
                            out=_ap(h_hbm, (4 * j) * 128 * HIDDEN + 128 * oc,
                                    [[HIDDEN, 128], [128 * HIDDEN, 4], [1, 128]]),
                            in_=h_sb[:, 512 * oc:512 * (oc + 1)],
                        ).then_inc(S["hsp"], 16)
                if lvl == 3:
                    sync.wait_ge(S["hsp"], 16 * 24 * 4)

            @blkB1.gpsimd
            def _(gpsimd):
                gpsimd.wait_ge(S["phb"], 1)
                gpsimd.wait_ge(S["wdc"], 16)
                for c in range(8):
                    gpsimd.dma_start(out=wg_sb[:, HIDDEN * c:HIDDEN * (c + 1)],
                                     in_=wgbf_hbm[128 * c:128 * (c + 1), :],
                                     ).then_inc(S["fwl"], 16)
                gpsimd.wait_ge(S["wdc"], 32)
                for c in range(8):
                    gpsimd.dma_start(out=wu_sb[:, HIDDEN * c:HIDDEN * (c + 1)],
                                     in_=wubf_hbm[128 * c:128 * (c + 1), :],
                                     ).then_inc(S["fwl"], 16)

            @blkB1.scalar
            def _(scalar):
                scalar.wait_ge(S["phb"], 1)
                for j in range(2):
                    for c in range(8):
                        scalar.dma_start(
                            out=xnfB[j % 2][:, 512 * c:512 * (c + 1)],
                            in_=xn2_hbm[L + 512 * j:L + 512 * (j + 1),
                                        128 * c:128 * (c + 1)],
                            transpose=True).then_inc(S["bxn"], 16)
                for j in range(4):
                    for oc in range(24):
                        g = 24 * j + oc
                        scalar.wait_ge(S["gumm"], 2 * g + 1)
                        if g >= 1:
                            scalar.wait_ge(S["hready"], g)
                        scalar.activation(sg_sb[:, :], pp_gate[g % 2][:, :],
                                          AF.Silu).then_inc(S["silu"], 1)
                    jj = j + 2
                    if jj <= 3:
                        scalar.wait_ge(S["gumm"], 48 * (jj - 1))
                        for c in range(8):
                            scalar.dma_start(
                                out=xnfB[jj % 2][:, 512 * c:512 * (c + 1)],
                                in_=xn2_hbm[L + 512 * jj:L + 512 * (jj + 1),
                                            128 * c:128 * (c + 1)],
                                transpose=True).then_inc(S["bxn"], 16)

            @blkB1.tensor
            def _(tensor):
                tensor.wait_ge(S["fwl"], 16 * 8)
                for j in range(4):
                    tensor.wait_ge(S["bxs"], j + 1)
                    for oc in range(24):
                        g = 24 * j + oc
                        if g >= 2:
                            tensor.wait_ge(S["silu"], g - 1)
                            tensor.wait_ge(S["hready"], g - 1)
                        for c in range(8):
                            mm = tensor.matmul(
                                pp_gate[g % 2][:, :],
                                wg_sb[:, HIDDEN * c + 128 * oc: HIDDEN * c + 128 * (oc + 1)],
                                xnfB[j % 2][:, 512 * c:512 * (c + 1)],
                                start=(c == 0), stop=(c == 7))
                            if c == 7:
                                mm.then_inc(S["gumm"], 1)
                        if g == 0:
                            tensor.wait_ge(S["fws"], 1)
                        for c in range(8):
                            mm = tensor.matmul(
                                pp_up[g % 2][:, :],
                                wu_sb[:, HIDDEN * c + 128 * oc: HIDDEN * c + 128 * (oc + 1)],
                                xnfB[j % 2][:, 512 * c:512 * (c + 1)],
                                start=(c == 0), stop=(c == 7))
                            if c == 7:
                                mm.then_inc(S["gumm"], 1)

            @blkB1.vector
            def _(vector):
                vector.wait_ge(S["phb"], 1)
                for j in range(4):
                    vector.wait_ge(S["bxn"], 16 * 8 * (j + 1))
                    vector.tensor_scalar_mul(xnfB[j % 2][:, :], xnfB[j % 2][:, :],
                                             1.0 / 127.0).then_inc(S["bxs"], 1)
                    if j == 0:
                        vector.wait_ge(S["fwl"], 16 * 16)
                        vector.tensor_scalar_mul(wu_sb[:, :], wu_sb[:, :],
                                                 1.0 / 127.0).then_inc(S["fws"], 1)
                    vector.wait_ge(S["hsp"], 16 * 24 * j)
                    for oc in range(24):
                        g = 24 * j + oc
                        vector.wait_ge(S["silu"], g + 1)
                        vector.wait_ge(S["gumm"], 2 * g + 2)
                        vector.tensor_mul(h_sb[:, 512 * oc:512 * (oc + 1)],
                                          sg_sb[:, :],
                                          pp_up[g % 2][:, :]).then_inc(S["hready"], 1)

        if lvl < 4:
            return nc
        with ExitStack() as pb2:
            b2 = pb2.enter_context
            wd_sb = b2(nc.sbuf_tensor("wd_sb", [128, 24 * DM], BF16))
            wo_sb = b2(nc.sbuf_tensor("wo_sb", [128, 8 * DM], BF16))
            afm_sb = b2(nc.sbuf_tensor("afm_sb", [128, 8 * TOUT], BF16))
            hsub_sb = [b2(nc.sbuf_tensor(f"hsub{i}", [128, 24 * 128], BF16)) for i in range(2)]
            xres_sb = [b2(nc.sbuf_tensor(f"xres{i}", [128, 4 * DM], F32)) for i in range(2)]
            out_sb = [b2(nc.sbuf_tensor(f"outsb{i}", [128, DM], F32)) for i in range(2)]
            pp_out = [b2(nc.psum_tensor(f"pp_out{i}", [128, DM], F32)) for i in range(2)]

            blkB2 = pb2.enter_context(nc.Block())

            B2_GATE = (16 * 24 * 4)

            @blkB2.sync
            def _(sync):
                if True:
                    for r in range(4):
                        sync.wait_ge(S["gumm"], 192)
                        sync.wait_ge(S["asp"], 16 * 8)
                        sync.dma_start(
                            out=afm_sb[:, TOUT * (4 + r):TOUT * (5 + r)],
                            in_=attn2_hbm[L:L + TOUT, 128 * (4 + r):128 * (5 + r)],
                            transpose=True).then_inc(S["afm_ld"], 16)
                    for r in range(16):
                        sync.wait_ge(S["hsp"], 16 * 24 * (r // 4 + 1))
                        if r >= 2:
                            sync.wait_ge(S["omm"], 2 * (r - 1))
                        sync.dma_start(
                            out=hsub_sb[r % 2][:, :],
                            in_=h_hbm[128 * r:128 * (r + 1), :],
                        ).then_inc(S["hld"], 16)
                        if r % 4 == 0:
                            g = r // 4
                            if g == 0:
                                sync.wait_ge(S["hready"], 96)
                                sync.wait_ge(S["hsp"], 16 * 24 * 4)
                            if g >= 2:
                                sync.wait_ge(S["oev"], 4 * (g - 1))
                            v = sync.value_load(dynr_sb[8 + g:8 + g + 1, 0:1])
                            sync.dma_start(
                                out=xres_sb[g % 2][:, :].rearrange(
                                    "t (u d) -> t u d", u=4),
                                in_=x_seq[ds(v, 512), :].rearrange(
                                    "(u t) d -> t u d", t=128)).then_inc(S["xres_ld"], 16)
                        sync.wait_ge(S["oev"], r + 1)
                        sync.dma_start(out=out_d[128 * r:128 * (r + 1), :],
                                       in_=out_sb[r % 2][:, :]).then_inc(S["ost"], 16)
                    sync.wait_ge(S["ost"], 16 * 16)

            @blkB2.gpsimd
            def _(gpsimd):
                gpsimd.wait_ge(S["gumm"], 192)
                gpsimd.wait_ge(S["wdc"], 16 * 4)
                for c in range(24):
                    gpsimd.dma_start(out=wd_sb[:, DM * c:DM * (c + 1)],
                                     in_=wdbf_hbm[128 * c:128 * (c + 1), :],
                                     ).then_inc(S["fwl"], 16)
                for c in range(8):
                    gpsimd.dma_start(out=wo_sb[:, DM * c:DM * (c + 1)],
                                     in_=wobf_hbm[128 * c:128 * (c + 1), :],
                                     ).then_inc(S["fwl"], 16)

            @blkB2.scalar
            def _(scalar):
                scalar.wait_ge(S["gumm"], 192)
                scalar.wait_ge(S["asp"], 16 * 8)
                for c in range(4):
                    scalar.dma_start(
                        out=afm_sb[:, TOUT * c:TOUT * (c + 1)],
                        in_=attn2_hbm[L:L + TOUT, 128 * c:128 * (c + 1)],
                        transpose=True).then_inc(S["afm_ld"], 16)

            @blkB2.tensor
            def _(tensor):
                tensor.wait_ge(S["fwl"], 16 * 48)
                for r in range(16):
                    tensor.wait_ge(S["hld"], 16 * (r + 1))
                    if r >= 2:
                        tensor.wait_ge(S["oev"], r - 1)
                    for half in range(2):
                        for oc in range(24):
                            tensor.matmul(
                                pp_out[r % 2][:, 512 * half:512 * (half + 1)],
                                hsub_sb[r % 2][:, 128 * oc:128 * (oc + 1)],
                                wd_sb[:, DM * oc + 512 * half: DM * oc + 512 * (half + 1)],
                                start=(oc == 0), stop=False)
                        if r == 0 and half == 0:
                            tensor.wait_ge(S["afm_ld"], 16 * 8)
                        for c in range(8):
                            mm = tensor.matmul(
                                pp_out[r % 2][:, 512 * half:512 * (half + 1)],
                                afm_sb[:, TOUT * c + 128 * r: TOUT * c + 128 * (r + 1)],
                                wo_sb[:, DM * c + 512 * half: DM * c + 512 * (half + 1)],
                                start=False, stop=(c == 7))
                            if c == 7:
                                mm.then_inc(S["omm"], 1)

            @blkB2.vector
            def _(vector):
                for r in range(16):
                    vector.wait_ge(S["omm"], 2 * (r + 1))
                    vector.wait_ge(S["xres_ld"], 16 * (r // 4 + 1))
                    if r >= 2:
                        vector.wait_ge(S["ost"], 16 * (r - 1))
                    vector.tensor_add(out_sb[r % 2][:, :], pp_out[r % 2][:, :],
                                      xres_sb[(r // 4) % 2][:, DM * (r % 4):DM * (r % 4 + 1)],
                                      ).then_inc(S["oev"], 1)
    return nc


_CACHE = {}


def _stage(inputs):
    x = np.ascontiguousarray(np.asarray(inputs["L0_attn_hidden_states"], np.float32))
    theta = np.asarray(inputs["L0_attn_theta"], np.float32)
    thetab = np.zeros((DM, H), np.float32)
    for h in range(H):
        thetab[64 * h:64 * h + 64, h] = theta[h, :, 0]
    tbmat = np.broadcast_to(
        np.asarray(inputs["L0_attn_theta_bias"], np.float32).reshape(1, H), (128, H))
    invk = np.tile(1.0 / np.arange(1, 65, dtype=np.float32), 2)[:, None]
    alpha2 = np.tile(np.asarray(inputs["L0_attn_alpha"], np.float32), 2)[:, None]
    gamma = np.asarray(inputs["L0_attn_gamma"], np.float32)[:, 0, :]
    beta = np.asarray(inputs["L0_attn_beta"], np.float32)[:, 0, :]
    W1 = np.asarray(inputs["L0_attn_W1"], np.float32)
    b1 = np.asarray(inputs["L0_attn_b1"], np.float32)[:, 0, :]

    def to_par(v):
        out = np.zeros((128, 512), np.float32)
        for p in range(8):
            for q in range(2):
                out[q * 64:q * 64 + 64, p * 64:p * 64 + 64] = v[2 * p + q][None, :]
        return out
    gpar, bepar, b0par = to_par(gamma), to_par(beta), to_par(b1)
    w0par = np.zeros((128, 512), np.float32)
    for p in range(8):
        for q in range(2):
            w0par[q * 64:q * 64 + 64, p * 64:p * 64 + 64] = W1[2 * p + q]
    gtm = np.zeros((1, DM), np.float32)
    betm = np.zeros((1, DM), np.float32)
    for j in range(H):
        gtm[0, 64 * j:64 * j + 64] = gamma[j]
        betm[0, 64 * j:64 * j + 64] = beta[j]
    shared = dict(
        wq=np.ascontiguousarray(np.asarray(inputs["L0_attn_wq"], np.float32)),
        wk=np.ascontiguousarray(np.asarray(inputs["L0_attn_wk"], np.float32)),
        wv=np.ascontiguousarray(np.asarray(inputs["L0_attn_wv"], np.float32)),
        wo=np.ascontiguousarray(np.asarray(inputs["L0_attn_wo"], np.float32)),
        thetab=thetab,
        tbmat=np.ascontiguousarray(tbmat).astype(np.float32),
        invk=invk, alpha2=np.ascontiguousarray(alpha2),
        gpar=gpar, bepar=bepar,
        gtm=np.ascontiguousarray(np.broadcast_to(gtm, (128, DM))),
        betm=np.ascontiguousarray(np.broadcast_to(betm, (128, DM))),
        w0par=w0par, b0par=b0par,
        wgT=np.ascontiguousarray(np.asarray(inputs["L0_ffn_w_gate"], np.int32).T),
        wuT=np.ascontiguousarray(np.asarray(inputs["L0_ffn_w_up"], np.int32).T),
        wdT=np.ascontiguousarray(np.asarray(inputs["L0_ffn_w_down"], np.int32).T),
    )
    in_maps = []
    for core in range(8):
        b, half = core // 2, core % 2
        row_off = half * TOUT
        dynr_np = np.zeros((NSLAB + 16, 1), np.uint32)
        for g in range(8):
            dynr_np[g, 0] = (L - row_off) + 512 * g
        for g in range(4):
            dynr_np[8 + g, 0] = row_off + 512 * g
        m = dict(shared)
        m["x_seq"] = np.ascontiguousarray(x[b])
        m["dynr"] = dynr_np
        in_maps.append(m)
    return in_maps


def kernel(**inputs):
    if "nc" not in _CACHE:
        _CACHE["nc"] = build_nc()
    nc = _CACHE["nc"]
    in_maps = _stage(inputs)
    res = run_bass_kernel_spmd(nc, in_maps, core_ids=list(range(8)))
    out = np.zeros((4, L, DM), np.float32)
    for core in range(8):
        b, half = core // 2, core % 2
        out[b, half * TOUT:(half + 1) * TOUT] = res.results[core]["out"]
    return out


# version 11
# speedup vs baseline: 1.1266x; 1.1266x over previous
import numpy as np
import ml_dtypes
from contextlib import ExitStack

import concourse.bass as bass
import concourse.mybir as mybir
from concourse.bass_utils import run_bass_kernel_spmd

F32 = mybir.dt.float32
BF16 = mybir.dt.bfloat16
I32 = mybir.dt.int32
U32 = mybir.dt.uint32
AF = mybir.ActivationFunctionType
ALU = mybir.AluOpType
AXX = mybir.AxisListType.X
ds = bass.ds
AP = bass.AP

L, DM, H, HD = 4096, 1024, 16, 64
NSLAB, NSLICE, NCH = 32, 8, 64
HIDDEN = 3072
TOUT = 2048
EPS_MAIN, EPS_SCAN = 1e-5, 1e-6


def _ap(t, offset, dims):
    tensor = t.tensor if hasattr(t, "tensor") else t
    return AP(tensor, offset, [list(d) for d in dims])


def _grp(t):
    return _ap(t, 0, [[8, 128], [1, 8], [0, 64]])


def _r3(t):
    return t[:, :].rearrange("p (g d) -> p g d", g=8)


def _build_tables(lvl, no_lr):
    vaN, saN = {}, {}
    va, sa = [0], [0]

    def V(*tag):
        va[0] += 1
        vaN[tag] = va[0]

    def A(*tag):
        sa[0] += 1
        saN[tag] = sa[0]

    if lvl >= 2:
        A("c0")
    for s in range(NSLICE):
        for u in range(4):
            k = 4 * s + u
            A("q", k)
            A("t", k)
        if lvl >= 1 and not no_lr:
            for u in range(4):
                A("g", 4 * s + u)
        if lvl >= 2 and s >= 1:
            for i in range(8 * (s - 1), 8 * s):
                for n in range(1, 7):
                    A("s", i, n)
    if lvl >= 2:
        for i in range(56, 64):
            for n in range(1, 7):
                A("s", i, n)
    for s in range(NSLICE):
        for u in range(4):
            V("r", 4 * s + u)
        if lvl >= 1 and not no_lr:
            for u in range(4):
                V("l", 4 * s + u)
            for u in range(4):
                V("e", 4 * s + u)
        if lvl >= 2 and s >= 1:
            for i in range(8 * (s - 1), 8 * s):
                for n in range(1, 9):
                    V("c", i, n)
    if lvl >= 2:
        for i in range(56, 64):
            for n in range(1, 9):
                V("c", i, n)
    return vaN, saN


def PMM(s, j):
    return 28 * s + j + 1


def PMM_LR(s, u):
    return 28 * s + 24 + u + 1


PH = {"ln": 0, "a1": 1, "a2": 2, "b1": 3, "full": 4}


def build_nc(upto="full", dbg=False, no_lr=False, no_dyn=False):
    lvl = PH[upto]
    vaN, saN = _build_tables(lvl, no_lr)
    nc = bass.Bass("TRN2", target_bir_lowering=False, detect_race_conditions=False)
    scratch_kind = "ExternalOutput" if dbg else "Internal"

    x_seq = nc.dram_tensor("x_seq", [L, DM], F32, kind="ExternalInput")
    dynr = nc.dram_tensor("dynr", [NSLAB + 16, 1], U32, kind="ExternalInput")
    wq_d = nc.dram_tensor("wq", [DM, DM], F32, kind="ExternalInput")
    wk_d = nc.dram_tensor("wk", [DM, DM], F32, kind="ExternalInput")
    wv_d = nc.dram_tensor("wv", [DM, DM], F32, kind="ExternalInput")
    wo_d = nc.dram_tensor("wo", [DM, DM], F32, kind="ExternalInput")
    thetab_d = nc.dram_tensor("thetab", [DM, H], F32, kind="ExternalInput")
    tbmat_d = nc.dram_tensor("tbmat", [128, H], F32, kind="ExternalInput")
    invk_d = nc.dram_tensor("invk", [128, 1], F32, kind="ExternalInput")
    alpha2_d = nc.dram_tensor("alpha2", [128, 1], F32, kind="ExternalInput")
    gpar_d = nc.dram_tensor("gpar", [128, 512], F32, kind="ExternalInput")
    bepar_d = nc.dram_tensor("bepar", [128, 512], F32, kind="ExternalInput")
    gtm_d = nc.dram_tensor("gtm", [128, DM], F32, kind="ExternalInput")
    betm_d = nc.dram_tensor("betm", [128, DM], F32, kind="ExternalInput")
    w0par_d = nc.dram_tensor("w0par", [128, 512], F32, kind="ExternalInput")
    b0par_d = nc.dram_tensor("b0par", [128, 512], F32, kind="ExternalInput")
    wgT_d = nc.dram_tensor("wgT", [DM, HIDDEN], I32, kind="ExternalInput")
    wuT_d = nc.dram_tensor("wuT", [DM, HIDDEN], I32, kind="ExternalInput")
    wdT_d = nc.dram_tensor("wdT", [HIDDEN, DM], I32, kind="ExternalInput")
    out_d = nc.dram_tensor("out", [TOUT, DM], F32, kind="ExternalOutput")

    xn_hbm = nc.dram_tensor("xn_hbm", [L, DM], BF16, kind=scratch_kind)
    xn2_hbm = nc.dram_tensor("xn2_hbm", [2 * L, DM], BF16, kind=scratch_kind)
    kfm_hbm = nc.dram_tensor("kfm_hbm", [DM, L], BF16, kind=scratch_kind)
    qfm_hbm = nc.dram_tensor("qfm_hbm", [DM, L], BF16, kind=scratch_kind)
    vfm_hbm = nc.dram_tensor("vfm_hbm", [DM, L], BF16, kind=scratch_kind)
    attn2_hbm = nc.dram_tensor("attn2_hbm", [2 * L, DM], BF16, kind=scratch_kind)
    h_hbm = nc.dram_tensor("h_hbm", [TOUT, HIDDEN], BF16, kind=scratch_kind)
    wgbf_hbm = nc.dram_tensor("wgbf_hbm", [DM, HIDDEN], BF16, kind="Internal")
    wubf_hbm = nc.dram_tensor("wubf_hbm", [DM, HIDDEN], BF16, kind="Internal")
    wdbf_hbm = nc.dram_tensor("wdbf_hbm", [HIDDEN, DM], BF16, kind="Internal")
    wobf_hbm = nc.dram_tensor("wobf_hbm", [DM, DM], BF16, kind="Internal")

    with ExitStack() as top:
        e = top.enter_context
        sem_names = [
            "cld", "wld", "x_ld", "xn_done", "xnsp", "xnsp2", "xnf_ld",
            "pmm", "evac", "ksp", "txp0", "txp1", "kvg",
            "va", "sa", "z1mm", "dwmm", "zqmm", "asp",
            "pc0", "pc1", "pc2", "pc3", "ac0", "ac1", "pre", "wdc", "bxs",
            "phb", "fwl", "fws", "bxn", "gumm", "silu", "hready",
            "hsp", "hld", "afm_ld", "xres_ld", "omm", "oev", "ost",
        ]
        S = {n: e(nc.semaphore(n)) for n in sem_names}
        dynr_sb = e(nc.sbuf_tensor("dynr_sb", [NSLAB + 16, 1], U32))

        with ExitStack() as pa:
            a = pa.enter_context
            thetab_sb = a(nc.sbuf_tensor("thetab_sb", [128, 8 * H], BF16))
            tbmat_sb = a(nc.sbuf_tensor("tbmat_sb", [128, H], F32))
            invk_sb = a(nc.sbuf_tensor("invk_sb", [128, 1], F32))
            alpha2_sb = a(nc.sbuf_tensor("alpha2_sb", [128, 1], F32))
            etab_sb = a(nc.sbuf_tensor("etab_sb", [128, 1], F32))
            gpar_sb = a(nc.sbuf_tensor("gpar_sb", [128, 512], BF16))
            bepar_sb = a(nc.sbuf_tensor("bepar_sb", [128, 512], BF16))
            g2par_sb = a(nc.sbuf_tensor("g2par_sb", [128, 512], BF16))
            gtm_sb = a(nc.sbuf_tensor("gtm_sb", [128, DM], BF16))
            betm_sb = a(nc.sbuf_tensor("betm_sb", [128, DM], BF16))
            gbetm_sb = a(nc.sbuf_tensor("gbetm_sb", [128, DM], BF16))
            wq_sb = a(nc.sbuf_tensor("wq_sb", [128, 8 * DM], BF16))
            wk_sb = a(nc.sbuf_tensor("wk_sb", [128, 8 * DM], BF16))
            wv_sb = a(nc.sbuf_tensor("wv_sb", [128, 8 * DM], BF16))
            x_slab = [a(nc.sbuf_tensor(f"x_slab{i}", [128, DM], F32)) for i in range(2)]
            sqx_sb = a(nc.sbuf_tensor("sqx_sb", [128, DM], F32))
            xnstg = a(nc.sbuf_tensor("xnstg", [128, 4 * DM], BF16))
            xnf_sb = [a(nc.sbuf_tensor(f"xnf_sb{i}", [128, 8 * 512], BF16)) for i in range(2)]
            kfm_sb = [a(nc.sbuf_tensor(f"kfm_sb{i}", [128, 8 * 512], BF16)) for i in range(2)]
            qfm_sb = [a(nc.sbuf_tensor(f"qfm_sb{i}", [128, 8 * 512], BF16)) for i in range(2)]
            vfm_sb = [a(nc.sbuf_tensor("vfm_sb0", [128, 8 * 512], BF16))] * 2
            st_s1 = a(nc.sbuf_tensor("st_s1", [128, 1], F32))
            st_s2 = a(nc.sbuf_tensor("st_s2", [128, 1], F32))
            st_m = a(nc.sbuf_tensor("st_m", [128, 1], F32))
            st_t1 = a(nc.sbuf_tensor("st_t1", [128, 1], F32))
            st_t2 = a(nc.sbuf_tensor("st_t2", [128, 1], F32))
            st_v = a(nc.sbuf_tensor("st_v", [128, 1], F32))
            st_d = a(nc.sbuf_tensor("st_d", [128, 1], F32))
            st_r = a(nc.sbuf_tensor("st_r", [128, 1], F32))
            lrall_sb = a(nc.sbuf_tensor("lrall_sb", [128, NSLAB * H], F32))
            sigall_sb = a(nc.sbuf_tensor("sigall_sb", [128, NSLAB * H], F32))
            eta_sb = a(nc.sbuf_tensor("eta_sb", [128, NSLAB * H], F32))
            txk_sb = [a(nc.sbuf_tensor(f"txk_sb{i}", [128, DM], BF16)) for i in range(2)]
            txq_sb = [a(nc.sbuf_tensor(f"txq_sb{i}", [128, DM], BF16)) for i in range(2)]
            txv_sb = [a(nc.sbuf_tensor(f"txv_sb{i}", [128, DM], BF16)) for i in range(2)]
            kvg_sb = [a(nc.sbuf_tensor(f"kvg_sb{i}", [128, DM], BF16)) for i in range(2)]
            w_par = a(nc.sbuf_tensor("w_par", [128, 512], F32))
            wbf = a(nc.sbuf_tensor("wbf", [128, 512], BF16))
            b_mat = a(nc.sbuf_tensor("b_mat", [128, 512], F32))
            kc_par = [a(nc.sbuf_tensor(f"kc_par{j}", [128, 512], BF16)) for j in range(2)]
            qc_par = [a(nc.sbuf_tensor(f"qc_par{j}", [128, 512], BF16)) for j in range(2)]
            kvg_par = [a(nc.sbuf_tensor(f"kvg_par{j}", [128, 512], BF16)) for j in range(2)]
            eta_par = [a(nc.sbuf_tensor(f"eta_par{j}", [128, 8], F32)) for j in range(2)]
            z1b_sb = a(nc.sbuf_tensor("z1b_sb", [128, 512], F32))
            sq_sb = a(nc.sbuf_tensor("sq_sb", [128, 512], F32))
            xh_sb = a(nc.sbuf_tensor("xh_sb", [128, 512], BF16))
            gxh_sb = a(nc.sbuf_tensor("gxh_sb", [128, 512], BF16))
            gx2_sb = a(nc.sbuf_tensor("gx2_sb", [128, 512], BF16))
            u_sb = a(nc.sbuf_tensor("u_sb", [128, 512], BF16))
            grad_sb = a(nc.sbuf_tensor("grad_sb", [128, 512], BF16))
            sck_sb = a(nc.sbuf_tensor("sck_sb", [128, 512], BF16))
            w1_sb = a(nc.sbuf_tensor("w1_sb", [128, 512], BF16))
            outp_sb = [a(nc.sbuf_tensor(f"outp_sb{j}", [128, 512], BF16)) for j in range(2)]
            c_s1 = a(nc.sbuf_tensor("c_s1", [128, 8], F32))
            c_s2 = a(nc.sbuf_tensor("c_s2", [128, 8], F32))
            c_m = a(nc.sbuf_tensor("c_m", [128, 8], F32))
            c_t = a(nc.sbuf_tensor("c_t", [128, 8], F32))
            c_v0 = a(nc.sbuf_tensor("c_v0", [128, 8], F32))
            c_d = a(nc.sbuf_tensor("c_d", [128, 8], F32))
            c_var = a(nc.sbuf_tensor("c_var", [128, 8], F32))
            c_std = a(nc.sbuf_tensor("c_std", [128, 8], F32))
            c_rstd = a(nc.sbuf_tensor("c_rstd", [128, 8], F32))
            c_r1 = a(nc.sbuf_tensor("c_r1", [128, 8], F32))
            c_r2 = a(nc.sbuf_tensor("c_r2", [128, 8], F32))
            c_a1 = a(nc.sbuf_tensor("c_a1", [128, 8], F32))
            c_a2 = a(nc.sbuf_tensor("c_a2", [128, 8], F32))
            c_eta2 = a(nc.sbuf_tensor("c_eta2", [128, 8], F32))
            c_eta2b = a(nc.sbuf_tensor("c_eta2b", [128, 8], BF16))
            attn_stage = a(nc.sbuf_tensor("attn_stage", [128, 4 * DM], BF16))
            pp_proj = [a(nc.psum_tensor(f"pp_proj{i}", [128, 512], F32)) for i in range(2)]
            pp_lr = a(nc.psum_tensor("pp_lr", [128, H], F32))
            pp_z1 = [a(nc.psum_tensor(f"pp_z1{i}", [128, 512], F32)) for i in range(2)]
            pp_zq = a(nc.psum_tensor("pp_zq", [128, 512], F32))
            pp_dw = a(nc.psum_tensor("pp_dw", [128, 512], F32))
            pp_db = a(nc.psum_tensor("pp_db", [128, 512], F32))

            blkA = pa.enter_context(nc.Block())

            def wsb(t):
                return (wk_sb, wq_sb, wv_sb)[t]

            def fmsb(t):
                return (kfm_sb, qfm_sb, vfm_sb)[t]

            def fmhbm(t):
                return (kfm_hbm, qfm_hbm, vfm_hbm)[t]

            @blkA.sync
            def _(sync):
                n_cld = 0
                for dst, src in ((tbmat_sb, tbmat_d), (invk_sb, invk_d),
                                 (alpha2_sb, alpha2_d), (w_par, w0par_d),
                                 (b_mat, b0par_d), (dynr_sb, dynr)):
                    sync.dma_start(out=dst[:, :], in_=src[:, :]).then_inc(S["cld"], 16)
                    n_cld += 1
                assert n_cld == 6
                sync.wait_ge(S["cld"], 16 * 6)
                if True:
                    def kqv_spill(s):
                        for t in range(3):
                            for p in range(8):
                                sync.wait_ge(S["evac"], 24 * s + t * 8 + p + 1)
                                sync.dma_start(
                                    out=fmhbm(t)[128 * p:128 * (p + 1), 512 * s:512 * (s + 1)],
                                    in_=fmsb(t)[s % 2][:, 512 * p:512 * (p + 1)],
                                ).then_inc(S["ksp"], 16)
                    def tx_load(m):
                        txs = S["txp%d" % (m % 2)]
                        sync.wait_ge(S["ksp"], 16 * 24 * (m // 4 + 1))
                        if m >= 2:
                            sync.wait_ge(S["kvg"], m - 1)
                            j = 2 * m - 3
                            sync.wait_ge(S["pc%d" % (j % 4)],
                                         16 * (6 * (j // 4) + 6))
                        sync.dma_start(out=txk_sb[m % 2][:, :],
                                       in_=kfm_hbm[:, 128 * m:128 * (m + 1)],
                                       transpose=True).then_inc(txs, 16)
                        sync.dma_start(out=txv_sb[m % 2][:, :],
                                       in_=vfm_hbm[:, 128 * m:128 * (m + 1)],
                                       transpose=True).then_inc(txs, 16)
                        sync.dma_start(out=txq_sb[m % 2][:, :],
                                       in_=qfm_hbm[:, 128 * m:128 * (m + 1)],
                                       transpose=True).then_inc(txs, 16)

                    def attn_spill(g):
                        sync.wait_ge(S["ac0"], 16 * 2 * (4 * g + 4))
                        sync.wait_ge(S["ac1"], 16 * 2 * (4 * g + 4))
                        v = sync.value_load(dynr_sb[g:g + 1, 0:1])
                        sync.dma_start(
                            out=attn2_hbm[ds(v, 512), :].rearrange(
                                "(u t) d -> t u d", t=128),
                            in_=_ap(attn_stage, 0,
                                    [[4 * DM, 128], [DM, 4], [1, DM]]),
                        ).then_inc(S["asp"], 16)

                    def batch_sync(y):
                        if y >= 1:
                            attn_spill(y - 1)
                        tx_load(4 * y)
                        tx_load(4 * y + 1)
                        tx_load(4 * y + 2)
                        tx_load(4 * y + 3)

                    for s in range(NSLICE):
                        for u in range(4):
                            k = 4 * s + u
                            if k >= 2:
                                sync.wait_ge(S["xn_done"], k - 1)
                            sync.dma_start(out=x_slab[k % 2][:, :],
                                           in_=x_seq[128 * k:128 * (k + 1), :],
                                           ).then_inc(S["x_ld"], 16)
                            sync.wait_ge(S["xn_done"], k + 1)
                            sync.dma_start(out=xn_hbm[128 * k:128 * (k + 1), :],
                                           in_=xnstg[:, DM * (k % 4):DM * (k % 4 + 1)],
                                           ).then_inc(S["xnsp"], 16)
                        if lvl >= 1 and s >= 1:
                            kqv_spill(s - 1)
                        if lvl >= 2 and s >= 1:
                            batch_sync(s - 1)
                    if lvl >= 1:
                        kqv_spill(NSLICE - 1)
                    if lvl >= 2:
                        batch_sync(NSLICE - 1)
                        attn_spill(7)
                        sync.wait_ge(S["asp"], 16 * 8)
                    sync.wait_ge(S["xnsp"], 16 * NSLAB)
                    if not no_dyn:
                        sync.wait_ge(S["xnsp2"], 16 * 8)
                    if lvl >= 1:
                        sync.wait_ge(S["ksp"], 16 * 24 * NSLICE)

            @blkA.gpsimd
            def _(gpsimd):
                n = 0
                for dst, src in ((wk_sb, wk_d), (wq_sb, wq_d), (wv_sb, wv_d)):
                    for c in range(8):
                        gpsimd.dma_start(out=dst[:, DM * c:DM * (c + 1)],
                                         in_=src[128 * c:128 * (c + 1), :]).then_inc(S["wld"], 16)
                        n += 1
                for c in range(8):
                    gpsimd.dma_start(out=thetab_sb[:, H * c:H * (c + 1)],
                                     in_=thetab_d[128 * c:128 * (c + 1), :]).then_inc(S["wld"], 16)
                    n += 1
                for dst, src in ((gpar_sb, gpar_d), (bepar_sb, bepar_d),
                                 (gtm_sb, gtm_d), (betm_sb, betm_d)):
                    gpsimd.dma_start(out=dst[:, :], in_=src[:, :]).then_inc(S["wld"], 16)
                    n += 1
                assert n == 36
                gpsimd.wait_ge(S["cld"], 16 * 6)
                if lvl >= 2:
                    gpsimd.wait_ge(S["pre"], 1)
                _casts = {1: (wgbf_hbm, wgT_d), 2: (wubf_hbm, wuT_d),
                          3: (wdbf_hbm, wdT_d), 4: (wobf_hbm, wo_d)}

                def par_copies(i):
                    m, u = i // 2, i % 2
                    pcs = S["pc%d" % (i % 4)]
                    txs = S["txp%d" % (m % 2)]
                    if i >= 2:
                        gpsimd.wait_ge(S["va"], vaN[("c", i - 2, 3)])
                    gpsimd.wait_ge(S["kvg"], m + 1)
                    for q in range(2):
                        gpsimd.dma_start(
                            out=_ap(kvg_par[i % 2], (64 * q) * 512,
                                    [[512, 64], [64, 8], [1, 64]]),
                            in_=_ap(kvg_sb[m % 2], (64 * u) * DM + 64 * q,
                                    [[DM, 64], [128, 8], [1, 64]]),
                        ).then_inc(pcs, 16)
                    gpsimd.wait_ge(txs, 16 * (3 * (m // 2) + 1))
                    for q in range(2):
                        gpsimd.dma_start(
                            out=_ap(kc_par[i % 2], (64 * q) * 512,
                                    [[512, 64], [64, 8], [1, 64]]),
                            in_=_ap(txk_sb[m % 2], (64 * u) * DM + 64 * q,
                                    [[DM, 64], [128, 8], [1, 64]]),
                        ).then_inc(pcs, 16)
                    if i >= 2:
                        gpsimd.wait_ge(S["va"], vaN[("c", i - 2, 8)])
                    gpsimd.wait_ge(txs, 16 * (3 * (m // 2) + 3))
                    for q in range(2):
                        gpsimd.dma_start(
                            out=_ap(qc_par[i % 2], (64 * q) * 512,
                                    [[512, 64], [64, 8], [1, 64]]),
                            in_=_ap(txq_sb[m % 2], (64 * u) * DM + 64 * q,
                                    [[DM, 64], [128, 8], [1, 64]]),
                        ).then_inc(pcs, 16)

                def attn_copy(i):
                    m, u = i // 2, i % 2
                    gpsimd.wait_ge(S["va"], vaN[("c", i, 8)])
                    if m >= 4 and u == 0:
                        gpsimd.wait_ge(S["asp"], 16 * (m // 4))
                    for q in range(2):
                        gpsimd.dma_start(
                            out=_ap(attn_stage, (m % 4) * DM
                                    + (64 * u) * (4 * DM) + 64 * q,
                                    [[4 * DM, 64], [128, 8], [1, 64]]),
                            in_=_ap(outp_sb[i % 2], (64 * q) * 512,
                                    [[512, 64], [64, 8], [1, 64]]),
                        ).then_inc(S["ac%d" % (i % 2)], 16)

                def kvg_one(m):
                    if m >= 2:
                        for j in (2 * m - 4, 2 * m - 3):
                            gpsimd.wait_ge(S["pc%d" % (j % 4)],
                                           16 * (6 * (j // 4) + 2))
                    gpsimd.wait_ge(S["txp%d" % (m % 2)],
                                   16 * (3 * (m // 2) + 2))
                    gpsimd.tensor_sub(kvg_sb[m % 2][:, :], txk_sb[m % 2][:, :],
                                      txv_sb[m % 2][:, :])
                    gpsimd.tensor_mul(kvg_sb[m % 2][:, :], kvg_sb[m % 2][:, :],
                                      gtm_sb[:, :])
                    gpsimd.tensor_add(kvg_sb[m % 2][:, :], kvg_sb[m % 2][:, :],
                                      gbetm_sb[:, :]).then_inc(S["kvg"], 1)

                def batch_gp(y):
                    if y >= 1:
                        attn_copy(8 * y - 2)
                        attn_copy(8 * y - 1)
                    kvg_one(4 * y)
                    kvg_one(4 * y + 1)
                    for i in range(8 * y, 8 * y + 8):
                        if i == 8 * y + 4:
                            kvg_one(4 * y + 2)
                        if i == 8 * y + 6:
                            kvg_one(4 * y + 3)
                        par_copies(i)
                        if i - 2 >= 8 * y:
                            attn_copy(i - 2)

                for s in range(NSLICE):
                    if lvl >= 3 and s in _casts:
                        dst, src = _casts[s]
                        gpsimd.dma_start(out=dst[:, :],
                                         in_=src[:, :]).then_inc(S["wdc"], 16)
                    if lvl >= 2 and s >= 1:
                        batch_gp(s - 1)
                    if not no_dyn:
                        gpsimd.wait_ge(S["xn_done"], 4 * s + 4)
                        v = gpsimd.value_load(dynr_sb[s:s + 1, 0:1])
                        src = _ap(xnstg, 0, [[4 * DM, 128], [DM, 4], [1, DM]])
                        gpsimd.dma_start(
                            out=xn2_hbm[ds(v, 512), :].rearrange("(u t) d -> t u d", t=128),
                            in_=src).then_inc(S["xnsp2"], 16)
                if lvl >= 2:
                    batch_gp(NSLICE - 1)
                    attn_copy(NCH - 2)
                    attn_copy(NCH - 1)

            WLD_ALL = 16 * 36

            def batch_mms_impl(tensor, y):
                for i in range(8 * y, 8 * y + 8):
                    m, u = i // 2, i % 2
                    sl = m // 4
                    off = (i * 64) % 512
                    tensor.wait_ge(S["sa"], saN[("c0",)] if i == 0
                                   else saN[("s", i - 1, 4)])
                    if i >= 2:
                        tensor.wait_ge(S["va"], vaN[("c", i - 2, 1)])
                    for p in range(8):
                        for q in range(2):
                            mm = tensor.matmul(
                                pp_z1[i % 2][64 * q:64 * (q + 1), 64 * p:64 * (p + 1)],
                                kfm_sb[sl % 2][64 * q:64 * (q + 1),
                                               512 * p + off:512 * p + off + 64],
                                wbf[64 * q:64 * (q + 1), 64 * p:64 * (p + 1)],
                                start=True, stop=True, tile_position=(64 * q, 64 * q))
                            if p == 7 and q == 1:
                                mm.then_inc(S["z1mm"], 1)
                    tensor.wait_ge(S["va"], vaN[("c", i, 3)])
                    if i >= 1:
                        tensor.wait_ge(S["va"], vaN[("c", i - 1, 5)])
                    for p in range(8):
                        for q in range(2):
                            mm = tensor.matmul(
                                pp_dw[64 * q:64 * (q + 1), 64 * p:64 * (p + 1)],
                                sck_sb[64 * q:64 * (q + 1), 64 * p:64 * (p + 1)],
                                grad_sb[64 * q:64 * (q + 1), 64 * p:64 * (p + 1)],
                                start=True, stop=True, tile_position=(64 * q, 64 * q))
                            if p == 7 and q == 1:
                                mm.then_inc(S["dwmm"], 1)
                    for p in range(8):
                        for q in range(2):
                            lhs = _ap(c_eta2b, q * 64 * 8 + p, [[8, 64], [0, 64]])
                            mm = tensor.matmul(
                                pp_db[64 * q:64 * (q + 1), 64 * p:64 * (p + 1)],
                                lhs,
                                grad_sb[64 * q:64 * (q + 1), 64 * p:64 * (p + 1)],
                                start=True, stop=True, tile_position=(64 * q, 64 * q))
                            if p == 7 and q == 1:
                                mm.then_inc(S["dwmm"], 1)
                    tensor.wait_ge(S["sa"], saN[("s", i, 4)])
                    if i >= 1:
                        tensor.wait_ge(S["va"], vaN[("c", i - 1, 6)])
                    for p in range(8):
                        for q in range(2):
                            mm = tensor.matmul(
                                pp_zq[64 * q:64 * (q + 1), 64 * p:64 * (p + 1)],
                                qfm_sb[sl % 2][64 * q:64 * (q + 1),
                                               512 * p + off:512 * p + off + 64],
                                wbf[64 * q:64 * (q + 1), 64 * p:64 * (p + 1)],
                                start=True, stop=True, tile_position=(64 * q, 64 * q))
                            if p == 7 and q == 1:
                                mm.then_inc(S["zqmm"], 1)

            @blkA.tensor
            def _(tensor):
                def batch_mms(y):
                    tensor.wait_ge(S["evac"], 24 * (y + 1))
                    batch_mms_impl(tensor, y)

                tensor.wait_ge(S["wld"], WLD_ALL)
                for s in range(NSLICE if lvl >= 1 else 0):
                    tensor.wait_ge(S["xnf_ld"], 16 * 8 * (s + 1))
                    for t in range(3):
                        for p in range(8):
                            g = 24 * s + t * 8 + p
                            if g >= 2:
                                tensor.wait_ge(S["evac"], g - 1)
                            for c in range(8):
                                mm = tensor.matmul(
                                    pp_proj[g % 2][:, :],
                                    wsb(t)[:, DM * c + 128 * p: DM * c + 128 * (p + 1)],
                                    xnf_sb[s % 2][:, 512 * c:512 * (c + 1)],
                                    start=(c == 0), stop=(c == 7))
                                if c == 7:
                                    mm.then_inc(S["pmm"], 1)
                    tensor.wait_ge(S["evac"], 24 * s + 8)
                    for u in range(4 if not no_lr else 0):
                        k = 4 * s + u
                        if k >= 1:
                            tensor.wait_ge(S["va"], vaN[("l", k - 1)])
                        for p in range(8):
                            mm = tensor.matmul(
                                pp_lr[:, :],
                                kfm_sb[s % 2][:, 512 * p + 128 * u: 512 * p + 128 * (u + 1)],
                                thetab_sb[:, H * p:H * (p + 1)],
                                start=(p == 0), stop=(p == 7))
                            if p == 7:
                                mm.then_inc(S["pmm"], 1)
                    if lvl >= 2 and s >= 1:
                        batch_mms(s - 1)
                if lvl >= 2:
                    batch_mms(NSLICE - 1)

            @blkA.scalar
            def _(scalar):
                def batch_sc(y):
                    for i in range(8 * y, 8 * y + 8):
                        m, u = i // 2, i % 2
                        scalar.wait_ge(S["va"], vaN[("e", m)])
                        if i >= 2:
                            scalar.wait_ge(S["va"], vaN[("c", i - 2, 3)])
                        last = None
                        for q in range(2):
                            src = _ap(eta_sb, (64 * u) * (NSLAB * H) + H * m + q,
                                      [[NSLAB * H, 64], [2, 8]])
                            dst_ap = _ap(eta_par[i % 2], (64 * q) * 8, [[8, 64], [1, 8]])
                            last = scalar.copy(dst_ap, src)
                        last.then_inc(S["sa"], 1)
                        scalar.wait_ge(S["va"], vaN[("c", i, 1)])
                        scalar.activation(sq_sb[:, :], z1b_sb[:, :],
                                          AF.Square).then_inc(S["sa"], 1)
                        scalar.wait_ge(S["va"], vaN[("c", i, 2)])
                        scalar.activation(c_std[:, :], c_var[:, :],
                                          AF.Sqrt).then_inc(S["sa"], 1)
                        scalar.wait_ge(S["va"], vaN[("c", i, 4)])
                        scalar.copy(wbf[:, :], w_par[:, :]).then_inc(S["sa"], 1)
                        scalar.wait_ge(S["va"], vaN[("c", i, 6)])
                        scalar.activation(sq_sb[:, :], z1b_sb[:, :],
                                          AF.Square).then_inc(S["sa"], 1)
                        scalar.wait_ge(S["va"], vaN[("c", i, 7)])
                        scalar.activation(c_std[:, :], c_var[:, :],
                                          AF.Sqrt).then_inc(S["sa"], 1)

                if lvl >= 2:
                    scalar.wait_ge(S["cld"], 16 * 6)
                    scalar.copy(wbf[:, :], w_par[:, :]).then_inc(S["sa"], 1)
                for s in range(NSLICE):
                    for u in range(4):
                        k = 4 * s + u
                        scalar.wait_ge(S["x_ld"], 16 * (k + 1))
                        scalar.activation(sqx_sb[:, :], x_slab[k % 2][:, :],
                                          AF.Square).then_inc(S["sa"], 1)
                        scalar.wait_ge(S["va"], vaN[("r", k)])
                        scalar.activation(st_r[:, :], st_t1[:, :],
                                          AF.Sqrt).then_inc(S["sa"], 1)
                    if lvl >= 1:
                        scalar.wait_ge(S["xnsp"], 16 * 4 * (s + 1))
                        if s >= 2:
                            scalar.wait_ge(S["pmm"], 28 * (s - 1))
                        for c in range(8):
                            scalar.dma_start(
                                out=xnf_sb[s % 2][:, 512 * c:512 * (c + 1)],
                                in_=xn_hbm[512 * s:512 * (s + 1), 128 * c:128 * (c + 1)],
                                transpose=True).then_inc(S["xnf_ld"], 16)
                        if not no_lr:
                            for u in range(4):
                                k = 4 * s + u
                                scalar.wait_ge(S["va"], vaN[("l", k)])
                                scalar.activation(sigall_sb[:, H * k:H * (k + 1)],
                                                  lrall_sb[:, H * k:H * (k + 1)],
                                                  AF.Sigmoid).then_inc(S["sa"], 1)
                    if lvl >= 2 and s >= 1:
                        batch_sc(s - 1)
                if lvl >= 2:
                    batch_sc(NSLICE - 1)

            def batch_vec_impl(vector, y):
                for i in range(8 * y, 8 * y + 8):
                    m, u = i // 2, i % 2
                    vector.wait_ge(S["z1mm"], i + 1)
                    vector.tensor_add(z1b_sb[:, :], pp_z1[i % 2][:, :],
                                      b_mat[:, :]).then_inc(S["va"], 1)
                    vector.reduce_sum(c_s1[:, :], _r3(z1b_sb), axis=AXX)
                    vector.wait_ge(S["sa"], saN[("s", i, 2)])
                    vector.reduce_sum(c_s2[:, :], _r3(sq_sb), axis=AXX)
                    vector.tensor_scalar_mul(c_m[:, :], c_s1[:, :], -1.0 / 64.0)
                    vector.tensor_scalar(c_v0[:, :], c_s2[:, :], 64.0 * EPS_SCAN,
                                         1.0 / 64.0, ALU.add, ALU.mult)
                    vector.tensor_mul(c_t[:, :], c_m[:, :], c_m[:, :])
                    vector.tensor_scalar_mul(c_d[:, :], c_s1[:, :], 1.0)
                    vector.tensor_sub(c_var[:, :], c_v0[:, :],
                                      c_t[:, :]).then_inc(S["va"], 1)
                    vector.wait_ge(S["sa"], saN[("s", i, 3)])
                    vector.reciprocal(c_rstd[:, :], c_std[:, :])
                    z3 = _r3(z1b_sb)
                    vector.tensor_tensor(z3, z3, _grp(c_m), ALU.add)
                    vector.tensor_tensor(_r3(xh_sb), z3, _grp(c_rstd), ALU.mult)
                    vector.wait_ge(S["sa"], saN[("s", i, 1)])
                    vector.wait_ge(S["pc%d" % (i % 4)], 16 * (6 * (i // 4) + 4))
                    vector.tensor_mul(gxh_sb[:, :], xh_sb[:, :], g2par_sb[:, :])
                    vector.tensor_add(gxh_sb[:, :], gxh_sb[:, :], kvg_par[i % 2][:, :])
                    vector.reduce_sum(c_r1[:, :], _r3(gxh_sb), axis=AXX)
                    vector.tensor_mul(gx2_sb[:, :], gxh_sb[:, :], xh_sb[:, :])
                    vector.reduce_sum(c_r2[:, :], _r3(gx2_sb), axis=AXX)
                    vector.tensor_scalar_mul(c_a1[:, :], c_r1[:, :], 1.0 / 64.0)
                    vector.tensor_scalar_mul(c_a2[:, :], c_r2[:, :], 1.0 / 64.0)
                    vector.tensor_mul(c_eta2[:, :], eta_par[i % 2][:, :], c_rstd[:, :])
                    vector.tensor_tensor(_r3(u_sb), _r3(xh_sb), _grp(c_a2), ALU.mult)
                    vector.tensor_copy(c_eta2b[:, :], c_eta2[:, :])
                    vector.tensor_sub(grad_sb[:, :], gxh_sb[:, :], u_sb[:, :])
                    vector.tensor_tensor(_r3(grad_sb), _r3(grad_sb), _grp(c_a1), ALU.subtract)
                    vector.tensor_tensor(_r3(sck_sb), _r3(kc_par[i % 2]),
                                         _ap(c_eta2b, 0, [[8, 128], [1, 8], [0, 64]]),
                                         ALU.mult).then_inc(S["va"], 1)
                    vector.wait_ge(S["dwmm"], 2 * i + 2)
                    vector.tensor_sub(w_par[:, :], w_par[:, :],
                                      pp_dw[:, :]).then_inc(S["va"], 1)
                    vector.tensor_sub(b_mat[:, :], b_mat[:, :],
                                      pp_db[:, :]).then_inc(S["va"], 1)
                    vector.wait_ge(S["zqmm"], i + 1)
                    vector.tensor_add(z1b_sb[:, :], pp_zq[:, :],
                                      b_mat[:, :]).then_inc(S["va"], 1)
                    vector.reduce_sum(c_s1[:, :], _r3(z1b_sb), axis=AXX)
                    vector.wait_ge(S["sa"], saN[("s", i, 5)])
                    vector.reduce_sum(c_s2[:, :], _r3(sq_sb), axis=AXX)
                    vector.tensor_scalar_mul(c_m[:, :], c_s1[:, :], -1.0 / 64.0)
                    vector.tensor_scalar(c_v0[:, :], c_s2[:, :], 64.0 * EPS_SCAN,
                                         1.0 / 64.0, ALU.add, ALU.mult)
                    vector.tensor_mul(c_t[:, :], c_m[:, :], c_m[:, :])
                    vector.tensor_scalar_mul(c_d[:, :], c_s1[:, :], 1.0)
                    vector.tensor_sub(c_var[:, :], c_v0[:, :],
                                      c_t[:, :]).then_inc(S["va"], 1)
                    vector.wait_ge(S["sa"], saN[("s", i, 6)])
                    vector.reciprocal(c_rstd[:, :], c_std[:, :])
                    z3b = _r3(z1b_sb)
                    vector.tensor_tensor(z3b, z3b, _grp(c_m), ALU.add)
                    vector.tensor_tensor(_r3(w1_sb), z3b, _grp(c_rstd), ALU.mult)
                    vector.tensor_mul(w1_sb[:, :], w1_sb[:, :], gpar_sb[:, :])
                    vector.tensor_add(w1_sb[:, :], w1_sb[:, :], bepar_sb[:, :])
                    vector.wait_ge(S["pc%d" % (i % 4)], 16 * (6 * (i // 4) + 6))
                    if i >= 2:
                        vector.wait_ge(S["ac%d" % (i % 2)], 16 * 2 * (i // 2))
                    vector.tensor_add(outp_sb[i % 2][:, :], w1_sb[:, :],
                                      qc_par[i % 2][:, :]).then_inc(S["va"], 1)

            @blkA.vector
            def _(vector):
                def batch_vec(y):
                    batch_vec_impl(vector, y)

                vector.wait_ge(S["wld"], WLD_ALL)
                vector.wait_ge(S["cld"], 16 * 6)
                vector.tensor_add(etab_sb[:, :], invk_sb[:, :], alpha2_sb[:, :])
                vector.tensor_mul(g2par_sb[:, :], gpar_sb[:, :], gpar_sb[:, :])
                vector.tensor_mul(gbetm_sb[:, :], gtm_sb[:, :],
                                  betm_sb[:, :]).then_inc(S["pre"], 1)
                vector.tensor_relu(etab_sb[:, :], etab_sb[:, :])
                for s in range(NSLICE):
                    for u in range(4):
                        k = 4 * s + u
                        vector.wait_ge(S["x_ld"], 16 * (k + 1))
                        vector.reduce_sum(st_s1[:, :], x_slab[k % 2][:, :], axis=AXX)
                        vector.wait_ge(S["sa"], saN[("q", k)])
                        vector.reduce_sum(st_s2[:, :], sqx_sb[:, :], axis=AXX)
                        vector.tensor_scalar_mul(st_m[:, :], st_s1[:, :], -1.0 / DM)
                        vector.tensor_scalar(st_v[:, :], st_s2[:, :], DM * EPS_MAIN,
                                             1.0 / DM, ALU.add, ALU.mult)
                        vector.tensor_mul(st_t2[:, :], st_m[:, :], st_m[:, :])
                        vector.tensor_scalar_mul(st_d[:, :], st_s1[:, :], 1.0)
                        vector.tensor_sub(st_v[:, :], st_v[:, :], st_t2[:, :])
                        vector.tensor_scalar_mul(st_d[:, :], st_s1[:, :], 1.0)
                        vector.reciprocal(st_t1[:, :], st_v[:, :]).then_inc(S["va"], 1)
                        vector.wait_ge(S["sa"], saN[("t", k)])
                        if k >= 4:
                            vector.wait_ge(S["xnsp"], 16 * (k - 3))
                            if not no_dyn:
                                vector.wait_ge(S["xnsp2"], 16 * (k // 4))
                        vector.tensor_scalar(xnstg[:, DM * (k % 4):DM * (k % 4 + 1)],
                                             x_slab[k % 2][:, :],
                                             st_m[:, :], st_r[:, :],
                                             ALU.add, ALU.mult).then_inc(S["xn_done"], 1)
                    if lvl >= 2 and s >= 2:
                        vector.wait_ge(S["zqmm"], 8 * (s - 1))
                    for t in range(3 if lvl >= 1 else 0):
                        for p in range(8):
                            g = 24 * s + t * 8 + p
                            j = t * 8 + p
                            vector.wait_ge(S["pmm"], PMM(s, j))
                            if s >= 2:
                                vector.wait_ge(S["ksp"], 16 * 24 * (s - 1))
                            if t == 2 and s >= 1:
                                vector.wait_ge(S["ksp"], 16 * 24 * s)
                            vector.tensor_copy(fmsb(t)[s % 2][:, 512 * p:512 * (p + 1)],
                                               pp_proj[g % 2][:, :]).then_inc(S["evac"], 1)
                    for u in range(4 if (lvl >= 1 and not no_lr) else 0):
                        k = 4 * s + u
                        vector.wait_ge(S["pmm"], PMM_LR(s, u))
                        vector.tensor_add(lrall_sb[:, H * k:H * (k + 1)], pp_lr[:, :],
                                          tbmat_sb[:, :]).then_inc(S["va"], 1)
                    for u in range(4 if (lvl >= 1 and not no_lr) else 0):
                        k = 4 * s + u
                        vector.wait_ge(S["sa"], saN[("g", k)])
                        vector.tensor_scalar(
                            eta_sb[:, H * k:H * (k + 1)],
                            sigall_sb[:, H * k:H * (k + 1)], etab_sb[:, :], 1.0 / 64.0,
                            ALU.mult, ALU.mult).then_inc(S["va"], 1)
                    if lvl >= 2 and s >= 1:
                        batch_vec(s - 1)
                if lvl >= 2:
                    batch_vec(NSLICE - 1)

        if lvl < 3:
            return nc
        with ExitStack() as pb1:
            b1 = pb1.enter_context
            wg_sb = b1(nc.sbuf_tensor("wg_sb", [128, 8 * HIDDEN], BF16))
            wu_sb = b1(nc.sbuf_tensor("wu_sb", [128, 8 * HIDDEN], BF16))
            xnfB = [b1(nc.sbuf_tensor(f"xnfB{i}", [128, 8 * 512], BF16)) for i in range(2)]
            h_sb = b1(nc.sbuf_tensor("h_sb", [128, 24 * 512], BF16))
            sg_sb = b1(nc.sbuf_tensor("sg_sb", [128, 512], F32))
            pp_gate = [b1(nc.psum_tensor(f"pp_gate{i}", [128, 512], F32)) for i in range(2)]
            pp_up = [b1(nc.psum_tensor(f"pp_up{i}", [128, 512], F32)) for i in range(2)]

            blkB1 = pb1.enter_context(nc.Block())

            @blkB1.sync
            def _(sync):
                sync.sem_inc(S["phb"], 1)
                for j in range(4):
                    for oc in range(24):
                        sync.wait_ge(S["hready"], 24 * j + oc + 1)
                        sync.dma_start(
                            out=_ap(h_hbm, (4 * j) * 128 * HIDDEN + 128 * oc,
                                    [[HIDDEN, 128], [128 * HIDDEN, 4], [1, 128]]),
                            in_=h_sb[:, 512 * oc:512 * (oc + 1)],
                        ).then_inc(S["hsp"], 16)
                if lvl == 3:
                    sync.wait_ge(S["hsp"], 16 * 24 * 4)

            @blkB1.gpsimd
            def _(gpsimd):
                gpsimd.wait_ge(S["phb"], 1)
                gpsimd.wait_ge(S["wdc"], 16)
                for c in range(8):
                    gpsimd.dma_start(out=wg_sb[:, HIDDEN * c:HIDDEN * (c + 1)],
                                     in_=wgbf_hbm[128 * c:128 * (c + 1), :],
                                     ).then_inc(S["fwl"], 16)
                gpsimd.wait_ge(S["wdc"], 32)
                for c in range(8):
                    gpsimd.dma_start(out=wu_sb[:, HIDDEN * c:HIDDEN * (c + 1)],
                                     in_=wubf_hbm[128 * c:128 * (c + 1), :],
                                     ).then_inc(S["fwl"], 16)

            @blkB1.scalar
            def _(scalar):
                scalar.wait_ge(S["phb"], 1)
                for j in range(2):
                    for c in range(8):
                        scalar.dma_start(
                            out=xnfB[j % 2][:, 512 * c:512 * (c + 1)],
                            in_=xn2_hbm[L + 512 * j:L + 512 * (j + 1),
                                        128 * c:128 * (c + 1)],
                            transpose=True).then_inc(S["bxn"], 16)
                for j in range(4):
                    for oc in range(24):
                        g = 24 * j + oc
                        scalar.wait_ge(S["gumm"], 2 * g + 1)
                        if g >= 1:
                            scalar.wait_ge(S["hready"], g)
                        scalar.activation(sg_sb[:, :], pp_gate[g % 2][:, :],
                                          AF.Silu).then_inc(S["silu"], 1)
                    jj = j + 2
                    if jj <= 3:
                        scalar.wait_ge(S["gumm"], 48 * (jj - 1))
                        for c in range(8):
                            scalar.dma_start(
                                out=xnfB[jj % 2][:, 512 * c:512 * (c + 1)],
                                in_=xn2_hbm[L + 512 * jj:L + 512 * (jj + 1),
                                            128 * c:128 * (c + 1)],
                                transpose=True).then_inc(S["bxn"], 16)

            @blkB1.tensor
            def _(tensor):
                tensor.wait_ge(S["fwl"], 16 * 8)
                for j in range(4):
                    tensor.wait_ge(S["bxs"], j + 1)
                    for oc in range(24):
                        g = 24 * j + oc
                        if g >= 2:
                            tensor.wait_ge(S["silu"], g - 1)
                            tensor.wait_ge(S["hready"], g - 1)
                        for c in range(8):
                            mm = tensor.matmul(
                                pp_gate[g % 2][:, :],
                                wg_sb[:, HIDDEN * c + 128 * oc: HIDDEN * c + 128 * (oc + 1)],
                                xnfB[j % 2][:, 512 * c:512 * (c + 1)],
                                start=(c == 0), stop=(c == 7))
                            if c == 7:
                                mm.then_inc(S["gumm"], 1)
                        if g == 0:
                            tensor.wait_ge(S["fws"], 1)
                        for c in range(8):
                            mm = tensor.matmul(
                                pp_up[g % 2][:, :],
                                wu_sb[:, HIDDEN * c + 128 * oc: HIDDEN * c + 128 * (oc + 1)],
                                xnfB[j % 2][:, 512 * c:512 * (c + 1)],
                                start=(c == 0), stop=(c == 7))
                            if c == 7:
                                mm.then_inc(S["gumm"], 1)

            @blkB1.vector
            def _(vector):
                vector.wait_ge(S["phb"], 1)
                for j in range(4):
                    vector.wait_ge(S["bxn"], 16 * 8 * (j + 1))
                    vector.tensor_scalar_mul(xnfB[j % 2][:, :], xnfB[j % 2][:, :],
                                             1.0 / 127.0).then_inc(S["bxs"], 1)
                    if j == 0:
                        vector.wait_ge(S["fwl"], 16 * 16)
                        vector.tensor_scalar_mul(wu_sb[:, :], wu_sb[:, :],
                                                 1.0 / 127.0).then_inc(S["fws"], 1)
                    vector.wait_ge(S["hsp"], 16 * 24 * j)
                    for oc in range(24):
                        g = 24 * j + oc
                        vector.wait_ge(S["silu"], g + 1)
                        vector.wait_ge(S["gumm"], 2 * g + 2)
                        vector.tensor_mul(h_sb[:, 512 * oc:512 * (oc + 1)],
                                          sg_sb[:, :],
                                          pp_up[g % 2][:, :]).then_inc(S["hready"], 1)

        if lvl < 4:
            return nc
        with ExitStack() as pb2:
            b2 = pb2.enter_context
            wd_sb = b2(nc.sbuf_tensor("wd_sb", [128, 24 * DM], BF16))
            wo_sb = b2(nc.sbuf_tensor("wo_sb", [128, 8 * DM], BF16))
            afm_sb = b2(nc.sbuf_tensor("afm_sb", [128, 8 * TOUT], BF16))
            hsub_sb = [b2(nc.sbuf_tensor(f"hsub{i}", [128, 24 * 128], BF16)) for i in range(2)]
            xres_sb = [b2(nc.sbuf_tensor(f"xres{i}", [128, 4 * DM], F32)) for i in range(2)]
            out_sb = [b2(nc.sbuf_tensor(f"outsb{i}", [128, DM], F32)) for i in range(2)]
            pp_out = [b2(nc.psum_tensor(f"pp_out{i}", [128, DM], F32)) for i in range(3)]

            blkB2 = pb2.enter_context(nc.Block())

            B2_GATE = (16 * 24 * 4)

            @blkB2.sync
            def _(sync):
                if True:
                    for r in range(4):
                        sync.wait_ge(S["gumm"], 192)
                        sync.wait_ge(S["asp"], 16 * 8)
                        sync.dma_start(
                            out=afm_sb[:, TOUT * (4 + r):TOUT * (5 + r)],
                            in_=attn2_hbm[L:L + TOUT, 128 * (4 + r):128 * (5 + r)],
                            transpose=True).then_inc(S["afm_ld"], 16)
                    for r in range(16):
                        sync.wait_ge(S["hsp"], 16 * 24 * (r // 4 + 1))
                        if r >= 2:
                            sync.wait_ge(S["omm"], 2 * (r - 1))
                        sync.dma_start(
                            out=hsub_sb[r % 2][:, :],
                            in_=h_hbm[128 * r:128 * (r + 1), :],
                        ).then_inc(S["hld"], 16)
                        if r % 4 == 0:
                            g = r // 4
                            if g == 0:
                                sync.wait_ge(S["hready"], 96)
                                sync.wait_ge(S["hsp"], 16 * 24 * 4)
                            if g >= 2:
                                sync.wait_ge(S["oev"], 4 * (g - 1))
                            v = sync.value_load(dynr_sb[8 + g:8 + g + 1, 0:1])
                            sync.dma_start(
                                out=xres_sb[g % 2][:, :].rearrange(
                                    "t (u d) -> t u d", u=4),
                                in_=x_seq[ds(v, 512), :].rearrange(
                                    "(u t) d -> t u d", t=128)).then_inc(S["xres_ld"], 16)
                        sync.wait_ge(S["oev"], r + 1)
                        sync.dma_start(out=out_d[128 * r:128 * (r + 1), :],
                                       in_=out_sb[r % 2][:, :]).then_inc(S["ost"], 16)
                    sync.wait_ge(S["ost"], 16 * 16)

            @blkB2.gpsimd
            def _(gpsimd):
                gpsimd.wait_ge(S["gumm"], 192)
                gpsimd.wait_ge(S["wdc"], 16 * 4)
                for c in range(24):
                    gpsimd.dma_start(out=wd_sb[:, DM * c:DM * (c + 1)],
                                     in_=wdbf_hbm[128 * c:128 * (c + 1), :],
                                     ).then_inc(S["fwl"], 16)
                for c in range(8):
                    gpsimd.dma_start(out=wo_sb[:, DM * c:DM * (c + 1)],
                                     in_=wobf_hbm[128 * c:128 * (c + 1), :],
                                     ).then_inc(S["fwl"], 16)

            @blkB2.scalar
            def _(scalar):
                scalar.wait_ge(S["gumm"], 192)
                scalar.wait_ge(S["asp"], 16 * 8)
                for c in range(4):
                    scalar.dma_start(
                        out=afm_sb[:, TOUT * c:TOUT * (c + 1)],
                        in_=attn2_hbm[L:L + TOUT, 128 * c:128 * (c + 1)],
                        transpose=True).then_inc(S["afm_ld"], 16)

            @blkB2.tensor
            def _(tensor):
                tensor.wait_ge(S["fwl"], 16 * 48)
                for r in range(16):
                    tensor.wait_ge(S["hld"], 16 * (r + 1))
                    if r >= 3:
                        tensor.wait_ge(S["oev"], r - 2)
                    for half in range(2):
                        for oc in range(24):
                            tensor.matmul(
                                pp_out[r % 3][:, 512 * half:512 * (half + 1)],
                                hsub_sb[r % 2][:, 128 * oc:128 * (oc + 1)],
                                wd_sb[:, DM * oc + 512 * half: DM * oc + 512 * (half + 1)],
                                start=(oc == 0), stop=False)
                        if r == 0 and half == 0:
                            tensor.wait_ge(S["afm_ld"], 16 * 8)
                        for c in range(8):
                            mm = tensor.matmul(
                                pp_out[r % 3][:, 512 * half:512 * (half + 1)],
                                afm_sb[:, TOUT * c + 128 * r: TOUT * c + 128 * (r + 1)],
                                wo_sb[:, DM * c + 512 * half: DM * c + 512 * (half + 1)],
                                start=False, stop=(c == 7))
                            if c == 7:
                                mm.then_inc(S["omm"], 1)

            @blkB2.vector
            def _(vector):
                for r in range(16):
                    vector.wait_ge(S["omm"], 2 * (r + 1))
                    vector.wait_ge(S["xres_ld"], 16 * (r // 4 + 1))
                    if r >= 2:
                        vector.wait_ge(S["ost"], 16 * (r - 1))
                    vector.tensor_add(out_sb[r % 2][:, :], pp_out[r % 3][:, :],
                                      xres_sb[(r // 4) % 2][:, DM * (r % 4):DM * (r % 4 + 1)],
                                      ).then_inc(S["oev"], 1)
    return nc


_CACHE = {}


def _stage(inputs):
    x = np.ascontiguousarray(np.asarray(inputs["L0_attn_hidden_states"], np.float32))
    theta = np.asarray(inputs["L0_attn_theta"], np.float32)
    thetab = np.zeros((DM, H), np.float32)
    for h in range(H):
        thetab[64 * h:64 * h + 64, h] = theta[h, :, 0]
    tbmat = np.broadcast_to(
        np.asarray(inputs["L0_attn_theta_bias"], np.float32).reshape(1, H), (128, H))
    invk = np.tile(1.0 / np.arange(1, 65, dtype=np.float32), 2)[:, None]
    alpha2 = np.tile(np.asarray(inputs["L0_attn_alpha"], np.float32), 2)[:, None]
    gamma = np.asarray(inputs["L0_attn_gamma"], np.float32)[:, 0, :]
    beta = np.asarray(inputs["L0_attn_beta"], np.float32)[:, 0, :]
    W1 = np.asarray(inputs["L0_attn_W1"], np.float32)
    b1 = np.asarray(inputs["L0_attn_b1"], np.float32)[:, 0, :]

    def to_par(v):
        out = np.zeros((128, 512), np.float32)
        for p in range(8):
            for q in range(2):
                out[q * 64:q * 64 + 64, p * 64:p * 64 + 64] = v[2 * p + q][None, :]
        return out
    gpar, bepar, b0par = to_par(gamma), to_par(beta), to_par(b1)
    w0par = np.zeros((128, 512), np.float32)
    for p in range(8):
        for q in range(2):
            w0par[q * 64:q * 64 + 64, p * 64:p * 64 + 64] = W1[2 * p + q]
    gtm = np.zeros((1, DM), np.float32)
    betm = np.zeros((1, DM), np.float32)
    for j in range(H):
        gtm[0, 64 * j:64 * j + 64] = gamma[j]
        betm[0, 64 * j:64 * j + 64] = beta[j]
    shared = dict(
        wq=np.ascontiguousarray(np.asarray(inputs["L0_attn_wq"], np.float32)),
        wk=np.ascontiguousarray(np.asarray(inputs["L0_attn_wk"], np.float32)),
        wv=np.ascontiguousarray(np.asarray(inputs["L0_attn_wv"], np.float32)),
        wo=np.ascontiguousarray(np.asarray(inputs["L0_attn_wo"], np.float32)),
        thetab=thetab,
        tbmat=np.ascontiguousarray(tbmat).astype(np.float32),
        invk=invk, alpha2=np.ascontiguousarray(alpha2),
        gpar=gpar, bepar=bepar,
        gtm=np.ascontiguousarray(np.broadcast_to(gtm, (128, DM))),
        betm=np.ascontiguousarray(np.broadcast_to(betm, (128, DM))),
        w0par=w0par, b0par=b0par,
        wgT=np.ascontiguousarray(np.asarray(inputs["L0_ffn_w_gate"], np.int32).T),
        wuT=np.ascontiguousarray(np.asarray(inputs["L0_ffn_w_up"], np.int32).T),
        wdT=np.ascontiguousarray(np.asarray(inputs["L0_ffn_w_down"], np.int32).T),
    )
    in_maps = []
    for core in range(8):
        b, half = core // 2, core % 2
        row_off = half * TOUT
        dynr_np = np.zeros((NSLAB + 16, 1), np.uint32)
        for g in range(8):
            dynr_np[g, 0] = (L - row_off) + 512 * g
        for g in range(4):
            dynr_np[8 + g, 0] = row_off + 512 * g
        m = dict(shared)
        m["x_seq"] = np.ascontiguousarray(x[b])
        m["dynr"] = dynr_np
        in_maps.append(m)
    return in_maps


def kernel(**inputs):
    if "nc" not in _CACHE:
        _CACHE["nc"] = build_nc()
    nc = _CACHE["nc"]
    in_maps = _stage(inputs)
    res = run_bass_kernel_spmd(nc, in_maps, core_ids=list(range(8)))
    out = np.zeros((4, L, DM), np.float32)
    for core in range(8):
        b, half = core // 2, core % 2
        out[b, half * TOUT:(half + 1) * TOUT] = res.results[core]["out"]
    return out


# version 12
# speedup vs baseline: 1.1859x; 1.0526x over previous
import numpy as np
import ml_dtypes
from contextlib import ExitStack

import concourse.bass as bass
import concourse.mybir as mybir
from concourse.bass_utils import run_bass_kernel_spmd

F32 = mybir.dt.float32
BF16 = mybir.dt.bfloat16
I32 = mybir.dt.int32
U32 = mybir.dt.uint32
AF = mybir.ActivationFunctionType
ALU = mybir.AluOpType
AXX = mybir.AxisListType.X
ds = bass.ds
AP = bass.AP

L, DM, H, HD = 4096, 1024, 16, 64
NSLAB, NSLICE, NCH = 32, 8, 64
HIDDEN = 3072
TOUT = 2048
EPS_MAIN, EPS_SCAN = 1e-5, 1e-6


def _ap(t, offset, dims):
    tensor = t.tensor if hasattr(t, "tensor") else t
    return AP(tensor, offset, [list(d) for d in dims])


def _grp(t):
    return _ap(t, 0, [[8, 128], [1, 8], [0, 64]])


def _r3(t):
    return t[:, :].rearrange("p (g d) -> p g d", g=8)


def _build_tables(lvl, no_lr):
    vaN, saN = {}, {}
    va, sa = [0], [0]

    def V(*tag):
        va[0] += 1
        vaN[tag] = va[0]

    def A(*tag):
        sa[0] += 1
        saN[tag] = sa[0]

    if lvl >= 2:
        A("c0")
    for s in range(NSLICE):
        for u in range(4):
            k = 4 * s + u
            A("q", k)
            A("t", k)
        if lvl >= 1 and not no_lr:
            for u in range(4):
                A("g", 4 * s + u)
        if lvl >= 2 and s >= 1:
            for i in range(8 * (s - 1), 8 * s):
                for n in range(1, 7):
                    A("s", i, n)
    if lvl >= 2:
        for i in range(56, 64):
            for n in range(1, 7):
                A("s", i, n)
    for s in range(NSLICE):
        for u in range(4):
            V("r", 4 * s + u)
        if lvl >= 1 and not no_lr:
            for u in range(4):
                V("l", 4 * s + u)
            for u in range(4):
                V("e", 4 * s + u)
        if lvl >= 2 and s >= 1:
            for i in range(8 * (s - 1), 8 * s):
                for n in range(1, 9):
                    V("c", i, n)
    if lvl >= 2:
        for i in range(56, 64):
            for n in range(1, 9):
                V("c", i, n)
    return vaN, saN


def PMM(s, j):
    return 28 * s + j + 1


def PMM_LR(s, u):
    return 28 * s + 24 + u + 1


PH = {"ln": 0, "a1": 1, "a2": 2, "b1": 3, "full": 4}


def build_nc(upto="full", dbg=False, no_lr=False, no_dyn=False):
    lvl = PH[upto]
    vaN, saN = _build_tables(lvl, no_lr)
    nc = bass.Bass("TRN2", target_bir_lowering=False, detect_race_conditions=False)
    scratch_kind = "ExternalOutput" if dbg else "Internal"

    x_seq = nc.dram_tensor("x_seq", [L, DM], F32, kind="ExternalInput")
    dynr = nc.dram_tensor("dynr", [NSLAB + 16, 1], U32, kind="ExternalInput")
    wq_d = nc.dram_tensor("wq", [DM, DM], F32, kind="ExternalInput")
    wk_d = nc.dram_tensor("wk", [DM, DM], F32, kind="ExternalInput")
    wv_d = nc.dram_tensor("wv", [DM, DM], F32, kind="ExternalInput")
    wo_d = nc.dram_tensor("wo", [DM, DM], F32, kind="ExternalInput")
    thetab_d = nc.dram_tensor("thetab", [DM, H], F32, kind="ExternalInput")
    tbmat_d = nc.dram_tensor("tbmat", [128, H], F32, kind="ExternalInput")
    invk_d = nc.dram_tensor("invk", [128, 1], F32, kind="ExternalInput")
    alpha2_d = nc.dram_tensor("alpha2", [128, 1], F32, kind="ExternalInput")
    gpar_d = nc.dram_tensor("gpar", [128, 512], F32, kind="ExternalInput")
    bepar_d = nc.dram_tensor("bepar", [128, 512], F32, kind="ExternalInput")
    gtm_d = nc.dram_tensor("gtm", [128, DM], F32, kind="ExternalInput")
    betm_d = nc.dram_tensor("betm", [128, DM], F32, kind="ExternalInput")
    w0par_d = nc.dram_tensor("w0par", [128, 512], F32, kind="ExternalInput")
    b0par_d = nc.dram_tensor("b0par", [128, 512], F32, kind="ExternalInput")
    wgT_d = nc.dram_tensor("wgT", [DM, HIDDEN], I32, kind="ExternalInput")
    wuT_d = nc.dram_tensor("wuT", [DM, HIDDEN], I32, kind="ExternalInput")
    wdT_d = nc.dram_tensor("wdT", [HIDDEN, DM], I32, kind="ExternalInput")
    out_d = nc.dram_tensor("out", [TOUT, DM], F32, kind="ExternalOutput")

    xn_hbm = nc.dram_tensor("xn_hbm", [L, DM], BF16, kind=scratch_kind)
    xn2_hbm = nc.dram_tensor("xn2_hbm", [2 * L, DM], BF16, kind=scratch_kind)
    kfm_hbm = nc.dram_tensor("kfm_hbm", [DM, L], BF16, kind=scratch_kind)
    qfm_hbm = nc.dram_tensor("qfm_hbm", [DM, L], BF16, kind=scratch_kind)
    vfm_hbm = nc.dram_tensor("vfm_hbm", [DM, L], BF16, kind=scratch_kind)
    attn2_hbm = nc.dram_tensor("attn2_hbm", [2 * L, DM], BF16, kind=scratch_kind)
    h_hbm = nc.dram_tensor("h_hbm", [TOUT, HIDDEN], BF16, kind=scratch_kind)
    wgbf_hbm = nc.dram_tensor("wgbf_hbm", [DM, HIDDEN], BF16, kind="Internal")
    wubf_hbm = nc.dram_tensor("wubf_hbm", [DM, HIDDEN], BF16, kind="Internal")
    wdbf_hbm = nc.dram_tensor("wdbf_hbm", [HIDDEN, DM], BF16, kind="Internal")
    wobf_hbm = nc.dram_tensor("wobf_hbm", [DM, DM], BF16, kind="Internal")

    with ExitStack() as top:
        e = top.enter_context
        sem_names = [
            "cld", "wld", "x_ld", "xn_done", "xnsp", "xnsp2", "xnf_ld",
            "pmm", "evac", "ksp", "txp0", "txp1", "kvg",
            "va", "sa", "z1mm", "dwmm", "zqmm", "asp",
            "pc0", "pc1", "pc2", "pc3", "ac0", "ac1", "pre", "wdc", "bxs",
            "phb", "fwl", "fws", "bxn", "gumm", "silu", "hready",
            "hsp", "hld", "afm_ld", "xres_ld", "omm", "oev", "ost",
        ]
        S = {n: e(nc.semaphore(n)) for n in sem_names}
        dynr_sb = e(nc.sbuf_tensor("dynr_sb", [NSLAB + 16, 1], U32))

        with ExitStack() as pa:
            a = pa.enter_context
            thetab_sb = a(nc.sbuf_tensor("thetab_sb", [128, 8 * H], BF16))
            tbmat_sb = a(nc.sbuf_tensor("tbmat_sb", [128, H], F32))
            invk_sb = a(nc.sbuf_tensor("invk_sb", [128, 1], F32))
            alpha2_sb = a(nc.sbuf_tensor("alpha2_sb", [128, 1], F32))
            etab_sb = a(nc.sbuf_tensor("etab_sb", [128, 1], F32))
            gpar_sb = a(nc.sbuf_tensor("gpar_sb", [128, 512], BF16))
            bepar_sb = a(nc.sbuf_tensor("bepar_sb", [128, 512], BF16))
            g2par_sb = a(nc.sbuf_tensor("g2par_sb", [128, 512], BF16))
            gtm_sb = a(nc.sbuf_tensor("gtm_sb", [128, DM], BF16))
            betm_sb = a(nc.sbuf_tensor("betm_sb", [128, DM], BF16))
            gbetm_sb = a(nc.sbuf_tensor("gbetm_sb", [128, DM], BF16))
            wq_sb = a(nc.sbuf_tensor("wq_sb", [128, 8 * DM], BF16))
            wk_sb = a(nc.sbuf_tensor("wk_sb", [128, 8 * DM], BF16))
            wv_sb = a(nc.sbuf_tensor("wv_sb", [128, 8 * DM], BF16))
            x_slab = [a(nc.sbuf_tensor(f"x_slab{i}", [128, DM], F32)) for i in range(2)]
            sqx_sb = a(nc.sbuf_tensor("sqx_sb", [128, DM], F32))
            xnstg = a(nc.sbuf_tensor("xnstg", [128, 4 * DM], BF16))
            xnf_sb = [a(nc.sbuf_tensor(f"xnf_sb{i}", [128, 8 * 512], BF16)) for i in range(2)]
            kfm_sb = [a(nc.sbuf_tensor(f"kfm_sb{i}", [128, 8 * 512], BF16)) for i in range(2)]
            qfm_sb = [a(nc.sbuf_tensor(f"qfm_sb{i}", [128, 8 * 512], BF16)) for i in range(2)]
            vfm_sb = [a(nc.sbuf_tensor("vfm_sb0", [128, 8 * 512], BF16))] * 2
            st_s1 = a(nc.sbuf_tensor("st_s1", [128, 1], F32))
            st_s2 = a(nc.sbuf_tensor("st_s2", [128, 1], F32))
            st_m = a(nc.sbuf_tensor("st_m", [128, 1], F32))
            st_t1 = a(nc.sbuf_tensor("st_t1", [128, 1], F32))
            st_t2 = a(nc.sbuf_tensor("st_t2", [128, 1], F32))
            st_v = a(nc.sbuf_tensor("st_v", [128, 1], F32))
            st_d = a(nc.sbuf_tensor("st_d", [128, 1], F32))
            st_r = a(nc.sbuf_tensor("st_r", [128, 1], F32))
            lrall_sb = a(nc.sbuf_tensor("lrall_sb", [128, NSLAB * H], F32))
            sigall_sb = a(nc.sbuf_tensor("sigall_sb", [128, NSLAB * H], F32))
            eta_sb = a(nc.sbuf_tensor("eta_sb", [128, NSLAB * H], F32))
            txk_sb = [a(nc.sbuf_tensor(f"txk_sb{i}", [128, DM], BF16)) for i in range(2)]
            txq_sb = [a(nc.sbuf_tensor(f"txq_sb{i}", [128, DM], BF16)) for i in range(2)]
            txv_sb = [a(nc.sbuf_tensor(f"txv_sb{i}", [128, DM], BF16)) for i in range(2)]
            kvg_sb = [a(nc.sbuf_tensor(f"kvg_sb{i}", [128, DM], BF16)) for i in range(2)]
            w_par = a(nc.sbuf_tensor("w_par", [128, 512], F32))
            wbf = a(nc.sbuf_tensor("wbf", [128, 512], BF16))
            b_mat = a(nc.sbuf_tensor("b_mat", [128, 512], F32))
            kc_par = [a(nc.sbuf_tensor(f"kc_par{j}", [128, 512], BF16)) for j in range(2)]
            qc_par = [a(nc.sbuf_tensor(f"qc_par{j}", [128, 512], BF16)) for j in range(2)]
            kvg_par = [a(nc.sbuf_tensor(f"kvg_par{j}", [128, 512], BF16)) for j in range(2)]
            eta_par = [a(nc.sbuf_tensor(f"eta_par{j}", [128, 8], F32)) for j in range(2)]
            z1b_sb = a(nc.sbuf_tensor("z1b_sb", [128, 512], F32))
            sq_sb = a(nc.sbuf_tensor("sq_sb", [128, 512], F32))
            xh_sb = a(nc.sbuf_tensor("xh_sb", [128, 512], BF16))
            gxh_sb = a(nc.sbuf_tensor("gxh_sb", [128, 512], BF16))
            gx2_sb = a(nc.sbuf_tensor("gx2_sb", [128, 512], BF16))
            u_sb = a(nc.sbuf_tensor("u_sb", [128, 512], BF16))
            grad_sb = a(nc.sbuf_tensor("grad_sb", [128, 512], BF16))
            sck_sb = a(nc.sbuf_tensor("sck_sb", [128, 512], BF16))
            w1_sb = a(nc.sbuf_tensor("w1_sb", [128, 512], BF16))
            outp_sb = [a(nc.sbuf_tensor(f"outp_sb{j}", [128, 512], BF16)) for j in range(2)]
            c_s1 = a(nc.sbuf_tensor("c_s1", [128, 8], F32))
            c_s2 = a(nc.sbuf_tensor("c_s2", [128, 8], F32))
            c_m = a(nc.sbuf_tensor("c_m", [128, 8], F32))
            c_t = a(nc.sbuf_tensor("c_t", [128, 8], F32))
            c_v0 = a(nc.sbuf_tensor("c_v0", [128, 8], F32))
            c_d = a(nc.sbuf_tensor("c_d", [128, 8], F32))
            c_var = a(nc.sbuf_tensor("c_var", [128, 8], F32))
            c_std = a(nc.sbuf_tensor("c_std", [128, 8], F32))
            c_rstd = a(nc.sbuf_tensor("c_rstd", [128, 8], F32))
            c_r1 = a(nc.sbuf_tensor("c_r1", [128, 8], F32))
            c_r2 = a(nc.sbuf_tensor("c_r2", [128, 8], F32))
            c_a1 = a(nc.sbuf_tensor("c_a1", [128, 8], F32))
            c_a2 = a(nc.sbuf_tensor("c_a2", [128, 8], F32))
            c_eta2 = a(nc.sbuf_tensor("c_eta2", [128, 8], F32))
            c_eta2b = a(nc.sbuf_tensor("c_eta2b", [128, 8], BF16))
            attn_stage = a(nc.sbuf_tensor("attn_stage", [128, 4 * DM], BF16))
            pp_proj = [a(nc.psum_tensor(f"pp_proj{i}", [128, 512], F32)) for i in range(2)]
            pp_lr = a(nc.psum_tensor("pp_lr", [128, H], F32))
            pp_z1 = [a(nc.psum_tensor(f"pp_z1{i}", [128, 512], F32)) for i in range(2)]
            pp_zq = a(nc.psum_tensor("pp_zq", [128, 512], F32))
            pp_dw = a(nc.psum_tensor("pp_dw", [128, 512], F32))
            pp_db = a(nc.psum_tensor("pp_db", [128, 512], F32))

            blkA = pa.enter_context(nc.Block())

            def wsb(t):
                return (wk_sb, wq_sb, wv_sb)[t]

            def fmsb(t):
                return (kfm_sb, qfm_sb, vfm_sb)[t]

            def fmhbm(t):
                return (kfm_hbm, qfm_hbm, vfm_hbm)[t]

            @blkA.sync
            def _(sync):
                n_cld = 0
                for dst, src in ((tbmat_sb, tbmat_d), (invk_sb, invk_d),
                                 (alpha2_sb, alpha2_d), (w_par, w0par_d),
                                 (b_mat, b0par_d), (dynr_sb, dynr)):
                    sync.dma_start(out=dst[:, :], in_=src[:, :]).then_inc(S["cld"], 16)
                    n_cld += 1
                assert n_cld == 6
                sync.wait_ge(S["cld"], 16 * 6)
                if True:
                    def kqv_spill(s):
                        for t in range(3):
                            for p in range(8):
                                sync.wait_ge(S["evac"], 24 * s + t * 8 + p + 1)
                                sync.dma_start(
                                    out=fmhbm(t)[128 * p:128 * (p + 1), 512 * s:512 * (s + 1)],
                                    in_=fmsb(t)[s % 2][:, 512 * p:512 * (p + 1)],
                                ).then_inc(S["ksp"], 16)
                    def tx_load(m):
                        txs = S["txp%d" % (m % 2)]
                        sync.wait_ge(S["ksp"], 16 * 24 * (m // 4 + 1))
                        if m >= 2:
                            sync.wait_ge(S["kvg"], m - 1)
                            j = 2 * m - 3
                            sync.wait_ge(S["pc%d" % (j % 4)],
                                         16 * (6 * (j // 4) + 6))
                        sync.dma_start(out=txk_sb[m % 2][:, :],
                                       in_=kfm_hbm[:, 128 * m:128 * (m + 1)],
                                       transpose=True).then_inc(txs, 16)
                        sync.dma_start(out=txv_sb[m % 2][:, :],
                                       in_=vfm_hbm[:, 128 * m:128 * (m + 1)],
                                       transpose=True).then_inc(txs, 16)
                        sync.dma_start(out=txq_sb[m % 2][:, :],
                                       in_=qfm_hbm[:, 128 * m:128 * (m + 1)],
                                       transpose=True).then_inc(txs, 16)

                    def par_copies(i):
                        m, u = i // 2, i % 2
                        pcs = S["pc%d" % (i % 4)]
                        txs = S["txp%d" % (m % 2)]
                        if i >= 2:
                            sync.wait_ge(S["va"], vaN[("c", i - 2, 3)])
                        sync.wait_ge(S["kvg"], m + 1)
                        for q in range(2):
                            sync.dma_start(
                                out=_ap(kvg_par[i % 2], (64 * q) * 512,
                                        [[512, 64], [64, 8], [1, 64]]),
                                in_=_ap(kvg_sb[m % 2], (64 * u) * DM + 64 * q,
                                        [[DM, 64], [128, 8], [1, 64]]),
                            ).then_inc(pcs, 16)
                        sync.wait_ge(txs, 16 * (3 * (m // 2) + 1))
                        for q in range(2):
                            sync.dma_start(
                                out=_ap(kc_par[i % 2], (64 * q) * 512,
                                        [[512, 64], [64, 8], [1, 64]]),
                                in_=_ap(txk_sb[m % 2], (64 * u) * DM + 64 * q,
                                        [[DM, 64], [128, 8], [1, 64]]),
                            ).then_inc(pcs, 16)
                        if i >= 2:
                            sync.wait_ge(S["va"], vaN[("c", i - 2, 8)])
                        sync.wait_ge(txs, 16 * (3 * (m // 2) + 3))
                        for q in range(2):
                            sync.dma_start(
                                out=_ap(qc_par[i % 2], (64 * q) * 512,
                                        [[512, 64], [64, 8], [1, 64]]),
                                in_=_ap(txq_sb[m % 2], (64 * u) * DM + 64 * q,
                                        [[DM, 64], [128, 8], [1, 64]]),
                            ).then_inc(pcs, 16)

                    def attn_copy(i):
                        m, u = i // 2, i % 2
                        sync.wait_ge(S["va"], vaN[("c", i, 8)])
                        if m >= 4 and u == 0:
                            sync.wait_ge(S["asp"], 16 * (m // 4))
                        for q in range(2):
                            sync.dma_start(
                                out=_ap(attn_stage, (m % 4) * DM
                                        + (64 * u) * (4 * DM) + 64 * q,
                                        [[4 * DM, 64], [128, 8], [1, 64]]),
                                in_=_ap(outp_sb[i % 2], (64 * q) * 512,
                                        [[512, 64], [64, 8], [1, 64]]),
                            ).then_inc(S["ac%d" % (i % 2)], 16)
                        if i % 8 == 7:
                            g = i // 8
                            sync.wait_ge(S["ac0"], 16 * 2 * (4 * g + 4))
                            sync.wait_ge(S["ac1"], 16 * 2 * (4 * g + 4))
                            v = sync.value_load(dynr_sb[g:g + 1, 0:1])
                            sync.dma_start(
                                out=attn2_hbm[ds(v, 512), :].rearrange(
                                    "(u t) d -> t u d", t=128),
                                in_=_ap(attn_stage, 0,
                                        [[4 * DM, 128], [DM, 4], [1, DM]]),
                            ).then_inc(S["asp"], 16)

                    def batch_sync(y):
                        tx_load(4 * y)
                        tx_load(4 * y + 1)
                        for i in range(8 * y, 8 * y + 8):
                            if i == 8 * y + 2:
                                tx_load(4 * y + 2)
                            if i == 8 * y + 4:
                                tx_load(4 * y + 3)
                            par_copies(i)
                            if i >= 2:
                                attn_copy(i - 2)

                    for s in range(NSLICE):
                        for u in range(4):
                            k = 4 * s + u
                            if k >= 2:
                                sync.wait_ge(S["xn_done"], k - 1)
                            sync.dma_start(out=x_slab[k % 2][:, :],
                                           in_=x_seq[128 * k:128 * (k + 1), :],
                                           ).then_inc(S["x_ld"], 16)
                            sync.wait_ge(S["xn_done"], k + 1)
                            sync.dma_start(out=xn_hbm[128 * k:128 * (k + 1), :],
                                           in_=xnstg[:, DM * (k % 4):DM * (k % 4 + 1)],
                                           ).then_inc(S["xnsp"], 16)
                        if lvl >= 1 and s >= 1:
                            kqv_spill(s - 1)
                        if lvl >= 2 and s >= 1:
                            batch_sync(s - 1)
                    if lvl >= 1:
                        kqv_spill(NSLICE - 1)
                    if lvl >= 2:
                        batch_sync(NSLICE - 1)
                        attn_copy(NCH - 2)
                        attn_copy(NCH - 1)
                        sync.wait_ge(S["asp"], 16 * 8)
                    sync.wait_ge(S["xnsp"], 16 * NSLAB)
                    if not no_dyn:
                        sync.wait_ge(S["xnsp2"], 16 * 8)
                    if lvl >= 1:
                        sync.wait_ge(S["ksp"], 16 * 24 * NSLICE)

            @blkA.gpsimd
            def _(gpsimd):
                n = 0
                for dst, src in ((wk_sb, wk_d), (wq_sb, wq_d), (wv_sb, wv_d)):
                    for c in range(8):
                        gpsimd.dma_start(out=dst[:, DM * c:DM * (c + 1)],
                                         in_=src[128 * c:128 * (c + 1), :]).then_inc(S["wld"], 16)
                        n += 1
                for c in range(8):
                    gpsimd.dma_start(out=thetab_sb[:, H * c:H * (c + 1)],
                                     in_=thetab_d[128 * c:128 * (c + 1), :]).then_inc(S["wld"], 16)
                    n += 1
                for dst, src in ((gpar_sb, gpar_d), (bepar_sb, bepar_d),
                                 (gtm_sb, gtm_d), (betm_sb, betm_d)):
                    gpsimd.dma_start(out=dst[:, :], in_=src[:, :]).then_inc(S["wld"], 16)
                    n += 1
                assert n == 36
                gpsimd.wait_ge(S["cld"], 16 * 6)
                if lvl >= 2:
                    gpsimd.wait_ge(S["pre"], 1)
                _casts = {1: (wgbf_hbm, wgT_d), 2: (wubf_hbm, wuT_d),
                          3: (wdbf_hbm, wdT_d), 4: (wobf_hbm, wo_d)}

                def kvg_batch(y):
                    for m in range(4 * y, 4 * y + 4):
                        if m >= 2:
                            for j in (2 * m - 4, 2 * m - 3):
                                gpsimd.wait_ge(S["pc%d" % (j % 4)],
                                               16 * (6 * (j // 4) + 2))
                        gpsimd.wait_ge(S["txp%d" % (m % 2)],
                                       16 * (3 * (m // 2) + 2))
                        gpsimd.tensor_sub(kvg_sb[m % 2][:, :], txk_sb[m % 2][:, :],
                                          txv_sb[m % 2][:, :])
                        gpsimd.tensor_mul(kvg_sb[m % 2][:, :], kvg_sb[m % 2][:, :],
                                          gtm_sb[:, :])
                        gpsimd.tensor_add(kvg_sb[m % 2][:, :], kvg_sb[m % 2][:, :],
                                          gbetm_sb[:, :]).then_inc(S["kvg"], 1)

                for s in range(NSLICE):
                    if lvl >= 3 and s in _casts:
                        dst, src = _casts[s]
                        gpsimd.dma_start(out=dst[:, :],
                                         in_=src[:, :]).then_inc(S["wdc"], 16)
                    if lvl >= 2 and s >= 1:
                        kvg_batch(s - 1)
                    if not no_dyn:
                        gpsimd.wait_ge(S["xn_done"], 4 * s + 4)
                        v = gpsimd.value_load(dynr_sb[s:s + 1, 0:1])
                        src = _ap(xnstg, 0, [[4 * DM, 128], [DM, 4], [1, DM]])
                        gpsimd.dma_start(
                            out=xn2_hbm[ds(v, 512), :].rearrange("(u t) d -> t u d", t=128),
                            in_=src).then_inc(S["xnsp2"], 16)
                if lvl >= 2:
                    kvg_batch(NSLICE - 1)

            WLD_ALL = 16 * 36

            def batch_mms_impl(tensor, y):
                for i in range(8 * y, 8 * y + 8):
                    m, u = i // 2, i % 2
                    sl = m // 4
                    off = (i * 64) % 512
                    tensor.wait_ge(S["sa"], saN[("c0",)] if i == 0
                                   else saN[("s", i - 1, 4)])
                    if i >= 2:
                        tensor.wait_ge(S["va"], vaN[("c", i - 2, 1)])
                    for p in range(8):
                        for q in range(2):
                            mm = tensor.matmul(
                                pp_z1[i % 2][64 * q:64 * (q + 1), 64 * p:64 * (p + 1)],
                                kfm_sb[sl % 2][64 * q:64 * (q + 1),
                                               512 * p + off:512 * p + off + 64],
                                wbf[64 * q:64 * (q + 1), 64 * p:64 * (p + 1)],
                                start=True, stop=True, tile_position=(64 * q, 64 * q))
                            if p == 7 and q == 1:
                                mm.then_inc(S["z1mm"], 1)
                    tensor.wait_ge(S["va"], vaN[("c", i, 3)])
                    if i >= 1:
                        tensor.wait_ge(S["va"], vaN[("c", i - 1, 5)])
                    for p in range(8):
                        for q in range(2):
                            mm = tensor.matmul(
                                pp_dw[64 * q:64 * (q + 1), 64 * p:64 * (p + 1)],
                                sck_sb[64 * q:64 * (q + 1), 64 * p:64 * (p + 1)],
                                grad_sb[64 * q:64 * (q + 1), 64 * p:64 * (p + 1)],
                                start=True, stop=True, tile_position=(64 * q, 64 * q))
                            if p == 7 and q == 1:
                                mm.then_inc(S["dwmm"], 1)
                    for p in range(8):
                        for q in range(2):
                            lhs = _ap(c_eta2b, q * 64 * 8 + p, [[8, 64], [0, 64]])
                            mm = tensor.matmul(
                                pp_db[64 * q:64 * (q + 1), 64 * p:64 * (p + 1)],
                                lhs,
                                grad_sb[64 * q:64 * (q + 1), 64 * p:64 * (p + 1)],
                                start=True, stop=True, tile_position=(64 * q, 64 * q))
                            if p == 7 and q == 1:
                                mm.then_inc(S["dwmm"], 1)
                    tensor.wait_ge(S["sa"], saN[("s", i, 4)])
                    if i >= 1:
                        tensor.wait_ge(S["va"], vaN[("c", i - 1, 6)])
                    for p in range(8):
                        for q in range(2):
                            mm = tensor.matmul(
                                pp_zq[64 * q:64 * (q + 1), 64 * p:64 * (p + 1)],
                                qfm_sb[sl % 2][64 * q:64 * (q + 1),
                                               512 * p + off:512 * p + off + 64],
                                wbf[64 * q:64 * (q + 1), 64 * p:64 * (p + 1)],
                                start=True, stop=True, tile_position=(64 * q, 64 * q))
                            if p == 7 and q == 1:
                                mm.then_inc(S["zqmm"], 1)

            @blkA.tensor
            def _(tensor):
                def batch_mms(y):
                    tensor.wait_ge(S["evac"], 24 * (y + 1))
                    batch_mms_impl(tensor, y)

                tensor.wait_ge(S["wld"], WLD_ALL)
                for s in range(NSLICE if lvl >= 1 else 0):
                    tensor.wait_ge(S["xnf_ld"], 16 * 8 * (s + 1))
                    for t in range(3):
                        for p in range(8):
                            g = 24 * s + t * 8 + p
                            if g >= 2:
                                tensor.wait_ge(S["evac"], g - 1)
                            for c in range(8):
                                mm = tensor.matmul(
                                    pp_proj[g % 2][:, :],
                                    wsb(t)[:, DM * c + 128 * p: DM * c + 128 * (p + 1)],
                                    xnf_sb[s % 2][:, 512 * c:512 * (c + 1)],
                                    start=(c == 0), stop=(c == 7))
                                if c == 7:
                                    mm.then_inc(S["pmm"], 1)
                    tensor.wait_ge(S["evac"], 24 * s + 8)
                    for u in range(4 if not no_lr else 0):
                        k = 4 * s + u
                        if k >= 1:
                            tensor.wait_ge(S["va"], vaN[("l", k - 1)])
                        for p in range(8):
                            mm = tensor.matmul(
                                pp_lr[:, :],
                                kfm_sb[s % 2][:, 512 * p + 128 * u: 512 * p + 128 * (u + 1)],
                                thetab_sb[:, H * p:H * (p + 1)],
                                start=(p == 0), stop=(p == 7))
                            if p == 7:
                                mm.then_inc(S["pmm"], 1)
                    if lvl >= 2 and s >= 1:
                        batch_mms(s - 1)
                if lvl >= 2:
                    batch_mms(NSLICE - 1)

            @blkA.scalar
            def _(scalar):
                def batch_sc(y):
                    for i in range(8 * y, 8 * y + 8):
                        m, u = i // 2, i % 2
                        scalar.wait_ge(S["va"], vaN[("e", m)])
                        if i >= 2:
                            scalar.wait_ge(S["va"], vaN[("c", i - 2, 3)])
                        last = None
                        for q in range(2):
                            src = _ap(eta_sb, (64 * u) * (NSLAB * H) + H * m + q,
                                      [[NSLAB * H, 64], [2, 8]])
                            dst_ap = _ap(eta_par[i % 2], (64 * q) * 8, [[8, 64], [1, 8]])
                            last = scalar.copy(dst_ap, src)
                        last.then_inc(S["sa"], 1)
                        scalar.wait_ge(S["va"], vaN[("c", i, 1)])
                        scalar.activation(sq_sb[:, :], z1b_sb[:, :],
                                          AF.Square).then_inc(S["sa"], 1)
                        scalar.wait_ge(S["va"], vaN[("c", i, 2)])
                        scalar.activation(c_std[:, :], c_var[:, :],
                                          AF.Sqrt).then_inc(S["sa"], 1)
                        scalar.wait_ge(S["va"], vaN[("c", i, 4)])
                        scalar.copy(wbf[:, :], w_par[:, :]).then_inc(S["sa"], 1)
                        scalar.wait_ge(S["va"], vaN[("c", i, 6)])
                        scalar.activation(sq_sb[:, :], z1b_sb[:, :],
                                          AF.Square).then_inc(S["sa"], 1)
                        scalar.wait_ge(S["va"], vaN[("c", i, 7)])
                        scalar.activation(c_std[:, :], c_var[:, :],
                                          AF.Sqrt).then_inc(S["sa"], 1)

                if lvl >= 2:
                    scalar.wait_ge(S["cld"], 16 * 6)
                    scalar.copy(wbf[:, :], w_par[:, :]).then_inc(S["sa"], 1)
                for s in range(NSLICE):
                    for u in range(4):
                        k = 4 * s + u
                        scalar.wait_ge(S["x_ld"], 16 * (k + 1))
                        scalar.activation(sqx_sb[:, :], x_slab[k % 2][:, :],
                                          AF.Square).then_inc(S["sa"], 1)
                        scalar.wait_ge(S["va"], vaN[("r", k)])
                        scalar.activation(st_r[:, :], st_t1[:, :],
                                          AF.Sqrt).then_inc(S["sa"], 1)
                    if lvl >= 1:
                        scalar.wait_ge(S["xnsp"], 16 * 4 * (s + 1))
                        if s >= 2:
                            scalar.wait_ge(S["pmm"], 28 * (s - 1))
                        for c in range(8):
                            scalar.dma_start(
                                out=xnf_sb[s % 2][:, 512 * c:512 * (c + 1)],
                                in_=xn_hbm[512 * s:512 * (s + 1), 128 * c:128 * (c + 1)],
                                transpose=True).then_inc(S["xnf_ld"], 16)
                        if not no_lr:
                            for u in range(4):
                                k = 4 * s + u
                                scalar.wait_ge(S["va"], vaN[("l", k)])
                                scalar.activation(sigall_sb[:, H * k:H * (k + 1)],
                                                  lrall_sb[:, H * k:H * (k + 1)],
                                                  AF.Sigmoid).then_inc(S["sa"], 1)
                    if lvl >= 2 and s >= 1:
                        batch_sc(s - 1)
                if lvl >= 2:
                    batch_sc(NSLICE - 1)

            def batch_vec_impl(vector, y):
                for i in range(8 * y, 8 * y + 8):
                    m, u = i // 2, i % 2
                    vector.wait_ge(S["z1mm"], i + 1)
                    vector.tensor_add(z1b_sb[:, :], pp_z1[i % 2][:, :],
                                      b_mat[:, :]).then_inc(S["va"], 1)
                    vector.reduce_sum(c_s1[:, :], _r3(z1b_sb), axis=AXX)
                    vector.wait_ge(S["sa"], saN[("s", i, 2)])
                    vector.reduce_sum(c_s2[:, :], _r3(sq_sb), axis=AXX)
                    vector.tensor_scalar_mul(c_m[:, :], c_s1[:, :], -1.0 / 64.0)
                    vector.tensor_scalar(c_v0[:, :], c_s2[:, :], 64.0 * EPS_SCAN,
                                         1.0 / 64.0, ALU.add, ALU.mult)
                    vector.tensor_mul(c_t[:, :], c_m[:, :], c_m[:, :])
                    vector.tensor_scalar_mul(c_d[:, :], c_s1[:, :], 1.0)
                    vector.tensor_sub(c_var[:, :], c_v0[:, :],
                                      c_t[:, :]).then_inc(S["va"], 1)
                    vector.wait_ge(S["sa"], saN[("s", i, 3)])
                    vector.reciprocal(c_rstd[:, :], c_std[:, :])
                    z3 = _r3(z1b_sb)
                    vector.tensor_tensor(z3, z3, _grp(c_m), ALU.add)
                    vector.tensor_tensor(_r3(xh_sb), z3, _grp(c_rstd), ALU.mult)
                    vector.wait_ge(S["sa"], saN[("s", i, 1)])
                    vector.wait_ge(S["pc%d" % (i % 4)], 16 * (6 * (i // 4) + 4))
                    vector.tensor_mul(gxh_sb[:, :], xh_sb[:, :], g2par_sb[:, :])
                    vector.tensor_add(gxh_sb[:, :], gxh_sb[:, :], kvg_par[i % 2][:, :])
                    vector.reduce_sum(c_r1[:, :], _r3(gxh_sb), axis=AXX)
                    vector.tensor_mul(gx2_sb[:, :], gxh_sb[:, :], xh_sb[:, :])
                    vector.reduce_sum(c_r2[:, :], _r3(gx2_sb), axis=AXX)
                    vector.tensor_scalar_mul(c_a1[:, :], c_r1[:, :], 1.0 / 64.0)
                    vector.tensor_scalar_mul(c_a2[:, :], c_r2[:, :], 1.0 / 64.0)
                    vector.tensor_mul(c_eta2[:, :], eta_par[i % 2][:, :], c_rstd[:, :])
                    vector.tensor_tensor(_r3(u_sb), _r3(xh_sb), _grp(c_a2), ALU.mult)
                    vector.tensor_copy(c_eta2b[:, :], c_eta2[:, :])
                    vector.tensor_sub(grad_sb[:, :], gxh_sb[:, :], u_sb[:, :])
                    vector.tensor_tensor(_r3(grad_sb), _r3(grad_sb), _grp(c_a1), ALU.subtract)
                    vector.tensor_tensor(_r3(sck_sb), _r3(kc_par[i % 2]),
                                         _ap(c_eta2b, 0, [[8, 128], [1, 8], [0, 64]]),
                                         ALU.mult).then_inc(S["va"], 1)
                    vector.wait_ge(S["dwmm"], 2 * i + 2)
                    vector.tensor_sub(w_par[:, :], w_par[:, :],
                                      pp_dw[:, :]).then_inc(S["va"], 1)
                    vector.tensor_sub(b_mat[:, :], b_mat[:, :],
                                      pp_db[:, :]).then_inc(S["va"], 1)
                    vector.wait_ge(S["zqmm"], i + 1)
                    vector.tensor_add(z1b_sb[:, :], pp_zq[:, :],
                                      b_mat[:, :]).then_inc(S["va"], 1)
                    vector.reduce_sum(c_s1[:, :], _r3(z1b_sb), axis=AXX)
                    vector.wait_ge(S["sa"], saN[("s", i, 5)])
                    vector.reduce_sum(c_s2[:, :], _r3(sq_sb), axis=AXX)
                    vector.tensor_scalar_mul(c_m[:, :], c_s1[:, :], -1.0 / 64.0)
                    vector.tensor_scalar(c_v0[:, :], c_s2[:, :], 64.0 * EPS_SCAN,
                                         1.0 / 64.0, ALU.add, ALU.mult)
                    vector.tensor_mul(c_t[:, :], c_m[:, :], c_m[:, :])
                    vector.tensor_scalar_mul(c_d[:, :], c_s1[:, :], 1.0)
                    vector.tensor_sub(c_var[:, :], c_v0[:, :],
                                      c_t[:, :]).then_inc(S["va"], 1)
                    vector.wait_ge(S["sa"], saN[("s", i, 6)])
                    vector.reciprocal(c_rstd[:, :], c_std[:, :])
                    z3b = _r3(z1b_sb)
                    vector.tensor_tensor(z3b, z3b, _grp(c_m), ALU.add)
                    vector.tensor_tensor(_r3(w1_sb), z3b, _grp(c_rstd), ALU.mult)
                    vector.tensor_mul(w1_sb[:, :], w1_sb[:, :], gpar_sb[:, :])
                    vector.tensor_add(w1_sb[:, :], w1_sb[:, :], bepar_sb[:, :])
                    vector.wait_ge(S["pc%d" % (i % 4)], 16 * (6 * (i // 4) + 6))
                    if i >= 2:
                        vector.wait_ge(S["ac%d" % (i % 2)], 16 * 2 * (i // 2))
                    vector.tensor_add(outp_sb[i % 2][:, :], w1_sb[:, :],
                                      qc_par[i % 2][:, :]).then_inc(S["va"], 1)

            @blkA.vector
            def _(vector):
                def batch_vec(y):
                    batch_vec_impl(vector, y)

                vector.wait_ge(S["wld"], WLD_ALL)
                vector.wait_ge(S["cld"], 16 * 6)
                vector.tensor_add(etab_sb[:, :], invk_sb[:, :], alpha2_sb[:, :])
                vector.tensor_mul(g2par_sb[:, :], gpar_sb[:, :], gpar_sb[:, :])
                vector.tensor_mul(gbetm_sb[:, :], gtm_sb[:, :],
                                  betm_sb[:, :]).then_inc(S["pre"], 1)
                vector.tensor_relu(etab_sb[:, :], etab_sb[:, :])
                for s in range(NSLICE):
                    for u in range(4):
                        k = 4 * s + u
                        vector.wait_ge(S["x_ld"], 16 * (k + 1))
                        vector.reduce_sum(st_s1[:, :], x_slab[k % 2][:, :], axis=AXX)
                        vector.wait_ge(S["sa"], saN[("q", k)])
                        vector.reduce_sum(st_s2[:, :], sqx_sb[:, :], axis=AXX)
                        vector.tensor_scalar_mul(st_m[:, :], st_s1[:, :], -1.0 / DM)
                        vector.tensor_scalar(st_v[:, :], st_s2[:, :], DM * EPS_MAIN,
                                             1.0 / DM, ALU.add, ALU.mult)
                        vector.tensor_mul(st_t2[:, :], st_m[:, :], st_m[:, :])
                        vector.tensor_scalar_mul(st_d[:, :], st_s1[:, :], 1.0)
                        vector.tensor_sub(st_v[:, :], st_v[:, :], st_t2[:, :])
                        vector.tensor_scalar_mul(st_d[:, :], st_s1[:, :], 1.0)
                        vector.reciprocal(st_t1[:, :], st_v[:, :]).then_inc(S["va"], 1)
                        vector.wait_ge(S["sa"], saN[("t", k)])
                        if k >= 4:
                            vector.wait_ge(S["xnsp"], 16 * (k - 3))
                            if not no_dyn:
                                vector.wait_ge(S["xnsp2"], 16 * (k // 4))
                        vector.tensor_scalar(xnstg[:, DM * (k % 4):DM * (k % 4 + 1)],
                                             x_slab[k % 2][:, :],
                                             st_m[:, :], st_r[:, :],
                                             ALU.add, ALU.mult).then_inc(S["xn_done"], 1)
                    if lvl >= 2 and s >= 2:
                        vector.wait_ge(S["zqmm"], 8 * (s - 1))
                    for t in range(3 if lvl >= 1 else 0):
                        for p in range(8):
                            g = 24 * s + t * 8 + p
                            j = t * 8 + p
                            vector.wait_ge(S["pmm"], PMM(s, j))
                            if s >= 2:
                                vector.wait_ge(S["ksp"], 16 * 24 * (s - 1))
                            if t == 2 and s >= 1:
                                vector.wait_ge(S["ksp"], 16 * 24 * s)
                            vector.tensor_copy(fmsb(t)[s % 2][:, 512 * p:512 * (p + 1)],
                                               pp_proj[g % 2][:, :]).then_inc(S["evac"], 1)
                    for u in range(4 if (lvl >= 1 and not no_lr) else 0):
                        k = 4 * s + u
                        vector.wait_ge(S["pmm"], PMM_LR(s, u))
                        vector.tensor_add(lrall_sb[:, H * k:H * (k + 1)], pp_lr[:, :],
                                          tbmat_sb[:, :]).then_inc(S["va"], 1)
                    for u in range(4 if (lvl >= 1 and not no_lr) else 0):
                        k = 4 * s + u
                        vector.wait_ge(S["sa"], saN[("g", k)])
                        vector.tensor_scalar(
                            eta_sb[:, H * k:H * (k + 1)],
                            sigall_sb[:, H * k:H * (k + 1)], etab_sb[:, :], 1.0 / 64.0,
                            ALU.mult, ALU.mult).then_inc(S["va"], 1)
                    if lvl >= 2 and s >= 1:
                        batch_vec(s - 1)
                if lvl >= 2:
                    batch_vec(NSLICE - 1)

        if lvl < 3:
            return nc
        with ExitStack() as pb1:
            b1 = pb1.enter_context
            wg_sb = b1(nc.sbuf_tensor("wg_sb", [128, 8 * HIDDEN], BF16))
            wu_sb = b1(nc.sbuf_tensor("wu_sb", [128, 8 * HIDDEN], BF16))
            xnfB = [b1(nc.sbuf_tensor(f"xnfB{i}", [128, 8 * 512], BF16)) for i in range(2)]
            h_sb = b1(nc.sbuf_tensor("h_sb", [128, 24 * 512], BF16))
            sg_sb = b1(nc.sbuf_tensor("sg_sb", [128, 512], F32))
            pp_gate = [b1(nc.psum_tensor(f"pp_gate{i}", [128, 512], F32)) for i in range(2)]
            pp_up = [b1(nc.psum_tensor(f"pp_up{i}", [128, 512], F32)) for i in range(2)]

            blkB1 = pb1.enter_context(nc.Block())

            @blkB1.sync
            def _(sync):
                sync.sem_inc(S["phb"], 1)
                for j in range(4):
                    for oc in range(24):
                        sync.wait_ge(S["hready"], 24 * j + oc + 1)
                        sync.dma_start(
                            out=_ap(h_hbm, (4 * j) * 128 * HIDDEN + 128 * oc,
                                    [[HIDDEN, 128], [128 * HIDDEN, 4], [1, 128]]),
                            in_=h_sb[:, 512 * oc:512 * (oc + 1)],
                        ).then_inc(S["hsp"], 16)
                if lvl == 3:
                    sync.wait_ge(S["hsp"], 16 * 24 * 4)

            @blkB1.gpsimd
            def _(gpsimd):
                gpsimd.wait_ge(S["phb"], 1)
                gpsimd.wait_ge(S["wdc"], 16)
                for c in range(8):
                    gpsimd.dma_start(out=wg_sb[:, HIDDEN * c:HIDDEN * (c + 1)],
                                     in_=wgbf_hbm[128 * c:128 * (c + 1), :],
                                     ).then_inc(S["fwl"], 16)
                gpsimd.wait_ge(S["wdc"], 32)
                for c in range(8):
                    gpsimd.dma_start(out=wu_sb[:, HIDDEN * c:HIDDEN * (c + 1)],
                                     in_=wubf_hbm[128 * c:128 * (c + 1), :],
                                     ).then_inc(S["fwl"], 16)

            @blkB1.scalar
            def _(scalar):
                scalar.wait_ge(S["phb"], 1)
                for j in range(2):
                    for c in range(8):
                        scalar.dma_start(
                            out=xnfB[j % 2][:, 512 * c:512 * (c + 1)],
                            in_=xn2_hbm[L + 512 * j:L + 512 * (j + 1),
                                        128 * c:128 * (c + 1)],
                            transpose=True).then_inc(S["bxn"], 16)
                for j in range(4):
                    for oc in range(24):
                        g = 24 * j + oc
                        scalar.wait_ge(S["gumm"], 2 * g + 1)
                        if g >= 1:
                            scalar.wait_ge(S["hready"], g)
                        scalar.activation(sg_sb[:, :], pp_gate[g % 2][:, :],
                                          AF.Silu).then_inc(S["silu"], 1)
                    jj = j + 2
                    if jj <= 3:
                        scalar.wait_ge(S["gumm"], 48 * (jj - 1))
                        for c in range(8):
                            scalar.dma_start(
                                out=xnfB[jj % 2][:, 512 * c:512 * (c + 1)],
                                in_=xn2_hbm[L + 512 * jj:L + 512 * (jj + 1),
                                            128 * c:128 * (c + 1)],
                                transpose=True).then_inc(S["bxn"], 16)

            @blkB1.tensor
            def _(tensor):
                tensor.wait_ge(S["fwl"], 16 * 8)
                for j in range(4):
                    tensor.wait_ge(S["bxs"], j + 1)
                    for oc in range(24):
                        g = 24 * j + oc
                        if g >= 2:
                            tensor.wait_ge(S["silu"], g - 1)
                            tensor.wait_ge(S["hready"], g - 1)
                        for c in range(8):
                            mm = tensor.matmul(
                                pp_gate[g % 2][:, :],
                                wg_sb[:, HIDDEN * c + 128 * oc: HIDDEN * c + 128 * (oc + 1)],
                                xnfB[j % 2][:, 512 * c:512 * (c + 1)],
                                start=(c == 0), stop=(c == 7))
                            if c == 7:
                                mm.then_inc(S["gumm"], 1)
                        if g == 0:
                            tensor.wait_ge(S["fws"], 1)
                        for c in range(8):
                            mm = tensor.matmul(
                                pp_up[g % 2][:, :],
                                wu_sb[:, HIDDEN * c + 128 * oc: HIDDEN * c + 128 * (oc + 1)],
                                xnfB[j % 2][:, 512 * c:512 * (c + 1)],
                                start=(c == 0), stop=(c == 7))
                            if c == 7:
                                mm.then_inc(S["gumm"], 1)

            @blkB1.vector
            def _(vector):
                vector.wait_ge(S["phb"], 1)
                for j in range(4):
                    vector.wait_ge(S["bxn"], 16 * 8 * (j + 1))
                    vector.tensor_scalar_mul(xnfB[j % 2][:, :], xnfB[j % 2][:, :],
                                             1.0 / 127.0).then_inc(S["bxs"], 1)
                    if j == 0:
                        vector.wait_ge(S["fwl"], 16 * 16)
                        vector.tensor_scalar_mul(wu_sb[:, :], wu_sb[:, :],
                                                 1.0 / 127.0).then_inc(S["fws"], 1)
                    vector.wait_ge(S["hsp"], 16 * 24 * j)
                    for oc in range(24):
                        g = 24 * j + oc
                        vector.wait_ge(S["silu"], g + 1)
                        vector.wait_ge(S["gumm"], 2 * g + 2)
                        vector.tensor_mul(h_sb[:, 512 * oc:512 * (oc + 1)],
                                          sg_sb[:, :],
                                          pp_up[g % 2][:, :]).then_inc(S["hready"], 1)

        if lvl < 4:
            return nc
        with ExitStack() as pb2:
            b2 = pb2.enter_context
            wd_sb = b2(nc.sbuf_tensor("wd_sb", [128, 24 * DM], BF16))
            wo_sb = b2(nc.sbuf_tensor("wo_sb", [128, 8 * DM], BF16))
            afm_sb = b2(nc.sbuf_tensor("afm_sb", [128, 8 * TOUT], BF16))
            hsub_sb = [b2(nc.sbuf_tensor(f"hsub{i}", [128, 24 * 128], BF16)) for i in range(2)]
            xres_sb = [b2(nc.sbuf_tensor(f"xres{i}", [128, 4 * DM], F32)) for i in range(2)]
            out_sb = [b2(nc.sbuf_tensor(f"outsb{i}", [128, DM], F32)) for i in range(2)]
            pp_out = [b2(nc.psum_tensor(f"pp_out{i}", [128, DM], F32)) for i in range(3)]

            blkB2 = pb2.enter_context(nc.Block())

            B2_GATE = (16 * 24 * 4)

            @blkB2.sync
            def _(sync):
                if True:
                    for r in range(4):
                        sync.wait_ge(S["gumm"], 192)
                        sync.wait_ge(S["asp"], 16 * 8)
                        sync.dma_start(
                            out=afm_sb[:, TOUT * (4 + r):TOUT * (5 + r)],
                            in_=attn2_hbm[L:L + TOUT, 128 * (4 + r):128 * (5 + r)],
                            transpose=True).then_inc(S["afm_ld"], 16)
                    for r in range(16):
                        sync.wait_ge(S["hsp"], 16 * 24 * (r // 4 + 1))
                        if r >= 2:
                            sync.wait_ge(S["omm"], 2 * (r - 1))
                        sync.dma_start(
                            out=hsub_sb[r % 2][:, :],
                            in_=h_hbm[128 * r:128 * (r + 1), :],
                        ).then_inc(S["hld"], 16)
                        if r % 4 == 0:
                            g = r // 4
                            if g == 0:
                                sync.wait_ge(S["hready"], 96)
                                sync.wait_ge(S["hsp"], 16 * 24 * 4)
                            if g >= 2:
                                sync.wait_ge(S["oev"], 4 * (g - 1))
                            v = sync.value_load(dynr_sb[8 + g:8 + g + 1, 0:1])
                            sync.dma_start(
                                out=xres_sb[g % 2][:, :].rearrange(
                                    "t (u d) -> t u d", u=4),
                                in_=x_seq[ds(v, 512), :].rearrange(
                                    "(u t) d -> t u d", t=128)).then_inc(S["xres_ld"], 16)
                        sync.wait_ge(S["oev"], r + 1)
                        sync.dma_start(out=out_d[128 * r:128 * (r + 1), :],
                                       in_=out_sb[r % 2][:, :]).then_inc(S["ost"], 16)
                    sync.wait_ge(S["ost"], 16 * 16)

            @blkB2.gpsimd
            def _(gpsimd):
                gpsimd.wait_ge(S["gumm"], 192)
                gpsimd.wait_ge(S["wdc"], 16 * 4)
                for c in range(24):
                    gpsimd.dma_start(out=wd_sb[:, DM * c:DM * (c + 1)],
                                     in_=wdbf_hbm[128 * c:128 * (c + 1), :],
                                     ).then_inc(S["fwl"], 16)
                for c in range(8):
                    gpsimd.dma_start(out=wo_sb[:, DM * c:DM * (c + 1)],
                                     in_=wobf_hbm[128 * c:128 * (c + 1), :],
                                     ).then_inc(S["fwl"], 16)

            @blkB2.scalar
            def _(scalar):
                scalar.wait_ge(S["gumm"], 192)
                scalar.wait_ge(S["asp"], 16 * 8)
                for c in range(4):
                    scalar.dma_start(
                        out=afm_sb[:, TOUT * c:TOUT * (c + 1)],
                        in_=attn2_hbm[L:L + TOUT, 128 * c:128 * (c + 1)],
                        transpose=True).then_inc(S["afm_ld"], 16)

            @blkB2.tensor
            def _(tensor):
                tensor.wait_ge(S["fwl"], 16 * 48)
                for r in range(16):
                    tensor.wait_ge(S["hld"], 16 * (r + 1))
                    if r >= 3:
                        tensor.wait_ge(S["oev"], r - 2)
                    for half in range(2):
                        for oc in range(24):
                            tensor.matmul(
                                pp_out[r % 3][:, 512 * half:512 * (half + 1)],
                                hsub_sb[r % 2][:, 128 * oc:128 * (oc + 1)],
                                wd_sb[:, DM * oc + 512 * half: DM * oc + 512 * (half + 1)],
                                start=(oc == 0), stop=False)
                        if r == 0 and half == 0:
                            tensor.wait_ge(S["afm_ld"], 16 * 8)
                        for c in range(8):
                            mm = tensor.matmul(
                                pp_out[r % 3][:, 512 * half:512 * (half + 1)],
                                afm_sb[:, TOUT * c + 128 * r: TOUT * c + 128 * (r + 1)],
                                wo_sb[:, DM * c + 512 * half: DM * c + 512 * (half + 1)],
                                start=False, stop=(c == 7))
                            if c == 7:
                                mm.then_inc(S["omm"], 1)

            @blkB2.vector
            def _(vector):
                for r in range(16):
                    vector.wait_ge(S["omm"], 2 * (r + 1))
                    vector.wait_ge(S["xres_ld"], 16 * (r // 4 + 1))
                    if r >= 2:
                        vector.wait_ge(S["ost"], 16 * (r - 1))
                    vector.tensor_add(out_sb[r % 2][:, :], pp_out[r % 3][:, :],
                                      xres_sb[(r // 4) % 2][:, DM * (r % 4):DM * (r % 4 + 1)],
                                      ).then_inc(S["oev"], 1)
    return nc


_CACHE = {}


def _stage(inputs):
    x = np.ascontiguousarray(np.asarray(inputs["L0_attn_hidden_states"], np.float32))
    theta = np.asarray(inputs["L0_attn_theta"], np.float32)
    thetab = np.zeros((DM, H), np.float32)
    for h in range(H):
        thetab[64 * h:64 * h + 64, h] = theta[h, :, 0]
    tbmat = np.broadcast_to(
        np.asarray(inputs["L0_attn_theta_bias"], np.float32).reshape(1, H), (128, H))
    invk = np.tile(1.0 / np.arange(1, 65, dtype=np.float32), 2)[:, None]
    alpha2 = np.tile(np.asarray(inputs["L0_attn_alpha"], np.float32), 2)[:, None]
    gamma = np.asarray(inputs["L0_attn_gamma"], np.float32)[:, 0, :]
    beta = np.asarray(inputs["L0_attn_beta"], np.float32)[:, 0, :]
    W1 = np.asarray(inputs["L0_attn_W1"], np.float32)
    b1 = np.asarray(inputs["L0_attn_b1"], np.float32)[:, 0, :]

    def to_par(v):
        out = np.zeros((128, 512), np.float32)
        for p in range(8):
            for q in range(2):
                out[q * 64:q * 64 + 64, p * 64:p * 64 + 64] = v[2 * p + q][None, :]
        return out
    gpar, bepar, b0par = to_par(gamma), to_par(beta), to_par(b1)
    w0par = np.zeros((128, 512), np.float32)
    for p in range(8):
        for q in range(2):
            w0par[q * 64:q * 64 + 64, p * 64:p * 64 + 64] = W1[2 * p + q]
    gtm = np.zeros((1, DM), np.float32)
    betm = np.zeros((1, DM), np.float32)
    for j in range(H):
        gtm[0, 64 * j:64 * j + 64] = gamma[j]
        betm[0, 64 * j:64 * j + 64] = beta[j]
    shared = dict(
        wq=np.ascontiguousarray(np.asarray(inputs["L0_attn_wq"], np.float32)),
        wk=np.ascontiguousarray(np.asarray(inputs["L0_attn_wk"], np.float32)),
        wv=np.ascontiguousarray(np.asarray(inputs["L0_attn_wv"], np.float32)),
        wo=np.ascontiguousarray(np.asarray(inputs["L0_attn_wo"], np.float32)),
        thetab=thetab,
        tbmat=np.ascontiguousarray(tbmat).astype(np.float32),
        invk=invk, alpha2=np.ascontiguousarray(alpha2),
        gpar=gpar, bepar=bepar,
        gtm=np.ascontiguousarray(np.broadcast_to(gtm, (128, DM))),
        betm=np.ascontiguousarray(np.broadcast_to(betm, (128, DM))),
        w0par=w0par, b0par=b0par,
        wgT=np.ascontiguousarray(np.asarray(inputs["L0_ffn_w_gate"], np.int32).T),
        wuT=np.ascontiguousarray(np.asarray(inputs["L0_ffn_w_up"], np.int32).T),
        wdT=np.ascontiguousarray(np.asarray(inputs["L0_ffn_w_down"], np.int32).T),
    )
    in_maps = []
    for core in range(8):
        b, half = core // 2, core % 2
        row_off = half * TOUT
        dynr_np = np.zeros((NSLAB + 16, 1), np.uint32)
        for g in range(8):
            dynr_np[g, 0] = (L - row_off) + 512 * g
        for g in range(4):
            dynr_np[8 + g, 0] = row_off + 512 * g
        m = dict(shared)
        m["x_seq"] = np.ascontiguousarray(x[b])
        m["dynr"] = dynr_np
        in_maps.append(m)
    return in_maps


def kernel(**inputs):
    if "nc" not in _CACHE:
        _CACHE["nc"] = build_nc()
    nc = _CACHE["nc"]
    in_maps = _stage(inputs)
    res = run_bass_kernel_spmd(nc, in_maps, core_ids=list(range(8)))
    out = np.zeros((4, L, DM), np.float32)
    for core in range(8):
        b, half = core // 2, core % 2
        out[b, half * TOUT:(half + 1) * TOUT] = res.results[core]["out"]
    return out


# version 13
# speedup vs baseline: 1.2296x; 1.0369x over previous
import numpy as np
import ml_dtypes
from contextlib import ExitStack

import concourse.bass as bass
import concourse.mybir as mybir
from concourse.bass_utils import run_bass_kernel_spmd

F32 = mybir.dt.float32
BF16 = mybir.dt.bfloat16
I32 = mybir.dt.int32
U32 = mybir.dt.uint32
AF = mybir.ActivationFunctionType
ALU = mybir.AluOpType
AXX = mybir.AxisListType.X
ds = bass.ds
AP = bass.AP

L, DM, H, HD = 4096, 1024, 16, 64
NSLAB, NSLICE, NCH = 32, 8, 64
HIDDEN = 3072
TOUT = 2048
EPS_MAIN, EPS_SCAN = 1e-5, 1e-6


def _ap(t, offset, dims):
    tensor = t.tensor if hasattr(t, "tensor") else t
    return AP(tensor, offset, [list(d) for d in dims])


def _grp(t):
    return _ap(t, 0, [[8, 128], [1, 8], [0, 64]])


def _r3(t):
    return t[:, :].rearrange("p (g d) -> p g d", g=8)


def _build_tables(lvl, no_lr):
    vaN, saN = {}, {}
    va, sa = [0], [0]

    def V(*tag):
        va[0] += 1
        vaN[tag] = va[0]

    def A(*tag):
        sa[0] += 1
        saN[tag] = sa[0]

    if lvl >= 2:
        A("c0")
    for s in range(NSLICE):
        for u in range(4):
            k = 4 * s + u
            A("q", k)
            A("t", k)
        if lvl >= 1 and not no_lr:
            for u in range(4):
                A("g", 4 * s + u)
        if lvl >= 2 and s >= 1:
            for i in range(8 * (s - 1), 8 * s):
                for n in range(1, 7):
                    A("s", i, n)
    if lvl >= 2:
        for i in range(56, 64):
            for n in range(1, 7):
                A("s", i, n)
    for s in range(NSLICE):
        for u in range(4):
            V("r", 4 * s + u)
        if lvl >= 1 and not no_lr:
            for u in range(4):
                V("l", 4 * s + u)
            for u in range(4):
                V("e", 4 * s + u)
        if lvl >= 2 and s >= 1:
            for i in range(8 * (s - 1), 8 * s):
                for n in range(1, 9):
                    V("c", i, n)
    if lvl >= 2:
        for i in range(56, 64):
            for n in range(1, 9):
                V("c", i, n)
    return vaN, saN


def PMM(s, j):
    return 28 * s + j + 1


def PMM_LR(s, u):
    return 28 * s + 24 + u + 1


PH = {"ln": 0, "a1": 1, "a2": 2, "b1": 3, "full": 4}


def build_nc(upto="full", dbg=False, no_lr=False, no_dyn=False):
    lvl = PH[upto]
    vaN, saN = _build_tables(lvl, no_lr)
    nc = bass.Bass("TRN2", target_bir_lowering=False, detect_race_conditions=False)
    scratch_kind = "ExternalOutput" if dbg else "Internal"

    x_seq = nc.dram_tensor("x_seq", [L, DM], F32, kind="ExternalInput")
    dynr = nc.dram_tensor("dynr", [NSLAB + 16, 1], U32, kind="ExternalInput")
    wq_d = nc.dram_tensor("wq", [DM, DM], F32, kind="ExternalInput")
    wk_d = nc.dram_tensor("wk", [DM, DM], F32, kind="ExternalInput")
    wv_d = nc.dram_tensor("wv", [DM, DM], F32, kind="ExternalInput")
    wo_d = nc.dram_tensor("wo", [DM, DM], F32, kind="ExternalInput")
    thetab_d = nc.dram_tensor("thetab", [DM, H], F32, kind="ExternalInput")
    tbmat_d = nc.dram_tensor("tbmat", [128, H], F32, kind="ExternalInput")
    invk_d = nc.dram_tensor("invk", [128, 1], F32, kind="ExternalInput")
    alpha2_d = nc.dram_tensor("alpha2", [128, 1], F32, kind="ExternalInput")
    gpar_d = nc.dram_tensor("gpar", [128, 512], F32, kind="ExternalInput")
    bepar_d = nc.dram_tensor("bepar", [128, 512], F32, kind="ExternalInput")
    gtm_d = nc.dram_tensor("gtm", [128, DM], F32, kind="ExternalInput")
    betm_d = nc.dram_tensor("betm", [128, DM], F32, kind="ExternalInput")
    w0par_d = nc.dram_tensor("w0par", [128, 512], F32, kind="ExternalInput")
    b0par_d = nc.dram_tensor("b0par", [128, 512], F32, kind="ExternalInput")
    wgT_d = nc.dram_tensor("wgT", [DM, HIDDEN], I32, kind="ExternalInput")
    wuT_d = nc.dram_tensor("wuT", [DM, HIDDEN], I32, kind="ExternalInput")
    wdT_d = nc.dram_tensor("wdT", [HIDDEN, DM], I32, kind="ExternalInput")
    out_d = nc.dram_tensor("out", [TOUT, DM], F32, kind="ExternalOutput")

    xn_hbm = nc.dram_tensor("xn_hbm", [L, DM], BF16, kind=scratch_kind)
    xn2_hbm = nc.dram_tensor("xn2_hbm", [2 * L, DM], BF16, kind=scratch_kind)
    kfm_hbm = nc.dram_tensor("kfm_hbm", [DM, L], BF16, kind=scratch_kind)
    qfm_hbm = nc.dram_tensor("qfm_hbm", [DM, L], BF16, kind=scratch_kind)
    vfm_hbm = nc.dram_tensor("vfm_hbm", [DM, L], BF16, kind=scratch_kind)
    attn2_hbm = nc.dram_tensor("attn2_hbm", [2 * L, DM], BF16, kind=scratch_kind)
    h_hbm = nc.dram_tensor("h_hbm", [TOUT, HIDDEN], BF16, kind=scratch_kind)
    wgbf_hbm = nc.dram_tensor("wgbf_hbm", [DM, HIDDEN], BF16, kind="Internal")
    wubf_hbm = nc.dram_tensor("wubf_hbm", [DM, HIDDEN], BF16, kind="Internal")
    wdbf_hbm = nc.dram_tensor("wdbf_hbm", [HIDDEN, DM], BF16, kind="Internal")
    wobf_hbm = nc.dram_tensor("wobf_hbm", [DM, DM], BF16, kind="Internal")

    with ExitStack() as top:
        e = top.enter_context
        sem_names = [
            "cld", "wld", "x_ld", "xn_done", "xnsp", "xnsp2", "xnf_ld",
            "pmm", "evac", "ksp", "txp0", "txp1", "kvg",
            "va", "sa", "z1mm", "dwmm", "zqmm", "asp",
            "pc0", "pc1", "pc2", "pc3", "ac0", "ac1", "pre", "wdc", "bxs",
            "phb", "fwl", "fws", "bxn", "gumm", "silu", "hready",
            "hsp", "hld", "afm_ld", "xres_ld", "omm", "oev", "ost",
        ]
        S = {n: e(nc.semaphore(n)) for n in sem_names}
        dynr_sb = e(nc.sbuf_tensor("dynr_sb", [NSLAB + 16, 1], U32))

        with ExitStack() as pa:
            a = pa.enter_context
            thetab_sb = a(nc.sbuf_tensor("thetab_sb", [128, 8 * H], BF16))
            tbmat_sb = a(nc.sbuf_tensor("tbmat_sb", [128, H], F32))
            invk_sb = a(nc.sbuf_tensor("invk_sb", [128, 1], F32))
            alpha2_sb = a(nc.sbuf_tensor("alpha2_sb", [128, 1], F32))
            etab_sb = a(nc.sbuf_tensor("etab_sb", [128, 1], F32))
            gpar_sb = a(nc.sbuf_tensor("gpar_sb", [128, 512], BF16))
            bepar_sb = a(nc.sbuf_tensor("bepar_sb", [128, 512], BF16))
            g2par_sb = a(nc.sbuf_tensor("g2par_sb", [128, 512], BF16))
            gtm_sb = a(nc.sbuf_tensor("gtm_sb", [128, DM], BF16))
            betm_sb = a(nc.sbuf_tensor("betm_sb", [128, DM], BF16))
            gbetm_sb = a(nc.sbuf_tensor("gbetm_sb", [128, DM], BF16))
            wq_sb = a(nc.sbuf_tensor("wq_sb", [128, 8 * DM], BF16))
            wk_sb = a(nc.sbuf_tensor("wk_sb", [128, 8 * DM], BF16))
            wv_sb = a(nc.sbuf_tensor("wv_sb", [128, 8 * DM], BF16))
            x_slab = [a(nc.sbuf_tensor(f"x_slab{i}", [128, DM], F32)) for i in range(3)]
            sqx_sb = a(nc.sbuf_tensor("sqx_sb", [128, DM], F32))
            xnstg = a(nc.sbuf_tensor("xnstg", [128, 4 * DM], BF16))
            xnf_sb = [a(nc.sbuf_tensor(f"xnf_sb{i}", [128, 8 * 512], BF16)) for i in range(2)]
            kfm_sb = [a(nc.sbuf_tensor(f"kfm_sb{i}", [128, 8 * 512], BF16)) for i in range(2)]
            qfm_sb = [a(nc.sbuf_tensor(f"qfm_sb{i}", [128, 8 * 512], BF16)) for i in range(2)]
            vfm_sb = [a(nc.sbuf_tensor("vfm_sb0", [128, 8 * 512], BF16))] * 2
            st_s1 = a(nc.sbuf_tensor("st_s1", [128, 1], F32))
            st_s2 = a(nc.sbuf_tensor("st_s2", [128, 1], F32))
            st_m = a(nc.sbuf_tensor("st_m", [128, 1], F32))
            st_t1 = a(nc.sbuf_tensor("st_t1", [128, 1], F32))
            st_t2 = a(nc.sbuf_tensor("st_t2", [128, 1], F32))
            st_v = a(nc.sbuf_tensor("st_v", [128, 1], F32))
            st_d = a(nc.sbuf_tensor("st_d", [128, 1], F32))
            st_r = a(nc.sbuf_tensor("st_r", [128, 1], F32))
            lrall_sb = a(nc.sbuf_tensor("lrall_sb", [128, NSLAB * H], F32))
            sigall_sb = a(nc.sbuf_tensor("sigall_sb", [128, NSLAB * H], F32))
            eta_sb = a(nc.sbuf_tensor("eta_sb", [128, NSLAB * H], F32))
            txk_sb = [a(nc.sbuf_tensor(f"txk_sb{i}", [128, DM], BF16)) for i in range(2)]
            txq_sb = [a(nc.sbuf_tensor(f"txq_sb{i}", [128, DM], BF16)) for i in range(2)]
            txv_sb = [a(nc.sbuf_tensor(f"txv_sb{i}", [128, DM], BF16)) for i in range(2)]
            kvg_sb = [a(nc.sbuf_tensor(f"kvg_sb{i}", [128, DM], BF16)) for i in range(2)]
            w_par = a(nc.sbuf_tensor("w_par", [128, 512], F32))
            wbf = a(nc.sbuf_tensor("wbf", [128, 512], BF16))
            b_mat = a(nc.sbuf_tensor("b_mat", [128, 512], F32))
            kc_par = [a(nc.sbuf_tensor(f"kc_par{j}", [128, 512], BF16)) for j in range(2)]
            qc_par = [a(nc.sbuf_tensor(f"qc_par{j}", [128, 512], BF16)) for j in range(2)]
            kvg_par = [a(nc.sbuf_tensor(f"kvg_par{j}", [128, 512], BF16)) for j in range(2)]
            eta_par = [a(nc.sbuf_tensor(f"eta_par{j}", [128, 8], F32)) for j in range(2)]
            z1b_sb = a(nc.sbuf_tensor("z1b_sb", [128, 512], F32))
            sq_sb = a(nc.sbuf_tensor("sq_sb", [128, 512], F32))
            xh_sb = a(nc.sbuf_tensor("xh_sb", [128, 512], BF16))
            gxh_sb = a(nc.sbuf_tensor("gxh_sb", [128, 512], BF16))
            gx2_sb = a(nc.sbuf_tensor("gx2_sb", [128, 512], BF16))
            u_sb = a(nc.sbuf_tensor("u_sb", [128, 512], BF16))
            grad_sb = a(nc.sbuf_tensor("grad_sb", [128, 512], BF16))
            sck_sb = a(nc.sbuf_tensor("sck_sb", [128, 512], BF16))
            w1_sb = a(nc.sbuf_tensor("w1_sb", [128, 512], BF16))
            outp_sb = [a(nc.sbuf_tensor(f"outp_sb{j}", [128, 512], BF16)) for j in range(2)]
            c_s1 = a(nc.sbuf_tensor("c_s1", [128, 8], F32))
            c_s2 = a(nc.sbuf_tensor("c_s2", [128, 8], F32))
            c_m = a(nc.sbuf_tensor("c_m", [128, 8], F32))
            c_t = a(nc.sbuf_tensor("c_t", [128, 8], F32))
            c_v0 = a(nc.sbuf_tensor("c_v0", [128, 8], F32))
            c_d = a(nc.sbuf_tensor("c_d", [128, 8], F32))
            c_var = a(nc.sbuf_tensor("c_var", [128, 8], F32))
            c_std = a(nc.sbuf_tensor("c_std", [128, 8], F32))
            c_rstd = a(nc.sbuf_tensor("c_rstd", [128, 8], F32))
            c_r1 = a(nc.sbuf_tensor("c_r1", [128, 8], F32))
            c_r2 = a(nc.sbuf_tensor("c_r2", [128, 8], F32))
            c_a1 = a(nc.sbuf_tensor("c_a1", [128, 8], F32))
            c_a2 = a(nc.sbuf_tensor("c_a2", [128, 8], F32))
            c_eta2 = a(nc.sbuf_tensor("c_eta2", [128, 8], F32))
            c_eta2b = a(nc.sbuf_tensor("c_eta2b", [128, 8], BF16))
            attn_stage = a(nc.sbuf_tensor("attn_stage", [128, 4 * DM], BF16))
            pp_proj = [a(nc.psum_tensor(f"pp_proj{i}", [128, 512], F32)) for i in range(2)]
            pp_lr = a(nc.psum_tensor("pp_lr", [128, H], F32))
            pp_z1 = [a(nc.psum_tensor(f"pp_z1{i}", [128, 512], F32)) for i in range(2)]
            pp_zq = a(nc.psum_tensor("pp_zq", [128, 512], F32))
            pp_dw = a(nc.psum_tensor("pp_dw", [128, 512], F32))
            pp_db = a(nc.psum_tensor("pp_db", [128, 512], F32))

            blkA = pa.enter_context(nc.Block())

            def wsb(t):
                return (wk_sb, wq_sb, wv_sb)[t]

            def fmsb(t):
                return (kfm_sb, qfm_sb, vfm_sb)[t]

            def fmhbm(t):
                return (kfm_hbm, qfm_hbm, vfm_hbm)[t]

            @blkA.sync
            def _(sync):
                n_cld = 0
                for dst, src in ((tbmat_sb, tbmat_d), (invk_sb, invk_d),
                                 (alpha2_sb, alpha2_d), (w_par, w0par_d),
                                 (b_mat, b0par_d), (dynr_sb, dynr)):
                    sync.dma_start(out=dst[:, :], in_=src[:, :]).then_inc(S["cld"], 16)
                    n_cld += 1
                assert n_cld == 6
                sync.wait_ge(S["cld"], 16 * 6)
                if True:
                    def kqv_spill(s):
                        for t in range(3):
                            for p in range(8):
                                sync.wait_ge(S["evac"], 24 * s + t * 8 + p + 1)
                                sync.dma_start(
                                    out=fmhbm(t)[128 * p:128 * (p + 1), 512 * s:512 * (s + 1)],
                                    in_=fmsb(t)[s % 2][:, 512 * p:512 * (p + 1)],
                                ).then_inc(S["ksp"], 16)
                    def tx_load(m):
                        txs = S["txp%d" % (m % 2)]
                        sync.wait_ge(S["ksp"], 16 * 24 * (m // 4 + 1))
                        if m >= 2:
                            sync.wait_ge(S["kvg"], m - 1)
                            j = 2 * m - 3
                            sync.wait_ge(S["pc%d" % (j % 4)],
                                         16 * (6 * (j // 4) + 6))
                        sync.dma_start(out=txk_sb[m % 2][:, :],
                                       in_=kfm_hbm[:, 128 * m:128 * (m + 1)],
                                       transpose=True).then_inc(txs, 16)
                        sync.dma_start(out=txv_sb[m % 2][:, :],
                                       in_=vfm_hbm[:, 128 * m:128 * (m + 1)],
                                       transpose=True).then_inc(txs, 16)
                        sync.dma_start(out=txq_sb[m % 2][:, :],
                                       in_=qfm_hbm[:, 128 * m:128 * (m + 1)],
                                       transpose=True).then_inc(txs, 16)

                    def par_copies(i):
                        m, u = i // 2, i % 2
                        pcs = S["pc%d" % (i % 4)]
                        txs = S["txp%d" % (m % 2)]
                        if i >= 2:
                            sync.wait_ge(S["va"], vaN[("c", i - 2, 3)])
                        sync.wait_ge(S["kvg"], m + 1)
                        for q in range(2):
                            sync.dma_start(
                                out=_ap(kvg_par[i % 2], (64 * q) * 512,
                                        [[512, 64], [64, 8], [1, 64]]),
                                in_=_ap(kvg_sb[m % 2], (64 * u) * DM + 64 * q,
                                        [[DM, 64], [128, 8], [1, 64]]),
                            ).then_inc(pcs, 16)
                        sync.wait_ge(txs, 16 * (3 * (m // 2) + 1))
                        for q in range(2):
                            sync.dma_start(
                                out=_ap(kc_par[i % 2], (64 * q) * 512,
                                        [[512, 64], [64, 8], [1, 64]]),
                                in_=_ap(txk_sb[m % 2], (64 * u) * DM + 64 * q,
                                        [[DM, 64], [128, 8], [1, 64]]),
                            ).then_inc(pcs, 16)
                        if i >= 2:
                            sync.wait_ge(S["va"], vaN[("c", i - 2, 8)])
                        sync.wait_ge(txs, 16 * (3 * (m // 2) + 3))
                        for q in range(2):
                            sync.dma_start(
                                out=_ap(qc_par[i % 2], (64 * q) * 512,
                                        [[512, 64], [64, 8], [1, 64]]),
                                in_=_ap(txq_sb[m % 2], (64 * u) * DM + 64 * q,
                                        [[DM, 64], [128, 8], [1, 64]]),
                            ).then_inc(pcs, 16)

                    def attn_copy(i):
                        m, u = i // 2, i % 2
                        sync.wait_ge(S["va"], vaN[("c", i, 8)])
                        if m >= 4 and u == 0:
                            sync.wait_ge(S["asp"], 16 * (m // 4))
                        for q in range(2):
                            sync.dma_start(
                                out=_ap(attn_stage, (m % 4) * DM
                                        + (64 * u) * (4 * DM) + 64 * q,
                                        [[4 * DM, 64], [128, 8], [1, 64]]),
                                in_=_ap(outp_sb[i % 2], (64 * q) * 512,
                                        [[512, 64], [64, 8], [1, 64]]),
                            ).then_inc(S["ac%d" % (i % 2)], 16)
                        if i % 8 == 7:
                            g = i // 8
                            sync.wait_ge(S["ac0"], 16 * 2 * (4 * g + 4))
                            sync.wait_ge(S["ac1"], 16 * 2 * (4 * g + 4))
                            v = sync.value_load(dynr_sb[g:g + 1, 0:1])
                            sync.dma_start(
                                out=attn2_hbm[ds(v, 512), :].rearrange(
                                    "(u t) d -> t u d", t=128),
                                in_=_ap(attn_stage, 0,
                                        [[4 * DM, 128], [DM, 4], [1, DM]]),
                            ).then_inc(S["asp"], 16)

                    def batch_sync(y):
                        tx_load(4 * y)
                        tx_load(4 * y + 1)
                        for i in range(8 * y, 8 * y + 8):
                            if i == 8 * y + 2:
                                tx_load(4 * y + 2)
                            if i == 8 * y + 4:
                                tx_load(4 * y + 3)
                            par_copies(i)
                            if i >= 2:
                                attn_copy(i - 2)

                    for k in range(3):
                        sync.dma_start(out=x_slab[k % 3][:, :],
                                       in_=x_seq[128 * k:128 * (k + 1), :],
                                       ).then_inc(S["x_ld"], 16)
                    for s in range(NSLICE):
                        for u in range(4):
                            k = 4 * s + u
                            sync.wait_ge(S["xn_done"], k + 1)
                            if k + 3 < NSLAB:
                                sync.dma_start(out=x_slab[(k + 3) % 3][:, :],
                                               in_=x_seq[128 * (k + 3):128 * (k + 4), :],
                                               ).then_inc(S["x_ld"], 16)
                            sync.dma_start(out=xn_hbm[128 * k:128 * (k + 1), :],
                                           in_=xnstg[:, DM * (k % 4):DM * (k % 4 + 1)],
                                           ).then_inc(S["xnsp"], 16)
                        if lvl >= 1 and s >= 1:
                            kqv_spill(s - 1)
                        if lvl >= 2 and s >= 1:
                            batch_sync(s - 1)
                    if lvl >= 1:
                        kqv_spill(NSLICE - 1)
                    if lvl >= 2:
                        batch_sync(NSLICE - 1)
                        attn_copy(NCH - 2)
                        attn_copy(NCH - 1)
                        sync.wait_ge(S["asp"], 16 * 8)
                    sync.wait_ge(S["xnsp"], 16 * NSLAB)
                    if not no_dyn:
                        sync.wait_ge(S["xnsp2"], 16 * 8)
                    if lvl >= 1:
                        sync.wait_ge(S["ksp"], 16 * 24 * NSLICE)

            @blkA.gpsimd
            def _(gpsimd):
                n = 0
                for dst, src in ((wk_sb, wk_d), (wq_sb, wq_d), (wv_sb, wv_d)):
                    for c in range(8):
                        gpsimd.dma_start(out=dst[:, DM * c:DM * (c + 1)],
                                         in_=src[128 * c:128 * (c + 1), :]).then_inc(S["wld"], 16)
                        n += 1
                for c in range(8):
                    gpsimd.dma_start(out=thetab_sb[:, H * c:H * (c + 1)],
                                     in_=thetab_d[128 * c:128 * (c + 1), :]).then_inc(S["wld"], 16)
                    n += 1
                for dst, src in ((gpar_sb, gpar_d), (bepar_sb, bepar_d),
                                 (gtm_sb, gtm_d), (betm_sb, betm_d)):
                    gpsimd.dma_start(out=dst[:, :], in_=src[:, :]).then_inc(S["wld"], 16)
                    n += 1
                assert n == 36
                gpsimd.wait_ge(S["cld"], 16 * 6)
                if lvl >= 2:
                    gpsimd.wait_ge(S["pre"], 1)
                _casts = {1: (wgbf_hbm, wgT_d), 2: (wubf_hbm, wuT_d),
                          3: (wdbf_hbm, wdT_d), 4: (wobf_hbm, wo_d)}

                def kvg_batch(y):
                    for m in range(4 * y, 4 * y + 4):
                        if m >= 2:
                            for j in (2 * m - 4, 2 * m - 3):
                                gpsimd.wait_ge(S["pc%d" % (j % 4)],
                                               16 * (6 * (j // 4) + 2))
                        gpsimd.wait_ge(S["txp%d" % (m % 2)],
                                       16 * (3 * (m // 2) + 2))
                        gpsimd.tensor_sub(kvg_sb[m % 2][:, :], txk_sb[m % 2][:, :],
                                          txv_sb[m % 2][:, :])
                        gpsimd.tensor_mul(kvg_sb[m % 2][:, :], kvg_sb[m % 2][:, :],
                                          gtm_sb[:, :])
                        gpsimd.tensor_add(kvg_sb[m % 2][:, :], kvg_sb[m % 2][:, :],
                                          gbetm_sb[:, :]).then_inc(S["kvg"], 1)

                for s in range(NSLICE):
                    if lvl >= 3 and s in _casts:
                        dst, src = _casts[s]
                        gpsimd.dma_start(out=dst[:, :],
                                         in_=src[:, :]).then_inc(S["wdc"], 16)
                    if lvl >= 2 and s >= 1:
                        kvg_batch(s - 1)
                    if not no_dyn:
                        gpsimd.wait_ge(S["xn_done"], 4 * s + 4)
                        v = gpsimd.value_load(dynr_sb[s:s + 1, 0:1])
                        src = _ap(xnstg, 0, [[4 * DM, 128], [DM, 4], [1, DM]])
                        gpsimd.dma_start(
                            out=xn2_hbm[ds(v, 512), :].rearrange("(u t) d -> t u d", t=128),
                            in_=src).then_inc(S["xnsp2"], 16)
                if lvl >= 2:
                    kvg_batch(NSLICE - 1)

            WLD_ALL = 16 * 36

            def batch_mms_impl(tensor, y):
                for i in range(8 * y, 8 * y + 8):
                    m, u = i // 2, i % 2
                    sl = m // 4
                    off = (i * 64) % 512
                    tensor.wait_ge(S["sa"], saN[("c0",)] if i == 0
                                   else saN[("s", i - 1, 4)])
                    if i >= 2:
                        tensor.wait_ge(S["va"], vaN[("c", i - 2, 1)])
                    for p in range(8):
                        for q in range(2):
                            mm = tensor.matmul(
                                pp_z1[i % 2][64 * q:64 * (q + 1), 64 * p:64 * (p + 1)],
                                kfm_sb[sl % 2][64 * q:64 * (q + 1),
                                               512 * p + off:512 * p + off + 64],
                                wbf[64 * q:64 * (q + 1), 64 * p:64 * (p + 1)],
                                start=True, stop=True, tile_position=(64 * q, 64 * q))
                            if p == 7 and q == 1:
                                mm.then_inc(S["z1mm"], 1)
                    tensor.wait_ge(S["va"], vaN[("c", i, 3)])
                    if i >= 1:
                        tensor.wait_ge(S["va"], vaN[("c", i - 1, 5)])
                    for p in range(8):
                        for q in range(2):
                            mm = tensor.matmul(
                                pp_dw[64 * q:64 * (q + 1), 64 * p:64 * (p + 1)],
                                sck_sb[64 * q:64 * (q + 1), 64 * p:64 * (p + 1)],
                                grad_sb[64 * q:64 * (q + 1), 64 * p:64 * (p + 1)],
                                start=True, stop=True, tile_position=(64 * q, 64 * q))
                            if p == 7 and q == 1:
                                mm.then_inc(S["dwmm"], 1)
                    for p in range(8):
                        for q in range(2):
                            lhs = _ap(c_eta2b, q * 64 * 8 + p, [[8, 64], [0, 64]])
                            mm = tensor.matmul(
                                pp_db[64 * q:64 * (q + 1), 64 * p:64 * (p + 1)],
                                lhs,
                                grad_sb[64 * q:64 * (q + 1), 64 * p:64 * (p + 1)],
                                start=True, stop=True, tile_position=(64 * q, 64 * q))
                            if p == 7 and q == 1:
                                mm.then_inc(S["dwmm"], 1)
                    tensor.wait_ge(S["sa"], saN[("s", i, 4)])
                    if i >= 1:
                        tensor.wait_ge(S["va"], vaN[("c", i - 1, 6)])
                    for p in range(8):
                        for q in range(2):
                            mm = tensor.matmul(
                                pp_zq[64 * q:64 * (q + 1), 64 * p:64 * (p + 1)],
                                qfm_sb[sl % 2][64 * q:64 * (q + 1),
                                               512 * p + off:512 * p + off + 64],
                                wbf[64 * q:64 * (q + 1), 64 * p:64 * (p + 1)],
                                start=True, stop=True, tile_position=(64 * q, 64 * q))
                            if p == 7 and q == 1:
                                mm.then_inc(S["zqmm"], 1)

            @blkA.tensor
            def _(tensor):
                def batch_mms(y):
                    tensor.wait_ge(S["evac"], 24 * (y + 1))
                    batch_mms_impl(tensor, y)

                tensor.wait_ge(S["wld"], WLD_ALL)
                for s in range(NSLICE if lvl >= 1 else 0):
                    tensor.wait_ge(S["xnf_ld"], 16 * 8 * (s + 1))
                    for t in range(3):
                        for p in range(8):
                            g = 24 * s + t * 8 + p
                            if g >= 2:
                                tensor.wait_ge(S["evac"], g - 1)
                            for c in range(8):
                                mm = tensor.matmul(
                                    pp_proj[g % 2][:, :],
                                    wsb(t)[:, DM * c + 128 * p: DM * c + 128 * (p + 1)],
                                    xnf_sb[s % 2][:, 512 * c:512 * (c + 1)],
                                    start=(c == 0), stop=(c == 7))
                                if c == 7:
                                    mm.then_inc(S["pmm"], 1)
                    tensor.wait_ge(S["evac"], 24 * s + 8)
                    for u in range(4 if not no_lr else 0):
                        k = 4 * s + u
                        if k >= 1:
                            tensor.wait_ge(S["va"], vaN[("l", k - 1)])
                        for p in range(8):
                            mm = tensor.matmul(
                                pp_lr[:, :],
                                kfm_sb[s % 2][:, 512 * p + 128 * u: 512 * p + 128 * (u + 1)],
                                thetab_sb[:, H * p:H * (p + 1)],
                                start=(p == 0), stop=(p == 7))
                            if p == 7:
                                mm.then_inc(S["pmm"], 1)
                    if lvl >= 2 and s >= 1:
                        batch_mms(s - 1)
                if lvl >= 2:
                    batch_mms(NSLICE - 1)

            @blkA.scalar
            def _(scalar):
                def batch_sc(y):
                    for i in range(8 * y, 8 * y + 8):
                        m, u = i // 2, i % 2
                        scalar.wait_ge(S["va"], vaN[("e", m)])
                        if i >= 2:
                            scalar.wait_ge(S["va"], vaN[("c", i - 2, 3)])
                        last = None
                        for q in range(2):
                            src = _ap(eta_sb, (64 * u) * (NSLAB * H) + H * m + q,
                                      [[NSLAB * H, 64], [2, 8]])
                            dst_ap = _ap(eta_par[i % 2], (64 * q) * 8, [[8, 64], [1, 8]])
                            last = scalar.copy(dst_ap, src)
                        last.then_inc(S["sa"], 1)
                        scalar.wait_ge(S["va"], vaN[("c", i, 1)])
                        scalar.activation(sq_sb[:, :], z1b_sb[:, :],
                                          AF.Square).then_inc(S["sa"], 1)
                        scalar.wait_ge(S["va"], vaN[("c", i, 2)])
                        scalar.activation(c_std[:, :], c_var[:, :],
                                          AF.Sqrt).then_inc(S["sa"], 1)
                        scalar.wait_ge(S["va"], vaN[("c", i, 4)])
                        scalar.copy(wbf[:, :], w_par[:, :]).then_inc(S["sa"], 1)
                        scalar.wait_ge(S["va"], vaN[("c", i, 6)])
                        scalar.activation(sq_sb[:, :], z1b_sb[:, :],
                                          AF.Square).then_inc(S["sa"], 1)
                        scalar.wait_ge(S["va"], vaN[("c", i, 7)])
                        scalar.activation(c_std[:, :], c_var[:, :],
                                          AF.Sqrt).then_inc(S["sa"], 1)

                if lvl >= 2:
                    scalar.wait_ge(S["cld"], 16 * 6)
                    scalar.copy(wbf[:, :], w_par[:, :]).then_inc(S["sa"], 1)
                for s in range(NSLICE):
                    for u in range(4):
                        k = 4 * s + u
                        scalar.wait_ge(S["x_ld"], 16 * (k + 1))
                        scalar.activation(sqx_sb[:, :], x_slab[k % 3][:, :],
                                          AF.Square).then_inc(S["sa"], 1)
                        scalar.wait_ge(S["va"], vaN[("r", k)])
                        scalar.activation(st_r[:, :], st_t1[:, :],
                                          AF.Sqrt).then_inc(S["sa"], 1)
                    if lvl >= 1:
                        scalar.wait_ge(S["xnsp"], 16 * 4 * (s + 1))
                        if s >= 2:
                            scalar.wait_ge(S["pmm"], 28 * (s - 1))
                        for c in range(8):
                            scalar.dma_start(
                                out=xnf_sb[s % 2][:, 512 * c:512 * (c + 1)],
                                in_=xn_hbm[512 * s:512 * (s + 1), 128 * c:128 * (c + 1)],
                                transpose=True).then_inc(S["xnf_ld"], 16)
                        if not no_lr:
                            for u in range(4):
                                k = 4 * s + u
                                scalar.wait_ge(S["va"], vaN[("l", k)])
                                scalar.activation(sigall_sb[:, H * k:H * (k + 1)],
                                                  lrall_sb[:, H * k:H * (k + 1)],
                                                  AF.Sigmoid).then_inc(S["sa"], 1)
                    if lvl >= 2 and s >= 1:
                        batch_sc(s - 1)
                if lvl >= 2:
                    batch_sc(NSLICE - 1)

            def batch_vec_impl(vector, y):
                for i in range(8 * y, 8 * y + 8):
                    m, u = i // 2, i % 2
                    vector.wait_ge(S["z1mm"], i + 1)
                    vector.tensor_add(z1b_sb[:, :], pp_z1[i % 2][:, :],
                                      b_mat[:, :]).then_inc(S["va"], 1)
                    vector.reduce_sum(c_s1[:, :], _r3(z1b_sb), axis=AXX)
                    vector.wait_ge(S["sa"], saN[("s", i, 2)])
                    vector.reduce_sum(c_s2[:, :], _r3(sq_sb), axis=AXX)
                    vector.tensor_scalar_mul(c_m[:, :], c_s1[:, :], -1.0 / 64.0)
                    vector.tensor_scalar(c_v0[:, :], c_s2[:, :], 64.0 * EPS_SCAN,
                                         1.0 / 64.0, ALU.add, ALU.mult)
                    vector.tensor_mul(c_t[:, :], c_m[:, :], c_m[:, :])
                    vector.tensor_scalar_mul(c_d[:, :], c_s1[:, :], 1.0)
                    vector.tensor_sub(c_var[:, :], c_v0[:, :],
                                      c_t[:, :]).then_inc(S["va"], 1)
                    vector.wait_ge(S["sa"], saN[("s", i, 3)])
                    vector.reciprocal(c_rstd[:, :], c_std[:, :])
                    z3 = _r3(z1b_sb)
                    vector.tensor_tensor(z3, z3, _grp(c_m), ALU.add)
                    vector.tensor_tensor(_r3(xh_sb), z3, _grp(c_rstd), ALU.mult)
                    vector.wait_ge(S["sa"], saN[("s", i, 1)])
                    vector.wait_ge(S["pc%d" % (i % 4)], 16 * (6 * (i // 4) + 4))
                    vector.tensor_mul(gxh_sb[:, :], xh_sb[:, :], g2par_sb[:, :])
                    vector.tensor_add(gxh_sb[:, :], gxh_sb[:, :], kvg_par[i % 2][:, :])
                    vector.reduce_sum(c_r1[:, :], _r3(gxh_sb), axis=AXX)
                    vector.tensor_mul(gx2_sb[:, :], gxh_sb[:, :], xh_sb[:, :])
                    vector.reduce_sum(c_r2[:, :], _r3(gx2_sb), axis=AXX)
                    vector.tensor_scalar_mul(c_a1[:, :], c_r1[:, :], 1.0 / 64.0)
                    vector.tensor_scalar_mul(c_a2[:, :], c_r2[:, :], 1.0 / 64.0)
                    vector.tensor_mul(c_eta2[:, :], eta_par[i % 2][:, :], c_rstd[:, :])
                    vector.tensor_tensor(_r3(u_sb), _r3(xh_sb), _grp(c_a2), ALU.mult)
                    vector.tensor_copy(c_eta2b[:, :], c_eta2[:, :])
                    vector.tensor_sub(grad_sb[:, :], gxh_sb[:, :], u_sb[:, :])
                    vector.tensor_tensor(_r3(grad_sb), _r3(grad_sb), _grp(c_a1), ALU.subtract)
                    vector.tensor_tensor(_r3(sck_sb), _r3(kc_par[i % 2]),
                                         _ap(c_eta2b, 0, [[8, 128], [1, 8], [0, 64]]),
                                         ALU.mult).then_inc(S["va"], 1)
                    vector.wait_ge(S["dwmm"], 2 * i + 2)
                    vector.tensor_sub(w_par[:, :], w_par[:, :],
                                      pp_dw[:, :]).then_inc(S["va"], 1)
                    vector.tensor_sub(b_mat[:, :], b_mat[:, :],
                                      pp_db[:, :]).then_inc(S["va"], 1)
                    vector.wait_ge(S["zqmm"], i + 1)
                    vector.tensor_add(z1b_sb[:, :], pp_zq[:, :],
                                      b_mat[:, :]).then_inc(S["va"], 1)
                    vector.reduce_sum(c_s1[:, :], _r3(z1b_sb), axis=AXX)
                    vector.wait_ge(S["sa"], saN[("s", i, 5)])
                    vector.reduce_sum(c_s2[:, :], _r3(sq_sb), axis=AXX)
                    vector.tensor_scalar_mul(c_m[:, :], c_s1[:, :], -1.0 / 64.0)
                    vector.tensor_scalar(c_v0[:, :], c_s2[:, :], 64.0 * EPS_SCAN,
                                         1.0 / 64.0, ALU.add, ALU.mult)
                    vector.tensor_mul(c_t[:, :], c_m[:, :], c_m[:, :])
                    vector.tensor_scalar_mul(c_d[:, :], c_s1[:, :], 1.0)
                    vector.tensor_sub(c_var[:, :], c_v0[:, :],
                                      c_t[:, :]).then_inc(S["va"], 1)
                    vector.wait_ge(S["sa"], saN[("s", i, 6)])
                    vector.reciprocal(c_rstd[:, :], c_std[:, :])
                    z3b = _r3(z1b_sb)
                    vector.tensor_tensor(z3b, z3b, _grp(c_m), ALU.add)
                    vector.tensor_tensor(_r3(w1_sb), z3b, _grp(c_rstd), ALU.mult)
                    vector.tensor_mul(w1_sb[:, :], w1_sb[:, :], gpar_sb[:, :])
                    vector.tensor_add(w1_sb[:, :], w1_sb[:, :], bepar_sb[:, :])
                    vector.wait_ge(S["pc%d" % (i % 4)], 16 * (6 * (i // 4) + 6))
                    if i >= 2:
                        vector.wait_ge(S["ac%d" % (i % 2)], 16 * 2 * (i // 2))
                    vector.tensor_add(outp_sb[i % 2][:, :], w1_sb[:, :],
                                      qc_par[i % 2][:, :]).then_inc(S["va"], 1)

            @blkA.vector
            def _(vector):
                def batch_vec(y):
                    batch_vec_impl(vector, y)

                vector.wait_ge(S["wld"], WLD_ALL)
                vector.wait_ge(S["cld"], 16 * 6)
                vector.tensor_add(etab_sb[:, :], invk_sb[:, :], alpha2_sb[:, :])
                vector.tensor_mul(g2par_sb[:, :], gpar_sb[:, :], gpar_sb[:, :])
                vector.tensor_mul(gbetm_sb[:, :], gtm_sb[:, :],
                                  betm_sb[:, :]).then_inc(S["pre"], 1)
                vector.tensor_relu(etab_sb[:, :], etab_sb[:, :])
                for s in range(NSLICE):
                    for u in range(4):
                        k = 4 * s + u
                        vector.wait_ge(S["x_ld"], 16 * (k + 1))
                        vector.reduce_sum(st_s1[:, :], x_slab[k % 3][:, :], axis=AXX)
                        vector.wait_ge(S["sa"], saN[("q", k)])
                        vector.reduce_sum(st_s2[:, :], sqx_sb[:, :], axis=AXX)
                        vector.tensor_scalar_mul(st_m[:, :], st_s1[:, :], -1.0 / DM)
                        vector.tensor_scalar(st_v[:, :], st_s2[:, :], DM * EPS_MAIN,
                                             1.0 / DM, ALU.add, ALU.mult)
                        vector.tensor_mul(st_t2[:, :], st_m[:, :], st_m[:, :])
                        vector.tensor_scalar_mul(st_d[:, :], st_s1[:, :], 1.0)
                        vector.tensor_sub(st_v[:, :], st_v[:, :], st_t2[:, :])
                        vector.tensor_scalar_mul(st_d[:, :], st_s1[:, :], 1.0)
                        vector.reciprocal(st_t1[:, :], st_v[:, :]).then_inc(S["va"], 1)
                        vector.wait_ge(S["sa"], saN[("t", k)])
                        if k >= 4:
                            vector.wait_ge(S["xnsp"], 16 * (k - 3))
                            if not no_dyn:
                                vector.wait_ge(S["xnsp2"], 16 * (k // 4))
                        vector.tensor_scalar(xnstg[:, DM * (k % 4):DM * (k % 4 + 1)],
                                             x_slab[k % 3][:, :],
                                             st_m[:, :], st_r[:, :],
                                             ALU.add, ALU.mult).then_inc(S["xn_done"], 1)
                    if lvl >= 2 and s >= 2:
                        vector.wait_ge(S["zqmm"], 8 * (s - 1))
                    for t in range(3 if lvl >= 1 else 0):
                        for p in range(8):
                            g = 24 * s + t * 8 + p
                            j = t * 8 + p
                            vector.wait_ge(S["pmm"], PMM(s, j))
                            if s >= 2:
                                vector.wait_ge(S["ksp"], 16 * 24 * (s - 1))
                            if t == 2 and s >= 1:
                                vector.wait_ge(S["ksp"], 16 * 24 * s)
                            vector.tensor_copy(fmsb(t)[s % 2][:, 512 * p:512 * (p + 1)],
                                               pp_proj[g % 2][:, :]).then_inc(S["evac"], 1)
                    for u in range(4 if (lvl >= 1 and not no_lr) else 0):
                        k = 4 * s + u
                        vector.wait_ge(S["pmm"], PMM_LR(s, u))
                        vector.tensor_add(lrall_sb[:, H * k:H * (k + 1)], pp_lr[:, :],
                                          tbmat_sb[:, :]).then_inc(S["va"], 1)
                    for u in range(4 if (lvl >= 1 and not no_lr) else 0):
                        k = 4 * s + u
                        vector.wait_ge(S["sa"], saN[("g", k)])
                        vector.tensor_scalar(
                            eta_sb[:, H * k:H * (k + 1)],
                            sigall_sb[:, H * k:H * (k + 1)], etab_sb[:, :], 1.0 / 64.0,
                            ALU.mult, ALU.mult).then_inc(S["va"], 1)
                    if lvl >= 2 and s >= 1:
                        batch_vec(s - 1)
                if lvl >= 2:
                    batch_vec(NSLICE - 1)

        if lvl < 3:
            return nc
        with ExitStack() as pb1:
            b1 = pb1.enter_context
            wg_sb = b1(nc.sbuf_tensor("wg_sb", [128, 8 * HIDDEN], BF16))
            wu_sb = b1(nc.sbuf_tensor("wu_sb", [128, 8 * HIDDEN], BF16))
            xnfB = [b1(nc.sbuf_tensor(f"xnfB{i}", [128, 8 * 512], BF16)) for i in range(2)]
            h_sb = b1(nc.sbuf_tensor("h_sb", [128, 24 * 512], BF16))
            sg_sb = b1(nc.sbuf_tensor("sg_sb", [128, 512], F32))
            pp_gate = [b1(nc.psum_tensor(f"pp_gate{i}", [128, 512], F32)) for i in range(2)]
            pp_up = [b1(nc.psum_tensor(f"pp_up{i}", [128, 512], F32)) for i in range(2)]

            blkB1 = pb1.enter_context(nc.Block())

            @blkB1.sync
            def _(sync):
                sync.sem_inc(S["phb"], 1)
                for j in range(4):
                    for oc in range(24):
                        sync.wait_ge(S["hready"], 24 * j + oc + 1)
                        sync.dma_start(
                            out=_ap(h_hbm, (4 * j) * 128 * HIDDEN + 128 * oc,
                                    [[HIDDEN, 128], [128 * HIDDEN, 4], [1, 128]]),
                            in_=h_sb[:, 512 * oc:512 * (oc + 1)],
                        ).then_inc(S["hsp"], 16)
                if lvl == 3:
                    sync.wait_ge(S["hsp"], 16 * 24 * 4)

            @blkB1.gpsimd
            def _(gpsimd):
                gpsimd.wait_ge(S["phb"], 1)
                gpsimd.wait_ge(S["wdc"], 16)
                for c in range(8):
                    gpsimd.dma_start(out=wg_sb[:, HIDDEN * c:HIDDEN * (c + 1)],
                                     in_=wgbf_hbm[128 * c:128 * (c + 1), :],
                                     ).then_inc(S["fwl"], 16)
                gpsimd.wait_ge(S["wdc"], 32)
                for c in range(8):
                    gpsimd.dma_start(out=wu_sb[:, HIDDEN * c:HIDDEN * (c + 1)],
                                     in_=wubf_hbm[128 * c:128 * (c + 1), :],
                                     ).then_inc(S["fwl"], 16)

            @blkB1.scalar
            def _(scalar):
                scalar.wait_ge(S["phb"], 1)
                for j in range(2):
                    for c in range(8):
                        scalar.dma_start(
                            out=xnfB[j % 2][:, 512 * c:512 * (c + 1)],
                            in_=xn2_hbm[L + 512 * j:L + 512 * (j + 1),
                                        128 * c:128 * (c + 1)],
                            transpose=True).then_inc(S["bxn"], 16)
                for j in range(4):
                    for oc in range(24):
                        g = 24 * j + oc
                        scalar.wait_ge(S["gumm"], 2 * g + 1)
                        if g >= 1:
                            scalar.wait_ge(S["hready"], g)
                        scalar.activation(sg_sb[:, :], pp_gate[g % 2][:, :],
                                          AF.Silu).then_inc(S["silu"], 1)
                    jj = j + 2
                    if jj <= 3:
                        scalar.wait_ge(S["gumm"], 48 * (jj - 1))
                        for c in range(8):
                            scalar.dma_start(
                                out=xnfB[jj % 2][:, 512 * c:512 * (c + 1)],
                                in_=xn2_hbm[L + 512 * jj:L + 512 * (jj + 1),
                                            128 * c:128 * (c + 1)],
                                transpose=True).then_inc(S["bxn"], 16)

            @blkB1.tensor
            def _(tensor):
                tensor.wait_ge(S["fwl"], 16 * 8)
                for j in range(4):
                    tensor.wait_ge(S["bxs"], j + 1)
                    for oc in range(24):
                        g = 24 * j + oc
                        if g >= 2:
                            tensor.wait_ge(S["silu"], g - 1)
                            tensor.wait_ge(S["hready"], g - 1)
                        for c in range(8):
                            mm = tensor.matmul(
                                pp_gate[g % 2][:, :],
                                wg_sb[:, HIDDEN * c + 128 * oc: HIDDEN * c + 128 * (oc + 1)],
                                xnfB[j % 2][:, 512 * c:512 * (c + 1)],
                                start=(c == 0), stop=(c == 7))
                            if c == 7:
                                mm.then_inc(S["gumm"], 1)
                        if g == 0:
                            tensor.wait_ge(S["fws"], 1)
                        for c in range(8):
                            mm = tensor.matmul(
                                pp_up[g % 2][:, :],
                                wu_sb[:, HIDDEN * c + 128 * oc: HIDDEN * c + 128 * (oc + 1)],
                                xnfB[j % 2][:, 512 * c:512 * (c + 1)],
                                start=(c == 0), stop=(c == 7))
                            if c == 7:
                                mm.then_inc(S["gumm"], 1)

            @blkB1.vector
            def _(vector):
                vector.wait_ge(S["phb"], 1)
                for j in range(4):
                    vector.wait_ge(S["bxn"], 16 * 8 * (j + 1))
                    vector.tensor_scalar_mul(xnfB[j % 2][:, :], xnfB[j % 2][:, :],
                                             1.0 / 127.0).then_inc(S["bxs"], 1)
                    if j == 0:
                        vector.wait_ge(S["fwl"], 16 * 16)
                        vector.tensor_scalar_mul(wu_sb[:, :], wu_sb[:, :],
                                                 1.0 / 127.0).then_inc(S["fws"], 1)
                    vector.wait_ge(S["hsp"], 16 * 24 * j)
                    for oc in range(24):
                        g = 24 * j + oc
                        vector.wait_ge(S["silu"], g + 1)
                        vector.wait_ge(S["gumm"], 2 * g + 2)
                        vector.tensor_mul(h_sb[:, 512 * oc:512 * (oc + 1)],
                                          sg_sb[:, :],
                                          pp_up[g % 2][:, :]).then_inc(S["hready"], 1)

        if lvl < 4:
            return nc
        with ExitStack() as pb2:
            b2 = pb2.enter_context
            wd_sb = b2(nc.sbuf_tensor("wd_sb", [128, 24 * DM], BF16))
            wo_sb = b2(nc.sbuf_tensor("wo_sb", [128, 8 * DM], BF16))
            afm_sb = b2(nc.sbuf_tensor("afm_sb", [128, 8 * TOUT], BF16))
            hsub_sb = [b2(nc.sbuf_tensor(f"hsub{i}", [128, 24 * 128], BF16)) for i in range(2)]
            xres_sb = [b2(nc.sbuf_tensor(f"xres{i}", [128, 4 * DM], F32)) for i in range(2)]
            out_sb = [b2(nc.sbuf_tensor(f"outsb{i}", [128, DM], F32)) for i in range(2)]
            pp_out = [b2(nc.psum_tensor(f"pp_out{i}", [128, DM], F32)) for i in range(3)]

            blkB2 = pb2.enter_context(nc.Block())

            B2_GATE = (16 * 24 * 4)

            @blkB2.sync
            def _(sync):
                if True:
                    for r in range(4):
                        sync.wait_ge(S["gumm"], 192)
                        sync.wait_ge(S["asp"], 16 * 8)
                        sync.dma_start(
                            out=afm_sb[:, TOUT * (4 + r):TOUT * (5 + r)],
                            in_=attn2_hbm[L:L + TOUT, 128 * (4 + r):128 * (5 + r)],
                            transpose=True).then_inc(S["afm_ld"], 16)
                    for r in range(16):
                        sync.wait_ge(S["hsp"], 16 * 24 * (r // 4 + 1))
                        if r >= 2:
                            sync.wait_ge(S["omm"], 2 * (r - 1))
                        sync.dma_start(
                            out=hsub_sb[r % 2][:, :],
                            in_=h_hbm[128 * r:128 * (r + 1), :],
                        ).then_inc(S["hld"], 16)
                        if r % 4 == 0:
                            g = r // 4
                            if g == 0:
                                sync.wait_ge(S["hready"], 96)
                                sync.wait_ge(S["hsp"], 16 * 24 * 4)
                            if g >= 2:
                                sync.wait_ge(S["oev"], 4 * (g - 1))
                            v = sync.value_load(dynr_sb[8 + g:8 + g + 1, 0:1])
                            sync.dma_start(
                                out=xres_sb[g % 2][:, :].rearrange(
                                    "t (u d) -> t u d", u=4),
                                in_=x_seq[ds(v, 512), :].rearrange(
                                    "(u t) d -> t u d", t=128)).then_inc(S["xres_ld"], 16)
                        sync.wait_ge(S["oev"], r + 1)
                        sync.dma_start(out=out_d[128 * r:128 * (r + 1), :],
                                       in_=out_sb[r % 2][:, :]).then_inc(S["ost"], 16)
                    sync.wait_ge(S["ost"], 16 * 16)

            @blkB2.gpsimd
            def _(gpsimd):
                gpsimd.wait_ge(S["gumm"], 192)
                gpsimd.wait_ge(S["wdc"], 16 * 4)
                for c in range(24):
                    gpsimd.dma_start(out=wd_sb[:, DM * c:DM * (c + 1)],
                                     in_=wdbf_hbm[128 * c:128 * (c + 1), :],
                                     ).then_inc(S["fwl"], 16)
                for c in range(8):
                    gpsimd.dma_start(out=wo_sb[:, DM * c:DM * (c + 1)],
                                     in_=wobf_hbm[128 * c:128 * (c + 1), :],
                                     ).then_inc(S["fwl"], 16)

            @blkB2.scalar
            def _(scalar):
                scalar.wait_ge(S["gumm"], 192)
                scalar.wait_ge(S["asp"], 16 * 8)
                for c in range(4):
                    scalar.dma_start(
                        out=afm_sb[:, TOUT * c:TOUT * (c + 1)],
                        in_=attn2_hbm[L:L + TOUT, 128 * c:128 * (c + 1)],
                        transpose=True).then_inc(S["afm_ld"], 16)

            @blkB2.tensor
            def _(tensor):
                tensor.wait_ge(S["fwl"], 16 * 48)
                for r in range(16):
                    tensor.wait_ge(S["hld"], 16 * (r + 1))
                    if r >= 3:
                        tensor.wait_ge(S["oev"], r - 2)
                    for half in range(2):
                        for oc in range(24):
                            tensor.matmul(
                                pp_out[r % 3][:, 512 * half:512 * (half + 1)],
                                hsub_sb[r % 2][:, 128 * oc:128 * (oc + 1)],
                                wd_sb[:, DM * oc + 512 * half: DM * oc + 512 * (half + 1)],
                                start=(oc == 0), stop=False)
                        if r == 0 and half == 0:
                            tensor.wait_ge(S["afm_ld"], 16 * 8)
                        for c in range(8):
                            mm = tensor.matmul(
                                pp_out[r % 3][:, 512 * half:512 * (half + 1)],
                                afm_sb[:, TOUT * c + 128 * r: TOUT * c + 128 * (r + 1)],
                                wo_sb[:, DM * c + 512 * half: DM * c + 512 * (half + 1)],
                                start=False, stop=(c == 7))
                            if c == 7:
                                mm.then_inc(S["omm"], 1)

            @blkB2.vector
            def _(vector):
                for r in range(16):
                    vector.wait_ge(S["omm"], 2 * (r + 1))
                    vector.wait_ge(S["xres_ld"], 16 * (r // 4 + 1))
                    if r >= 2:
                        vector.wait_ge(S["ost"], 16 * (r - 1))
                    vector.tensor_add(out_sb[r % 2][:, :], pp_out[r % 3][:, :],
                                      xres_sb[(r // 4) % 2][:, DM * (r % 4):DM * (r % 4 + 1)],
                                      ).then_inc(S["oev"], 1)
    return nc


_CACHE = {}


def _stage(inputs):
    x = np.ascontiguousarray(np.asarray(inputs["L0_attn_hidden_states"], np.float32))
    theta = np.asarray(inputs["L0_attn_theta"], np.float32)
    thetab = np.zeros((DM, H), np.float32)
    for h in range(H):
        thetab[64 * h:64 * h + 64, h] = theta[h, :, 0]
    tbmat = np.broadcast_to(
        np.asarray(inputs["L0_attn_theta_bias"], np.float32).reshape(1, H), (128, H))
    invk = np.tile(1.0 / np.arange(1, 65, dtype=np.float32), 2)[:, None]
    alpha2 = np.tile(np.asarray(inputs["L0_attn_alpha"], np.float32), 2)[:, None]
    gamma = np.asarray(inputs["L0_attn_gamma"], np.float32)[:, 0, :]
    beta = np.asarray(inputs["L0_attn_beta"], np.float32)[:, 0, :]
    W1 = np.asarray(inputs["L0_attn_W1"], np.float32)
    b1 = np.asarray(inputs["L0_attn_b1"], np.float32)[:, 0, :]

    def to_par(v):
        out = np.zeros((128, 512), np.float32)
        for p in range(8):
            for q in range(2):
                out[q * 64:q * 64 + 64, p * 64:p * 64 + 64] = v[2 * p + q][None, :]
        return out
    gpar, bepar, b0par = to_par(gamma), to_par(beta), to_par(b1)
    w0par = np.zeros((128, 512), np.float32)
    for p in range(8):
        for q in range(2):
            w0par[q * 64:q * 64 + 64, p * 64:p * 64 + 64] = W1[2 * p + q]
    gtm = np.zeros((1, DM), np.float32)
    betm = np.zeros((1, DM), np.float32)
    for j in range(H):
        gtm[0, 64 * j:64 * j + 64] = gamma[j]
        betm[0, 64 * j:64 * j + 64] = beta[j]
    shared = dict(
        wq=np.ascontiguousarray(np.asarray(inputs["L0_attn_wq"], np.float32)),
        wk=np.ascontiguousarray(np.asarray(inputs["L0_attn_wk"], np.float32)),
        wv=np.ascontiguousarray(np.asarray(inputs["L0_attn_wv"], np.float32)),
        wo=np.ascontiguousarray(np.asarray(inputs["L0_attn_wo"], np.float32)),
        thetab=thetab,
        tbmat=np.ascontiguousarray(tbmat).astype(np.float32),
        invk=invk, alpha2=np.ascontiguousarray(alpha2),
        gpar=gpar, bepar=bepar,
        gtm=np.ascontiguousarray(np.broadcast_to(gtm, (128, DM))),
        betm=np.ascontiguousarray(np.broadcast_to(betm, (128, DM))),
        w0par=w0par, b0par=b0par,
        wgT=np.ascontiguousarray(np.asarray(inputs["L0_ffn_w_gate"], np.int32).T),
        wuT=np.ascontiguousarray(np.asarray(inputs["L0_ffn_w_up"], np.int32).T),
        wdT=np.ascontiguousarray(np.asarray(inputs["L0_ffn_w_down"], np.int32).T),
    )
    in_maps = []
    for core in range(8):
        b, half = core // 2, core % 2
        row_off = half * TOUT
        dynr_np = np.zeros((NSLAB + 16, 1), np.uint32)
        for g in range(8):
            dynr_np[g, 0] = (L - row_off) + 512 * g
        for g in range(4):
            dynr_np[8 + g, 0] = row_off + 512 * g
        m = dict(shared)
        m["x_seq"] = np.ascontiguousarray(x[b])
        m["dynr"] = dynr_np
        in_maps.append(m)
    return in_maps


def kernel(**inputs):
    if "nc" not in _CACHE:
        _CACHE["nc"] = build_nc()
    nc = _CACHE["nc"]
    in_maps = _stage(inputs)
    res = run_bass_kernel_spmd(nc, in_maps, core_ids=list(range(8)))
    out = np.zeros((4, L, DM), np.float32)
    for core in range(8):
        b, half = core // 2, core % 2
        out[b, half * TOUT:(half + 1) * TOUT] = res.results[core]["out"]
    return out


# version 14 (final)
# speedup vs baseline: 1.2521x; 1.0183x over previous
import numpy as np
import ml_dtypes
from contextlib import ExitStack

import concourse.bass as bass
import concourse.mybir as mybir
from concourse.bass_utils import run_bass_kernel_spmd

F32 = mybir.dt.float32
BF16 = mybir.dt.bfloat16
I32 = mybir.dt.int32
U32 = mybir.dt.uint32
AF = mybir.ActivationFunctionType
ALU = mybir.AluOpType
AXX = mybir.AxisListType.X
ds = bass.ds
AP = bass.AP

L, DM, H, HD = 4096, 1024, 16, 64
NSLAB, NSLICE, NCH = 32, 8, 64
HIDDEN = 3072
TOUT = 2048
EPS_MAIN, EPS_SCAN = 1e-5, 1e-6


def _ap(t, offset, dims):
    tensor = t.tensor if hasattr(t, "tensor") else t
    return AP(tensor, offset, [list(d) for d in dims])


def _grp(t):
    return _ap(t, 0, [[8, 128], [1, 8], [0, 64]])


def _r3(t):
    return t[:, :].rearrange("p (g d) -> p g d", g=8)


def _build_tables(lvl, no_lr):
    vaN, saN = {}, {}
    va, sa = [0], [0]

    def V(*tag):
        va[0] += 1
        vaN[tag] = va[0]

    def A(*tag):
        sa[0] += 1
        saN[tag] = sa[0]

    if lvl >= 2:
        A("c0")
    for s in range(NSLICE):
        for u in range(4):
            k = 4 * s + u
            A("q", k)
            A("t", k)
        if lvl >= 1 and not no_lr:
            for u in range(4):
                A("g", 4 * s + u)
        if lvl >= 2 and s >= 1:
            for i in range(8 * (s - 1), 8 * s):
                for n in range(1, 7):
                    A("s", i, n)
    if lvl >= 2:
        for i in range(56, 64):
            for n in range(1, 7):
                A("s", i, n)
    for s in range(NSLICE):
        for u in range(4):
            V("r", 4 * s + u)
        if lvl >= 1 and not no_lr:
            for u in range(4):
                V("l", 4 * s + u)
            for u in range(4):
                V("e", 4 * s + u)
        if lvl >= 2 and s >= 1:
            for i in range(8 * (s - 1), 8 * s):
                for n in range(1, 9):
                    V("c", i, n)
    if lvl >= 2:
        for i in range(56, 64):
            for n in range(1, 9):
                V("c", i, n)
    return vaN, saN


def PMM(s, j):
    return 28 * s + j + 1


def PMM_LR(s, u):
    return 28 * s + 24 + u + 1


PH = {"ln": 0, "a1": 1, "a2": 2, "b1": 3, "full": 4}


def build_nc(upto="full", dbg=False, no_lr=False, no_dyn=False):
    lvl = PH[upto]
    vaN, saN = _build_tables(lvl, no_lr)
    nc = bass.Bass("TRN2", target_bir_lowering=False, detect_race_conditions=False)
    scratch_kind = "ExternalOutput" if dbg else "Internal"

    x_seq = nc.dram_tensor("x_seq", [L, DM], F32, kind="ExternalInput")
    dynr = nc.dram_tensor("dynr", [NSLAB + 16, 1], U32, kind="ExternalInput")
    wq_d = nc.dram_tensor("wq", [DM, DM], F32, kind="ExternalInput")
    wk_d = nc.dram_tensor("wk", [DM, DM], F32, kind="ExternalInput")
    wv_d = nc.dram_tensor("wv", [DM, DM], F32, kind="ExternalInput")
    wo_d = nc.dram_tensor("wo", [DM, DM], F32, kind="ExternalInput")
    thetab_d = nc.dram_tensor("thetab", [DM, H], F32, kind="ExternalInput")
    tbmat_d = nc.dram_tensor("tbmat", [128, H], F32, kind="ExternalInput")
    invk_d = nc.dram_tensor("invk", [128, 1], F32, kind="ExternalInput")
    alpha2_d = nc.dram_tensor("alpha2", [128, 1], F32, kind="ExternalInput")
    gpar_d = nc.dram_tensor("gpar", [128, 512], F32, kind="ExternalInput")
    bepar_d = nc.dram_tensor("bepar", [128, 512], F32, kind="ExternalInput")
    gtm_d = nc.dram_tensor("gtm", [128, DM], F32, kind="ExternalInput")
    betm_d = nc.dram_tensor("betm", [128, DM], F32, kind="ExternalInput")
    w0par_d = nc.dram_tensor("w0par", [128, 512], F32, kind="ExternalInput")
    b0par_d = nc.dram_tensor("b0par", [128, 512], F32, kind="ExternalInput")
    wgT_d = nc.dram_tensor("wgT", [DM, HIDDEN], I32, kind="ExternalInput")
    wuT_d = nc.dram_tensor("wuT", [DM, HIDDEN], I32, kind="ExternalInput")
    wdT_d = nc.dram_tensor("wdT", [HIDDEN, DM], I32, kind="ExternalInput")
    out_d = nc.dram_tensor("out", [TOUT, DM], F32, kind="ExternalOutput")

    xn_hbm = nc.dram_tensor("xn_hbm", [L, DM], BF16, kind=scratch_kind)
    xn2_hbm = nc.dram_tensor("xn2_hbm", [2 * L, DM], BF16, kind=scratch_kind)
    kfm_hbm = nc.dram_tensor("kfm_hbm", [DM, L], BF16, kind=scratch_kind)
    qfm_hbm = nc.dram_tensor("qfm_hbm", [DM, L], BF16, kind=scratch_kind)
    vfm_hbm = nc.dram_tensor("vfm_hbm", [DM, L], BF16, kind=scratch_kind)
    attn2_hbm = nc.dram_tensor("attn2_hbm", [2 * L, DM], BF16, kind=scratch_kind)
    h_hbm = nc.dram_tensor("h_hbm", [TOUT, HIDDEN], BF16, kind=scratch_kind)
    wgbf_hbm = nc.dram_tensor("wgbf_hbm", [DM, HIDDEN], BF16, kind="Internal")
    wubf_hbm = nc.dram_tensor("wubf_hbm", [DM, HIDDEN], BF16, kind="Internal")
    wdbf_hbm = nc.dram_tensor("wdbf_hbm", [HIDDEN, DM], BF16, kind="Internal")
    wobf_hbm = nc.dram_tensor("wobf_hbm", [DM, DM], BF16, kind="Internal")

    with ExitStack() as top:
        e = top.enter_context
        sem_names = [
            "cld", "wld", "x_ld", "xn_done", "xnsp", "xnsp2", "xnf_ld",
            "pmm", "evac", "ksp", "txp0", "txp1", "kvg",
            "va", "sa", "z1mm", "dwmm", "zqmm", "asp",
            "pc0", "pc1", "pc2", "pc3", "ac0", "ac1", "pre", "wdc", "bxs",
            "phb", "fwl", "fws", "bxn", "gumm", "silu", "hready",
            "hsp", "hld", "afm_ld", "xres_ld", "omm", "oev", "ost",
        ]
        S = {n: e(nc.semaphore(n)) for n in sem_names}
        dynr_sb = e(nc.sbuf_tensor("dynr_sb", [NSLAB + 16, 1], U32))

        with ExitStack() as pa:
            a = pa.enter_context
            thetab_sb = a(nc.sbuf_tensor("thetab_sb", [128, 8 * H], BF16))
            tbmat_sb = a(nc.sbuf_tensor("tbmat_sb", [128, H], F32))
            invk_sb = a(nc.sbuf_tensor("invk_sb", [128, 1], F32))
            alpha2_sb = a(nc.sbuf_tensor("alpha2_sb", [128, 1], F32))
            etab_sb = a(nc.sbuf_tensor("etab_sb", [128, 1], F32))
            gpar_sb = a(nc.sbuf_tensor("gpar_sb", [128, 512], BF16))
            bepar_sb = a(nc.sbuf_tensor("bepar_sb", [128, 512], BF16))
            g2par_sb = a(nc.sbuf_tensor("g2par_sb", [128, 512], BF16))
            gtm_sb = a(nc.sbuf_tensor("gtm_sb", [128, DM], BF16))
            betm_sb = a(nc.sbuf_tensor("betm_sb", [128, DM], BF16))
            gbetm_sb = a(nc.sbuf_tensor("gbetm_sb", [128, DM], BF16))
            wq_sb = a(nc.sbuf_tensor("wq_sb", [128, 8 * DM], BF16))
            wk_sb = a(nc.sbuf_tensor("wk_sb", [128, 8 * DM], BF16))
            wv_sb = a(nc.sbuf_tensor("wv_sb", [128, 8 * DM], BF16))
            x_slab = [a(nc.sbuf_tensor(f"x_slab{i}", [128, DM], F32)) for i in range(3)]
            sqx_sb = a(nc.sbuf_tensor("sqx_sb", [128, DM], F32))
            xnstg = a(nc.sbuf_tensor("xnstg", [128, 4 * DM], BF16))
            xnf_sb = [a(nc.sbuf_tensor(f"xnf_sb{i}", [128, 8 * 512], BF16)) for i in range(2)]
            kfm_sb = [a(nc.sbuf_tensor(f"kfm_sb{i}", [128, 8 * 512], BF16)) for i in range(2)]
            qfm_sb = [a(nc.sbuf_tensor(f"qfm_sb{i}", [128, 8 * 512], BF16)) for i in range(2)]
            vfm_sb = [a(nc.sbuf_tensor("vfm_sb0", [128, 8 * 512], BF16))] * 2
            st_s1 = a(nc.sbuf_tensor("st_s1", [128, 1], F32))
            st_s2 = a(nc.sbuf_tensor("st_s2", [128, 1], F32))
            st_m = a(nc.sbuf_tensor("st_m", [128, 1], F32))
            st_t1 = a(nc.sbuf_tensor("st_t1", [128, 1], F32))
            st_t2 = a(nc.sbuf_tensor("st_t2", [128, 1], F32))
            st_v = a(nc.sbuf_tensor("st_v", [128, 1], F32))
            st_d = a(nc.sbuf_tensor("st_d", [128, 1], F32))
            st_r = a(nc.sbuf_tensor("st_r", [128, 1], F32))
            lrall_sb = a(nc.sbuf_tensor("lrall_sb", [128, NSLAB * H], F32))
            sigall_sb = a(nc.sbuf_tensor("sigall_sb", [128, NSLAB * H], F32))
            eta_sb = a(nc.sbuf_tensor("eta_sb", [128, NSLAB * H], F32))
            txk_sb = [a(nc.sbuf_tensor(f"txk_sb{i}", [128, DM], BF16)) for i in range(2)]
            txq_sb = [a(nc.sbuf_tensor(f"txq_sb{i}", [128, DM], BF16)) for i in range(2)]
            txv_sb = [a(nc.sbuf_tensor(f"txv_sb{i}", [128, DM], BF16)) for i in range(2)]
            kvg_sb = [a(nc.sbuf_tensor(f"kvg_sb{i}", [128, DM], BF16)) for i in range(2)]
            w_par = a(nc.sbuf_tensor("w_par", [128, 512], F32))
            wbf = a(nc.sbuf_tensor("wbf", [128, 512], BF16))
            b_mat = a(nc.sbuf_tensor("b_mat", [128, 512], F32))
            kc_par = [a(nc.sbuf_tensor(f"kc_par{j}", [128, 512], BF16)) for j in range(2)]
            qc_par = [a(nc.sbuf_tensor(f"qc_par{j}", [128, 512], BF16)) for j in range(2)]
            kvg_par = [a(nc.sbuf_tensor(f"kvg_par{j}", [128, 512], BF16)) for j in range(2)]
            eta_par = [a(nc.sbuf_tensor(f"eta_par{j}", [128, 8], F32)) for j in range(2)]
            z1b_sb = a(nc.sbuf_tensor("z1b_sb", [128, 512], F32))
            sq_sb = a(nc.sbuf_tensor("sq_sb", [128, 512], F32))
            xh_sb = a(nc.sbuf_tensor("xh_sb", [128, 512], BF16))
            gxh_sb = a(nc.sbuf_tensor("gxh_sb", [128, 512], BF16))
            gx2_sb = a(nc.sbuf_tensor("gx2_sb", [128, 512], BF16))
            u_sb = a(nc.sbuf_tensor("u_sb", [128, 512], BF16))
            grad_sb = a(nc.sbuf_tensor("grad_sb", [128, 512], BF16))
            sck_sb = a(nc.sbuf_tensor("sck_sb", [128, 512], BF16))
            w1_sb = a(nc.sbuf_tensor("w1_sb", [128, 512], BF16))
            outp_sb = [a(nc.sbuf_tensor(f"outp_sb{j}", [128, 512], BF16)) for j in range(2)]
            c_s1 = a(nc.sbuf_tensor("c_s1", [128, 8], F32))
            c_s2 = a(nc.sbuf_tensor("c_s2", [128, 8], F32))
            c_m = a(nc.sbuf_tensor("c_m", [128, 8], F32))
            c_t = a(nc.sbuf_tensor("c_t", [128, 8], F32))
            c_v0 = a(nc.sbuf_tensor("c_v0", [128, 8], F32))
            c_d = a(nc.sbuf_tensor("c_d", [128, 8], F32))
            c_var = a(nc.sbuf_tensor("c_var", [128, 8], F32))
            c_std = a(nc.sbuf_tensor("c_std", [128, 8], F32))
            c_rstd = a(nc.sbuf_tensor("c_rstd", [128, 8], F32))
            c_r1 = a(nc.sbuf_tensor("c_r1", [128, 8], F32))
            c_r2 = a(nc.sbuf_tensor("c_r2", [128, 8], F32))
            c_a1 = a(nc.sbuf_tensor("c_a1", [128, 8], F32))
            c_a2 = a(nc.sbuf_tensor("c_a2", [128, 8], F32))
            c_eta2 = a(nc.sbuf_tensor("c_eta2", [128, 8], F32))
            c_eta2b = a(nc.sbuf_tensor("c_eta2b", [128, 8], BF16))
            attn_stage = a(nc.sbuf_tensor("attn_stage", [128, 4 * DM], BF16))
            pp_proj = [a(nc.psum_tensor(f"pp_proj{i}", [128, 512], F32)) for i in range(2)]
            pp_lr = a(nc.psum_tensor("pp_lr", [128, H], F32))
            pp_z1 = [a(nc.psum_tensor(f"pp_z1{i}", [128, 512], F32)) for i in range(2)]
            pp_zq = a(nc.psum_tensor("pp_zq", [128, 512], F32))
            pp_dw = a(nc.psum_tensor("pp_dw", [128, 512], F32))
            pp_db = a(nc.psum_tensor("pp_db", [128, 512], F32))

            blkA = pa.enter_context(nc.Block())

            def wsb(t):
                return (wk_sb, wq_sb, wv_sb)[t]

            def fmsb(t):
                return (kfm_sb, qfm_sb, vfm_sb)[t]

            def fmhbm(t):
                return (kfm_hbm, qfm_hbm, vfm_hbm)[t]

            @blkA.sync
            def _(sync):
                n_cld = 0
                for dst, src in ((tbmat_sb, tbmat_d), (invk_sb, invk_d),
                                 (alpha2_sb, alpha2_d), (w_par, w0par_d),
                                 (b_mat, b0par_d), (dynr_sb, dynr)):
                    sync.dma_start(out=dst[:, :], in_=src[:, :]).then_inc(S["cld"], 16)
                    n_cld += 1
                assert n_cld == 6
                sync.wait_ge(S["cld"], 16 * 6)
                if True:
                    def kqv_spill(s):
                        for t in range(3):
                            for p in range(8):
                                sync.wait_ge(S["evac"], 24 * s + t * 8 + p + 1)
                                sync.dma_start(
                                    out=fmhbm(t)[128 * p:128 * (p + 1), 512 * s:512 * (s + 1)],
                                    in_=fmsb(t)[s % 2][:, 512 * p:512 * (p + 1)],
                                ).then_inc(S["ksp"], 16)
                    def tx_load(m):
                        txs = S["txp%d" % (m % 2)]
                        sync.wait_ge(S["ksp"], 16 * 24 * (m // 4 + 1))
                        if m >= 2:
                            sync.wait_ge(S["kvg"], m - 1)
                            j = 2 * m - 3
                            sync.wait_ge(S["pc%d" % (j % 4)],
                                         16 * (6 * (j // 4) + 6))
                        sync.dma_start(out=txk_sb[m % 2][:, :],
                                       in_=kfm_hbm[:, 128 * m:128 * (m + 1)],
                                       transpose=True).then_inc(txs, 16)
                        sync.dma_start(out=txv_sb[m % 2][:, :],
                                       in_=vfm_hbm[:, 128 * m:128 * (m + 1)],
                                       transpose=True).then_inc(txs, 16)
                        sync.dma_start(out=txq_sb[m % 2][:, :],
                                       in_=qfm_hbm[:, 128 * m:128 * (m + 1)],
                                       transpose=True).then_inc(txs, 16)

                    def par_copies(i):
                        m, u = i // 2, i % 2
                        pcs = S["pc%d" % (i % 4)]
                        txs = S["txp%d" % (m % 2)]
                        if i >= 2:
                            sync.wait_ge(S["va"], vaN[("c", i - 2, 3)])
                        sync.wait_ge(S["kvg"], m + 1)
                        for q in range(2):
                            sync.dma_start(
                                out=_ap(kvg_par[i % 2], (64 * q) * 512,
                                        [[512, 64], [64, 8], [1, 64]]),
                                in_=_ap(kvg_sb[m % 2], (64 * u) * DM + 64 * q,
                                        [[DM, 64], [128, 8], [1, 64]]),
                            ).then_inc(pcs, 16)
                        sync.wait_ge(txs, 16 * (3 * (m // 2) + 1))
                        for q in range(2):
                            sync.dma_start(
                                out=_ap(kc_par[i % 2], (64 * q) * 512,
                                        [[512, 64], [64, 8], [1, 64]]),
                                in_=_ap(txk_sb[m % 2], (64 * u) * DM + 64 * q,
                                        [[DM, 64], [128, 8], [1, 64]]),
                            ).then_inc(pcs, 16)
                        if i >= 2:
                            sync.wait_ge(S["va"], vaN[("c", i - 2, 8)])
                        sync.wait_ge(txs, 16 * (3 * (m // 2) + 3))
                        for q in range(2):
                            sync.dma_start(
                                out=_ap(qc_par[i % 2], (64 * q) * 512,
                                        [[512, 64], [64, 8], [1, 64]]),
                                in_=_ap(txq_sb[m % 2], (64 * u) * DM + 64 * q,
                                        [[DM, 64], [128, 8], [1, 64]]),
                            ).then_inc(pcs, 16)

                    def attn_copy(i):
                        m, u = i // 2, i % 2
                        sync.wait_ge(S["va"], vaN[("c", i, 8)])
                        if m >= 4 and u == 0:
                            sync.wait_ge(S["asp"], 16 * (m // 4))
                        for q in range(2):
                            sync.dma_start(
                                out=_ap(attn_stage, (m % 4) * DM
                                        + (64 * u) * (4 * DM) + 64 * q,
                                        [[4 * DM, 64], [128, 8], [1, 64]]),
                                in_=_ap(outp_sb[i % 2], (64 * q) * 512,
                                        [[512, 64], [64, 8], [1, 64]]),
                            ).then_inc(S["ac%d" % (i % 2)], 16)
                        if i % 8 == 7:
                            g = i // 8
                            sync.wait_ge(S["ac0"], 16 * 2 * (4 * g + 4))
                            sync.wait_ge(S["ac1"], 16 * 2 * (4 * g + 4))
                            v = sync.value_load(dynr_sb[g:g + 1, 0:1])
                            sync.dma_start(
                                out=attn2_hbm[ds(v, 512), :].rearrange(
                                    "(u t) d -> t u d", t=128),
                                in_=_ap(attn_stage, 0,
                                        [[4 * DM, 128], [DM, 4], [1, DM]]),
                            ).then_inc(S["asp"], 16)

                    def batch_sync(y):
                        tx_load(4 * y)
                        tx_load(4 * y + 1)
                        for i in range(8 * y, 8 * y + 8):
                            if i == 8 * y + 2:
                                tx_load(4 * y + 2)
                            if i == 8 * y + 4:
                                tx_load(4 * y + 3)
                            par_copies(i)
                            if i >= 2:
                                attn_copy(i - 2)

                    for k in range(3):
                        sync.dma_start(out=x_slab[k % 3][:, :],
                                       in_=x_seq[128 * k:128 * (k + 1), :],
                                       ).then_inc(S["x_ld"], 16)
                    for s in range(NSLICE):
                        for u in range(4):
                            k = 4 * s + u
                            sync.wait_ge(S["xn_done"], k + 1)
                            if k + 3 < NSLAB:
                                sync.dma_start(out=x_slab[(k + 3) % 3][:, :],
                                               in_=x_seq[128 * (k + 3):128 * (k + 4), :],
                                               ).then_inc(S["x_ld"], 16)
                            sync.dma_start(out=xn_hbm[128 * k:128 * (k + 1), :],
                                           in_=xnstg[:, DM * (k % 4):DM * (k % 4 + 1)],
                                           ).then_inc(S["xnsp"], 16)
                        if lvl >= 1 and s >= 1:
                            kqv_spill(s - 1)
                        if lvl >= 2 and s >= 1:
                            batch_sync(s - 1)
                    if lvl >= 1:
                        kqv_spill(NSLICE - 1)
                    if lvl >= 2:
                        batch_sync(NSLICE - 1)
                        attn_copy(NCH - 2)
                        attn_copy(NCH - 1)
                        sync.wait_ge(S["asp"], 16 * 8)
                    sync.wait_ge(S["xnsp"], 16 * NSLAB)
                    if not no_dyn:
                        sync.wait_ge(S["xnsp2"], 16 * 8)
                    if lvl >= 1:
                        sync.wait_ge(S["ksp"], 16 * 24 * NSLICE)

            @blkA.gpsimd
            def _(gpsimd):
                n = 0
                for dst, src in ((wk_sb, wk_d), (wq_sb, wq_d), (wv_sb, wv_d)):
                    for c in range(8):
                        gpsimd.dma_start(out=dst[:, DM * c:DM * (c + 1)],
                                         in_=src[128 * c:128 * (c + 1), :]).then_inc(S["wld"], 16)
                        n += 1
                for c in range(8):
                    gpsimd.dma_start(out=thetab_sb[:, H * c:H * (c + 1)],
                                     in_=thetab_d[128 * c:128 * (c + 1), :]).then_inc(S["wld"], 16)
                    n += 1
                for dst, src in ((gpar_sb, gpar_d), (bepar_sb, bepar_d),
                                 (gtm_sb, gtm_d), (betm_sb, betm_d)):
                    gpsimd.dma_start(out=dst[:, :], in_=src[:, :]).then_inc(S["wld"], 16)
                    n += 1
                assert n == 36
                gpsimd.wait_ge(S["cld"], 16 * 6)
                if lvl >= 2:
                    gpsimd.wait_ge(S["pre"], 1)
                _casts = {1: (wgbf_hbm, wgT_d), 2: (wubf_hbm, wuT_d),
                          3: (wdbf_hbm, wdT_d), 4: (wobf_hbm, wo_d)}

                def kvg_batch(y):
                    for m in range(4 * y, 4 * y + 4):
                        if m >= 2:
                            for j in (2 * m - 4, 2 * m - 3):
                                gpsimd.wait_ge(S["pc%d" % (j % 4)],
                                               16 * (6 * (j // 4) + 2))
                        gpsimd.wait_ge(S["txp%d" % (m % 2)],
                                       16 * (3 * (m // 2) + 2))
                        gpsimd.tensor_sub(kvg_sb[m % 2][:, :], txk_sb[m % 2][:, :],
                                          txv_sb[m % 2][:, :])
                        gpsimd.tensor_mul(kvg_sb[m % 2][:, :], kvg_sb[m % 2][:, :],
                                          gtm_sb[:, :])
                        gpsimd.tensor_add(kvg_sb[m % 2][:, :], kvg_sb[m % 2][:, :],
                                          gbetm_sb[:, :]).then_inc(S["kvg"], 1)

                for s in range(NSLICE):
                    if lvl >= 3 and s in _casts:
                        dst, src = _casts[s]
                        gpsimd.dma_start(out=dst[:, :],
                                         in_=src[:, :]).then_inc(S["wdc"], 16)
                    if lvl >= 2 and s >= 1:
                        kvg_batch(s - 1)
                    if not no_dyn:
                        gpsimd.wait_ge(S["xn_done"], 4 * s + 4)
                        v = gpsimd.value_load(dynr_sb[s:s + 1, 0:1])
                        src = _ap(xnstg, 0, [[4 * DM, 128], [DM, 4], [1, DM]])
                        gpsimd.dma_start(
                            out=xn2_hbm[ds(v, 512), :].rearrange("(u t) d -> t u d", t=128),
                            in_=src).then_inc(S["xnsp2"], 16)
                if lvl >= 2:
                    kvg_batch(NSLICE - 1)

            WLD_ALL = 16 * 36

            def batch_mms_impl(tensor, y):
                for i in range(8 * y, 8 * y + 8):
                    m, u = i // 2, i % 2
                    sl = m // 4
                    off = (i * 64) % 512
                    tensor.wait_ge(S["sa"], saN[("c0",)] if i == 0
                                   else saN[("s", i - 1, 4)])
                    if i >= 2:
                        tensor.wait_ge(S["va"], vaN[("c", i - 2, 1)])
                    for p in range(8):
                        for q in range(2):
                            mm = tensor.matmul(
                                pp_z1[i % 2][64 * q:64 * (q + 1), 64 * p:64 * (p + 1)],
                                kfm_sb[sl % 2][64 * q:64 * (q + 1),
                                               512 * p + off:512 * p + off + 64],
                                wbf[64 * q:64 * (q + 1), 64 * p:64 * (p + 1)],
                                start=True, stop=True, tile_position=(64 * q, 64 * q))
                            if p == 7 and q == 1:
                                mm.then_inc(S["z1mm"], 1)
                    tensor.wait_ge(S["va"], vaN[("c", i, 3)])
                    if i >= 1:
                        tensor.wait_ge(S["va"], vaN[("c", i - 1, 5)])
                    for p in range(8):
                        for q in range(2):
                            mm = tensor.matmul(
                                pp_dw[64 * q:64 * (q + 1), 64 * p:64 * (p + 1)],
                                sck_sb[64 * q:64 * (q + 1), 64 * p:64 * (p + 1)],
                                grad_sb[64 * q:64 * (q + 1), 64 * p:64 * (p + 1)],
                                start=True, stop=True, tile_position=(64 * q, 64 * q))
                            if p == 7 and q == 1:
                                mm.then_inc(S["dwmm"], 1)
                    for p in range(8):
                        for q in range(2):
                            lhs = _ap(c_eta2b, q * 64 * 8 + p, [[8, 64], [0, 64]])
                            mm = tensor.matmul(
                                pp_db[64 * q:64 * (q + 1), 64 * p:64 * (p + 1)],
                                lhs,
                                grad_sb[64 * q:64 * (q + 1), 64 * p:64 * (p + 1)],
                                start=True, stop=True, tile_position=(64 * q, 64 * q))
                            if p == 7 and q == 1:
                                mm.then_inc(S["dwmm"], 1)
                    tensor.wait_ge(S["sa"], saN[("s", i, 4)])
                    if i >= 1:
                        tensor.wait_ge(S["va"], vaN[("c", i - 1, 6)])
                    for p in range(8):
                        for q in range(2):
                            mm = tensor.matmul(
                                pp_zq[64 * q:64 * (q + 1), 64 * p:64 * (p + 1)],
                                qfm_sb[sl % 2][64 * q:64 * (q + 1),
                                               512 * p + off:512 * p + off + 64],
                                wbf[64 * q:64 * (q + 1), 64 * p:64 * (p + 1)],
                                start=True, stop=True, tile_position=(64 * q, 64 * q))
                            if p == 7 and q == 1:
                                mm.then_inc(S["zqmm"], 1)

            @blkA.tensor
            def _(tensor):
                def batch_mms(y):
                    tensor.wait_ge(S["evac"], 24 * (y + 1))
                    batch_mms_impl(tensor, y)

                tensor.wait_ge(S["wld"], WLD_ALL)
                for s in range(NSLICE if lvl >= 1 else 0):
                    tensor.wait_ge(S["xnf_ld"], 16 * 8 * (s + 1))
                    for t in range(3):
                        for p in range(8):
                            g = 24 * s + t * 8 + p
                            if g >= 2:
                                tensor.wait_ge(S["evac"], g - 1)
                            for c in range(8):
                                mm = tensor.matmul(
                                    pp_proj[g % 2][:, :],
                                    wsb(t)[:, DM * c + 128 * p: DM * c + 128 * (p + 1)],
                                    xnf_sb[s % 2][:, 512 * c:512 * (c + 1)],
                                    start=(c == 0), stop=(c == 7))
                                if c == 7:
                                    mm.then_inc(S["pmm"], 1)
                    tensor.wait_ge(S["evac"], 24 * s + 8)
                    for u in range(4 if not no_lr else 0):
                        k = 4 * s + u
                        if k >= 1:
                            tensor.wait_ge(S["va"], vaN[("l", k - 1)])
                        for p in range(8):
                            mm = tensor.matmul(
                                pp_lr[:, :],
                                kfm_sb[s % 2][:, 512 * p + 128 * u: 512 * p + 128 * (u + 1)],
                                thetab_sb[:, H * p:H * (p + 1)],
                                start=(p == 0), stop=(p == 7))
                            if p == 7:
                                mm.then_inc(S["pmm"], 1)
                    if lvl >= 2 and s >= 1:
                        batch_mms(s - 1)
                if lvl >= 2:
                    batch_mms(NSLICE - 1)

            @blkA.scalar
            def _(scalar):
                def batch_sc(y):
                    for i in range(8 * y, 8 * y + 8):
                        m, u = i // 2, i % 2
                        scalar.wait_ge(S["va"], vaN[("e", m)])
                        if i >= 2:
                            scalar.wait_ge(S["va"], vaN[("c", i - 2, 3)])
                        last = None
                        for q in range(2):
                            src = _ap(eta_sb, (64 * u) * (NSLAB * H) + H * m + q,
                                      [[NSLAB * H, 64], [2, 8]])
                            dst_ap = _ap(eta_par[i % 2], (64 * q) * 8, [[8, 64], [1, 8]])
                            last = scalar.copy(dst_ap, src)
                        last.then_inc(S["sa"], 1)
                        scalar.wait_ge(S["va"], vaN[("c", i, 1)])
                        scalar.activation(sq_sb[:, :], z1b_sb[:, :],
                                          AF.Square).then_inc(S["sa"], 1)
                        scalar.wait_ge(S["va"], vaN[("c", i, 2)])
                        scalar.activation(c_std[:, :], c_var[:, :],
                                          AF.Sqrt).then_inc(S["sa"], 1)
                        scalar.wait_ge(S["va"], vaN[("c", i, 4)])
                        scalar.copy(wbf[:, :], w_par[:, :]).then_inc(S["sa"], 1)
                        scalar.wait_ge(S["va"], vaN[("c", i, 6)])
                        scalar.activation(sq_sb[:, :], z1b_sb[:, :],
                                          AF.Square).then_inc(S["sa"], 1)
                        scalar.wait_ge(S["va"], vaN[("c", i, 7)])
                        scalar.activation(c_std[:, :], c_var[:, :],
                                          AF.Sqrt).then_inc(S["sa"], 1)

                if lvl >= 2:
                    scalar.wait_ge(S["cld"], 16 * 6)
                    scalar.copy(wbf[:, :], w_par[:, :]).then_inc(S["sa"], 1)
                for s in range(NSLICE):
                    for u in range(4):
                        k = 4 * s + u
                        scalar.wait_ge(S["x_ld"], 16 * (k + 1))
                        scalar.activation(sqx_sb[:, :], x_slab[k % 3][:, :],
                                          AF.Square).then_inc(S["sa"], 1)
                        scalar.wait_ge(S["va"], vaN[("r", k)])
                        scalar.activation(st_r[:, :], st_t1[:, :],
                                          AF.Sqrt).then_inc(S["sa"], 1)
                    if lvl >= 1:
                        scalar.wait_ge(S["xnsp"], 16 * 4 * (s + 1))
                        if s >= 2:
                            scalar.wait_ge(S["pmm"], 28 * (s - 1))
                        for c in range(8):
                            scalar.dma_start(
                                out=xnf_sb[s % 2][:, 512 * c:512 * (c + 1)],
                                in_=xn_hbm[512 * s:512 * (s + 1), 128 * c:128 * (c + 1)],
                                transpose=True).then_inc(S["xnf_ld"], 16)
                        if not no_lr:
                            for u in range(4):
                                k = 4 * s + u
                                scalar.wait_ge(S["va"], vaN[("l", k)])
                                scalar.activation(sigall_sb[:, H * k:H * (k + 1)],
                                                  lrall_sb[:, H * k:H * (k + 1)],
                                                  AF.Sigmoid).then_inc(S["sa"], 1)
                    if lvl >= 2 and s >= 1:
                        batch_sc(s - 1)
                if lvl >= 2:
                    batch_sc(NSLICE - 1)

            def batch_vec_impl(vector, y):
                for i in range(8 * y, 8 * y + 8):
                    m, u = i // 2, i % 2
                    vector.wait_ge(S["z1mm"], i + 1)
                    vector.tensor_add(z1b_sb[:, :], pp_z1[i % 2][:, :],
                                      b_mat[:, :]).then_inc(S["va"], 1)
                    vector.reduce_sum(c_s1[:, :], _r3(z1b_sb), axis=AXX)
                    vector.wait_ge(S["sa"], saN[("s", i, 2)])
                    vector.reduce_sum(c_s2[:, :], _r3(sq_sb), axis=AXX)
                    vector.tensor_scalar_mul(c_m[:, :], c_s1[:, :], -1.0 / 64.0)
                    vector.tensor_scalar(c_v0[:, :], c_s2[:, :], 64.0 * EPS_SCAN,
                                         1.0 / 64.0, ALU.add, ALU.mult)
                    vector.tensor_mul(c_t[:, :], c_m[:, :], c_m[:, :])
                    vector.tensor_scalar_mul(c_d[:, :], c_s1[:, :], 1.0)
                    vector.tensor_sub(c_var[:, :], c_v0[:, :],
                                      c_t[:, :]).then_inc(S["va"], 1)
                    vector.wait_ge(S["sa"], saN[("s", i, 3)])
                    vector.reciprocal(c_rstd[:, :], c_std[:, :])
                    z3 = _r3(z1b_sb)
                    vector.tensor_tensor(z3, z3, _grp(c_m), ALU.add)
                    vector.tensor_tensor(_r3(xh_sb), z3, _grp(c_rstd), ALU.mult)
                    vector.wait_ge(S["sa"], saN[("s", i, 1)])
                    vector.wait_ge(S["pc%d" % (i % 4)], 16 * (6 * (i // 4) + 4))
                    vector.tensor_mul(gxh_sb[:, :], xh_sb[:, :], g2par_sb[:, :])
                    vector.tensor_add(gxh_sb[:, :], gxh_sb[:, :], kvg_par[i % 2][:, :])
                    vector.reduce_sum(c_r1[:, :], _r3(gxh_sb), axis=AXX)
                    vector.tensor_mul(gx2_sb[:, :], gxh_sb[:, :], xh_sb[:, :])
                    vector.reduce_sum(c_r2[:, :], _r3(gx2_sb), axis=AXX)
                    vector.tensor_scalar_mul(c_a1[:, :], c_r1[:, :], 1.0 / 64.0)
                    vector.tensor_scalar_mul(c_a2[:, :], c_r2[:, :], 1.0 / 64.0)
                    vector.tensor_mul(c_eta2[:, :], eta_par[i % 2][:, :], c_rstd[:, :])
                    vector.tensor_tensor(_r3(u_sb), _r3(xh_sb), _grp(c_a2), ALU.mult)
                    vector.tensor_copy(c_eta2b[:, :], c_eta2[:, :])
                    vector.tensor_sub(grad_sb[:, :], gxh_sb[:, :], u_sb[:, :])
                    vector.tensor_tensor(_r3(grad_sb), _r3(grad_sb), _grp(c_a1), ALU.subtract)
                    vector.tensor_tensor(_r3(sck_sb), _r3(kc_par[i % 2]),
                                         _ap(c_eta2b, 0, [[8, 128], [1, 8], [0, 64]]),
                                         ALU.mult).then_inc(S["va"], 1)
                    vector.wait_ge(S["dwmm"], 2 * i + 1)
                    vector.tensor_sub(w_par[:, :], w_par[:, :],
                                      pp_dw[:, :]).then_inc(S["va"], 1)
                    vector.wait_ge(S["dwmm"], 2 * i + 2)
                    vector.tensor_sub(b_mat[:, :], b_mat[:, :],
                                      pp_db[:, :]).then_inc(S["va"], 1)
                    vector.wait_ge(S["zqmm"], i + 1)
                    vector.tensor_add(z1b_sb[:, :], pp_zq[:, :],
                                      b_mat[:, :]).then_inc(S["va"], 1)
                    vector.reduce_sum(c_s1[:, :], _r3(z1b_sb), axis=AXX)
                    vector.wait_ge(S["sa"], saN[("s", i, 5)])
                    vector.reduce_sum(c_s2[:, :], _r3(sq_sb), axis=AXX)
                    vector.tensor_scalar_mul(c_m[:, :], c_s1[:, :], -1.0 / 64.0)
                    vector.tensor_scalar(c_v0[:, :], c_s2[:, :], 64.0 * EPS_SCAN,
                                         1.0 / 64.0, ALU.add, ALU.mult)
                    vector.tensor_mul(c_t[:, :], c_m[:, :], c_m[:, :])
                    vector.tensor_scalar_mul(c_d[:, :], c_s1[:, :], 1.0)
                    vector.tensor_sub(c_var[:, :], c_v0[:, :],
                                      c_t[:, :]).then_inc(S["va"], 1)
                    vector.wait_ge(S["sa"], saN[("s", i, 6)])
                    vector.reciprocal(c_rstd[:, :], c_std[:, :])
                    z3b = _r3(z1b_sb)
                    vector.tensor_tensor(z3b, z3b, _grp(c_m), ALU.add)
                    vector.tensor_tensor(_r3(w1_sb), z3b, _grp(c_rstd), ALU.mult)
                    vector.tensor_mul(w1_sb[:, :], w1_sb[:, :], gpar_sb[:, :])
                    vector.tensor_add(w1_sb[:, :], w1_sb[:, :], bepar_sb[:, :])
                    vector.wait_ge(S["pc%d" % (i % 4)], 16 * (6 * (i // 4) + 6))
                    if i >= 2:
                        vector.wait_ge(S["ac%d" % (i % 2)], 16 * 2 * (i // 2))
                    vector.tensor_add(outp_sb[i % 2][:, :], w1_sb[:, :],
                                      qc_par[i % 2][:, :]).then_inc(S["va"], 1)

            @blkA.vector
            def _(vector):
                def batch_vec(y):
                    batch_vec_impl(vector, y)

                vector.wait_ge(S["wld"], WLD_ALL)
                vector.wait_ge(S["cld"], 16 * 6)
                vector.tensor_add(etab_sb[:, :], invk_sb[:, :], alpha2_sb[:, :])
                vector.tensor_mul(g2par_sb[:, :], gpar_sb[:, :], gpar_sb[:, :])
                vector.tensor_mul(gbetm_sb[:, :], gtm_sb[:, :],
                                  betm_sb[:, :]).then_inc(S["pre"], 1)
                vector.tensor_relu(etab_sb[:, :], etab_sb[:, :])
                for s in range(NSLICE):
                    for u in range(4):
                        k = 4 * s + u
                        vector.wait_ge(S["x_ld"], 16 * (k + 1))
                        vector.reduce_sum(st_s1[:, :], x_slab[k % 3][:, :], axis=AXX)
                        vector.wait_ge(S["sa"], saN[("q", k)])
                        vector.reduce_sum(st_s2[:, :], sqx_sb[:, :], axis=AXX)
                        vector.tensor_scalar_mul(st_m[:, :], st_s1[:, :], -1.0 / DM)
                        vector.tensor_scalar(st_v[:, :], st_s2[:, :], DM * EPS_MAIN,
                                             1.0 / DM, ALU.add, ALU.mult)
                        vector.tensor_mul(st_t2[:, :], st_m[:, :], st_m[:, :])
                        vector.tensor_scalar_mul(st_d[:, :], st_s1[:, :], 1.0)
                        vector.tensor_sub(st_v[:, :], st_v[:, :], st_t2[:, :])
                        vector.tensor_scalar_mul(st_d[:, :], st_s1[:, :], 1.0)
                        vector.reciprocal(st_t1[:, :], st_v[:, :]).then_inc(S["va"], 1)
                        vector.wait_ge(S["sa"], saN[("t", k)])
                        if k >= 4:
                            vector.wait_ge(S["xnsp"], 16 * (k - 3))
                            if not no_dyn:
                                vector.wait_ge(S["xnsp2"], 16 * (k // 4))
                        vector.tensor_scalar(xnstg[:, DM * (k % 4):DM * (k % 4 + 1)],
                                             x_slab[k % 3][:, :],
                                             st_m[:, :], st_r[:, :],
                                             ALU.add, ALU.mult).then_inc(S["xn_done"], 1)
                    if lvl >= 2 and s >= 2:
                        vector.wait_ge(S["zqmm"], 8 * (s - 1))
                    for t in range(3 if lvl >= 1 else 0):
                        for p in range(8):
                            g = 24 * s + t * 8 + p
                            j = t * 8 + p
                            vector.wait_ge(S["pmm"], PMM(s, j))
                            if s >= 2:
                                vector.wait_ge(S["ksp"], 16 * 24 * (s - 1))
                            if t == 2 and s >= 1:
                                vector.wait_ge(S["ksp"], 16 * 24 * s)
                            vector.tensor_copy(fmsb(t)[s % 2][:, 512 * p:512 * (p + 1)],
                                               pp_proj[g % 2][:, :]).then_inc(S["evac"], 1)
                    for u in range(4 if (lvl >= 1 and not no_lr) else 0):
                        k = 4 * s + u
                        vector.wait_ge(S["pmm"], PMM_LR(s, u))
                        vector.tensor_add(lrall_sb[:, H * k:H * (k + 1)], pp_lr[:, :],
                                          tbmat_sb[:, :]).then_inc(S["va"], 1)
                    for u in range(4 if (lvl >= 1 and not no_lr) else 0):
                        k = 4 * s + u
                        vector.wait_ge(S["sa"], saN[("g", k)])
                        vector.tensor_scalar(
                            eta_sb[:, H * k:H * (k + 1)],
                            sigall_sb[:, H * k:H * (k + 1)], etab_sb[:, :], 1.0 / 64.0,
                            ALU.mult, ALU.mult).then_inc(S["va"], 1)
                    if lvl >= 2 and s >= 1:
                        batch_vec(s - 1)
                if lvl >= 2:
                    batch_vec(NSLICE - 1)

        if lvl < 3:
            return nc
        with ExitStack() as pb1:
            b1 = pb1.enter_context
            wg_sb = b1(nc.sbuf_tensor("wg_sb", [128, 8 * HIDDEN], BF16))
            wu_sb = b1(nc.sbuf_tensor("wu_sb", [128, 8 * HIDDEN], BF16))
            xnfB = [b1(nc.sbuf_tensor(f"xnfB{i}", [128, 8 * 512], BF16)) for i in range(2)]
            h_sb = b1(nc.sbuf_tensor("h_sb", [128, 24 * 512], BF16))
            sg_sb = b1(nc.sbuf_tensor("sg_sb", [128, 512], F32))
            pp_gate = [b1(nc.psum_tensor(f"pp_gate{i}", [128, 512], F32)) for i in range(2)]
            pp_up = [b1(nc.psum_tensor(f"pp_up{i}", [128, 512], F32)) for i in range(2)]

            blkB1 = pb1.enter_context(nc.Block())

            @blkB1.sync
            def _(sync):
                sync.sem_inc(S["phb"], 1)
                for j in range(4):
                    for oc in range(24):
                        sync.wait_ge(S["hready"], 24 * j + oc + 1)
                        sync.dma_start(
                            out=_ap(h_hbm, (4 * j) * 128 * HIDDEN + 128 * oc,
                                    [[HIDDEN, 128], [128 * HIDDEN, 4], [1, 128]]),
                            in_=h_sb[:, 512 * oc:512 * (oc + 1)],
                        ).then_inc(S["hsp"], 16)
                if lvl == 3:
                    sync.wait_ge(S["hsp"], 16 * 24 * 4)

            @blkB1.gpsimd
            def _(gpsimd):
                gpsimd.wait_ge(S["phb"], 1)
                gpsimd.wait_ge(S["wdc"], 16)
                for c in range(8):
                    gpsimd.dma_start(out=wg_sb[:, HIDDEN * c:HIDDEN * (c + 1)],
                                     in_=wgbf_hbm[128 * c:128 * (c + 1), :],
                                     ).then_inc(S["fwl"], 16)
                gpsimd.wait_ge(S["wdc"], 32)
                for c in range(8):
                    gpsimd.dma_start(out=wu_sb[:, HIDDEN * c:HIDDEN * (c + 1)],
                                     in_=wubf_hbm[128 * c:128 * (c + 1), :],
                                     ).then_inc(S["fwl"], 16)

            @blkB1.scalar
            def _(scalar):
                scalar.wait_ge(S["phb"], 1)
                for j in range(2):
                    for c in range(8):
                        scalar.dma_start(
                            out=xnfB[j % 2][:, 512 * c:512 * (c + 1)],
                            in_=xn2_hbm[L + 512 * j:L + 512 * (j + 1),
                                        128 * c:128 * (c + 1)],
                            transpose=True).then_inc(S["bxn"], 16)
                for j in range(4):
                    for oc in range(24):
                        g = 24 * j + oc
                        scalar.wait_ge(S["gumm"], 2 * g + 1)
                        if g >= 1:
                            scalar.wait_ge(S["hready"], g)
                        scalar.activation(sg_sb[:, :], pp_gate[g % 2][:, :],
                                          AF.Silu).then_inc(S["silu"], 1)
                    jj = j + 2
                    if jj <= 3:
                        scalar.wait_ge(S["gumm"], 48 * (jj - 1))
                        for c in range(8):
                            scalar.dma_start(
                                out=xnfB[jj % 2][:, 512 * c:512 * (c + 1)],
                                in_=xn2_hbm[L + 512 * jj:L + 512 * (jj + 1),
                                            128 * c:128 * (c + 1)],
                                transpose=True).then_inc(S["bxn"], 16)

            @blkB1.tensor
            def _(tensor):
                tensor.wait_ge(S["fwl"], 16 * 8)
                for j in range(4):
                    tensor.wait_ge(S["bxs"], j + 1)
                    for oc in range(24):
                        g = 24 * j + oc
                        if g >= 2:
                            tensor.wait_ge(S["silu"], g - 1)
                            tensor.wait_ge(S["hready"], g - 1)
                        for c in range(8):
                            mm = tensor.matmul(
                                pp_gate[g % 2][:, :],
                                wg_sb[:, HIDDEN * c + 128 * oc: HIDDEN * c + 128 * (oc + 1)],
                                xnfB[j % 2][:, 512 * c:512 * (c + 1)],
                                start=(c == 0), stop=(c == 7))
                            if c == 7:
                                mm.then_inc(S["gumm"], 1)
                        if g == 0:
                            tensor.wait_ge(S["fws"], 1)
                        for c in range(8):
                            mm = tensor.matmul(
                                pp_up[g % 2][:, :],
                                wu_sb[:, HIDDEN * c + 128 * oc: HIDDEN * c + 128 * (oc + 1)],
                                xnfB[j % 2][:, 512 * c:512 * (c + 1)],
                                start=(c == 0), stop=(c == 7))
                            if c == 7:
                                mm.then_inc(S["gumm"], 1)

            @blkB1.vector
            def _(vector):
                vector.wait_ge(S["phb"], 1)
                for j in range(4):
                    vector.wait_ge(S["bxn"], 16 * 8 * (j + 1))
                    vector.tensor_scalar_mul(xnfB[j % 2][:, :], xnfB[j % 2][:, :],
                                             1.0 / 127.0).then_inc(S["bxs"], 1)
                    if j == 0:
                        vector.wait_ge(S["fwl"], 16 * 16)
                        vector.tensor_scalar_mul(wu_sb[:, :], wu_sb[:, :],
                                                 1.0 / 127.0).then_inc(S["fws"], 1)
                    vector.wait_ge(S["hsp"], 16 * 24 * j)
                    for oc in range(24):
                        g = 24 * j + oc
                        vector.wait_ge(S["silu"], g + 1)
                        vector.wait_ge(S["gumm"], 2 * g + 2)
                        vector.tensor_mul(h_sb[:, 512 * oc:512 * (oc + 1)],
                                          sg_sb[:, :],
                                          pp_up[g % 2][:, :]).then_inc(S["hready"], 1)

        if lvl < 4:
            return nc
        with ExitStack() as pb2:
            b2 = pb2.enter_context
            wd_sb = b2(nc.sbuf_tensor("wd_sb", [128, 24 * DM], BF16))
            wo_sb = b2(nc.sbuf_tensor("wo_sb", [128, 8 * DM], BF16))
            afm_sb = b2(nc.sbuf_tensor("afm_sb", [128, 8 * TOUT], BF16))
            hsub_sb = [b2(nc.sbuf_tensor(f"hsub{i}", [128, 24 * 128], BF16)) for i in range(2)]
            xres_sb = [b2(nc.sbuf_tensor(f"xres{i}", [128, 4 * DM], F32)) for i in range(2)]
            out_sb = [b2(nc.sbuf_tensor(f"outsb{i}", [128, DM], F32)) for i in range(2)]
            pp_out = [b2(nc.psum_tensor(f"pp_out{i}", [128, DM], F32)) for i in range(3)]

            blkB2 = pb2.enter_context(nc.Block())

            B2_GATE = (16 * 24 * 4)

            @blkB2.sync
            def _(sync):
                if True:
                    for r in range(4):
                        sync.wait_ge(S["gumm"], 192)
                        sync.wait_ge(S["asp"], 16 * 8)
                        sync.dma_start(
                            out=afm_sb[:, TOUT * (4 + r):TOUT * (5 + r)],
                            in_=attn2_hbm[L:L + TOUT, 128 * (4 + r):128 * (5 + r)],
                            transpose=True).then_inc(S["afm_ld"], 16)
                    for r in range(16):
                        sync.wait_ge(S["hsp"], 16 * 24 * (r // 4 + 1))
                        if r >= 2:
                            sync.wait_ge(S["omm"], 2 * (r - 1))
                        sync.dma_start(
                            out=hsub_sb[r % 2][:, :],
                            in_=h_hbm[128 * r:128 * (r + 1), :],
                        ).then_inc(S["hld"], 16)
                        if r % 4 == 0:
                            g = r // 4
                            if g == 0:
                                sync.wait_ge(S["hready"], 96)
                                sync.wait_ge(S["hsp"], 16 * 24 * 4)
                            if g >= 2:
                                sync.wait_ge(S["oev"], 4 * (g - 1))
                            v = sync.value_load(dynr_sb[8 + g:8 + g + 1, 0:1])
                            sync.dma_start(
                                out=xres_sb[g % 2][:, :].rearrange(
                                    "t (u d) -> t u d", u=4),
                                in_=x_seq[ds(v, 512), :].rearrange(
                                    "(u t) d -> t u d", t=128)).then_inc(S["xres_ld"], 16)
                        sync.wait_ge(S["oev"], r + 1)
                        sync.dma_start(out=out_d[128 * r:128 * (r + 1), :],
                                       in_=out_sb[r % 2][:, :]).then_inc(S["ost"], 16)
                    sync.wait_ge(S["ost"], 16 * 16)

            @blkB2.gpsimd
            def _(gpsimd):
                gpsimd.wait_ge(S["gumm"], 192)
                gpsimd.wait_ge(S["wdc"], 16 * 4)
                for c in range(24):
                    gpsimd.dma_start(out=wd_sb[:, DM * c:DM * (c + 1)],
                                     in_=wdbf_hbm[128 * c:128 * (c + 1), :],
                                     ).then_inc(S["fwl"], 16)
                for c in range(8):
                    gpsimd.dma_start(out=wo_sb[:, DM * c:DM * (c + 1)],
                                     in_=wobf_hbm[128 * c:128 * (c + 1), :],
                                     ).then_inc(S["fwl"], 16)

            @blkB2.scalar
            def _(scalar):
                scalar.wait_ge(S["gumm"], 192)
                scalar.wait_ge(S["asp"], 16 * 8)
                for c in range(4):
                    scalar.dma_start(
                        out=afm_sb[:, TOUT * c:TOUT * (c + 1)],
                        in_=attn2_hbm[L:L + TOUT, 128 * c:128 * (c + 1)],
                        transpose=True).then_inc(S["afm_ld"], 16)

            @blkB2.tensor
            def _(tensor):
                tensor.wait_ge(S["fwl"], 16 * 48)
                for r in range(16):
                    tensor.wait_ge(S["hld"], 16 * (r + 1))
                    if r >= 3:
                        tensor.wait_ge(S["oev"], r - 2)
                    for half in range(2):
                        for oc in range(24):
                            tensor.matmul(
                                pp_out[r % 3][:, 512 * half:512 * (half + 1)],
                                hsub_sb[r % 2][:, 128 * oc:128 * (oc + 1)],
                                wd_sb[:, DM * oc + 512 * half: DM * oc + 512 * (half + 1)],
                                start=(oc == 0), stop=False)
                        if r == 0 and half == 0:
                            tensor.wait_ge(S["afm_ld"], 16 * 8)
                        for c in range(8):
                            mm = tensor.matmul(
                                pp_out[r % 3][:, 512 * half:512 * (half + 1)],
                                afm_sb[:, TOUT * c + 128 * r: TOUT * c + 128 * (r + 1)],
                                wo_sb[:, DM * c + 512 * half: DM * c + 512 * (half + 1)],
                                start=False, stop=(c == 7))
                            if c == 7:
                                mm.then_inc(S["omm"], 1)

            @blkB2.vector
            def _(vector):
                for r in range(16):
                    vector.wait_ge(S["omm"], 2 * (r + 1))
                    vector.wait_ge(S["xres_ld"], 16 * (r // 4 + 1))
                    if r >= 2:
                        vector.wait_ge(S["ost"], 16 * (r - 1))
                    vector.tensor_add(out_sb[r % 2][:, :], pp_out[r % 3][:, :],
                                      xres_sb[(r // 4) % 2][:, DM * (r % 4):DM * (r % 4 + 1)],
                                      ).then_inc(S["oev"], 1)
    return nc


_CACHE = {}


def _stage(inputs):
    x = np.ascontiguousarray(np.asarray(inputs["L0_attn_hidden_states"], np.float32))
    theta = np.asarray(inputs["L0_attn_theta"], np.float32)
    thetab = np.zeros((DM, H), np.float32)
    for h in range(H):
        thetab[64 * h:64 * h + 64, h] = theta[h, :, 0]
    tbmat = np.broadcast_to(
        np.asarray(inputs["L0_attn_theta_bias"], np.float32).reshape(1, H), (128, H))
    invk = np.tile(1.0 / np.arange(1, 65, dtype=np.float32), 2)[:, None]
    alpha2 = np.tile(np.asarray(inputs["L0_attn_alpha"], np.float32), 2)[:, None]
    gamma = np.asarray(inputs["L0_attn_gamma"], np.float32)[:, 0, :]
    beta = np.asarray(inputs["L0_attn_beta"], np.float32)[:, 0, :]
    W1 = np.asarray(inputs["L0_attn_W1"], np.float32)
    b1 = np.asarray(inputs["L0_attn_b1"], np.float32)[:, 0, :]

    def to_par(v):
        out = np.zeros((128, 512), np.float32)
        for p in range(8):
            for q in range(2):
                out[q * 64:q * 64 + 64, p * 64:p * 64 + 64] = v[2 * p + q][None, :]
        return out
    gpar, bepar, b0par = to_par(gamma), to_par(beta), to_par(b1)
    w0par = np.zeros((128, 512), np.float32)
    for p in range(8):
        for q in range(2):
            w0par[q * 64:q * 64 + 64, p * 64:p * 64 + 64] = W1[2 * p + q]
    gtm = np.zeros((1, DM), np.float32)
    betm = np.zeros((1, DM), np.float32)
    for j in range(H):
        gtm[0, 64 * j:64 * j + 64] = gamma[j]
        betm[0, 64 * j:64 * j + 64] = beta[j]
    shared = dict(
        wq=np.ascontiguousarray(np.asarray(inputs["L0_attn_wq"], np.float32)),
        wk=np.ascontiguousarray(np.asarray(inputs["L0_attn_wk"], np.float32)),
        wv=np.ascontiguousarray(np.asarray(inputs["L0_attn_wv"], np.float32)),
        wo=np.ascontiguousarray(np.asarray(inputs["L0_attn_wo"], np.float32)),
        thetab=thetab,
        tbmat=np.ascontiguousarray(tbmat).astype(np.float32),
        invk=invk, alpha2=np.ascontiguousarray(alpha2),
        gpar=gpar, bepar=bepar,
        gtm=np.ascontiguousarray(np.broadcast_to(gtm, (128, DM))),
        betm=np.ascontiguousarray(np.broadcast_to(betm, (128, DM))),
        w0par=w0par, b0par=b0par,
        wgT=np.ascontiguousarray(np.asarray(inputs["L0_ffn_w_gate"], np.int32).T),
        wuT=np.ascontiguousarray(np.asarray(inputs["L0_ffn_w_up"], np.int32).T),
        wdT=np.ascontiguousarray(np.asarray(inputs["L0_ffn_w_down"], np.int32).T),
    )
    in_maps = []
    for core in range(8):
        b, half = core // 2, core % 2
        row_off = half * TOUT
        dynr_np = np.zeros((NSLAB + 16, 1), np.uint32)
        for g in range(8):
            dynr_np[g, 0] = (L - row_off) + 512 * g
        for g in range(4):
            dynr_np[8 + g, 0] = row_off + 512 * g
        m = dict(shared)
        m["x_seq"] = np.ascontiguousarray(x[b])
        m["dynr"] = dynr_np
        in_maps.append(m)
    return in_maps


def kernel(**inputs):
    if "nc" not in _CACHE:
        _CACHE["nc"] = build_nc()
    nc = _CACHE["nc"]
    in_maps = _stage(inputs)
    res = run_bass_kernel_spmd(nc, in_maps, core_ids=list(range(8)))
    out = np.zeros((4, L, DM), np.float32)
    for core in range(8):
        b, half = core // 2, core % 2
        out[b, half * TOUT:(half + 1) * TOUT] = res.results[core]["out"]
    return out
